# version 04
# baseline (speedup 1.0000x reference)
import numpy as np
import ml_dtypes
from contextlib import ExitStack

import concourse.bass as bass
import concourse.mybir as mybir
from concourse.bass_utils import run_bass_kernel_spmd

F32 = mybir.dt.float32
BF16 = mybir.dt.bfloat16
AF = mybir.ActivationFunctionType
ALU = mybir.AluOpType
AX = mybir.AxisListType.X

DM = 4096
NH = 128
HD = 64
SS = 128
NG = 8
INTER = NH * HD
CONV = INTER + 2 * NG * SS
PROJ = INTER + CONV + NH
NT = 2048
PAD, LOOK = 125, 3
TEXT = PAD + LOOK + NT
NCH, CL = 16, 128
EPS = 1e-5

DEBUG = False
STAGES = None


def on(name):
    return STAGES is None or name in STAGES


class Slot:
    __slots__ = ("name", "w", "r")
    def __init__(self, name):
        self.name = name
        self.w = None
        self.r = []


SERIALIZE = False
SERIAL_STAGES = set()
NDCH = 8


class K:
    ENGS = ("sync", "tensor", "vector", "scalar", "gpsimd")

    def __init__(self, nc):
        self.nc = nc
        self.items = {e: [] for e in self.ENGS}
        self.sems = {e: (nc.alloc_semaphore("sem_" + e), [0]) for e in self.ENGS}
        self.dch = [(nc.alloc_semaphore(f"dch{i}"), [0]) for i in range(NDCH)]
        self.drr = 0
        self.seen = {e: {} for e in self.ENGS}
        self.lastop = None
        self.barrier = []

    def op(self, eng, emit, reads=(), writes=(), dma=False, ch=None):
        waits = []
        def need(tok, war=False):
            if tok is None:
                return
            sem, n, src = tok
            if war and src == eng:
                return
            if self.seen[eng].get(sem.name, 0) >= n:
                return
            self.seen[eng][sem.name] = n
            waits.append((sem, n))
        for tok in self.barrier:
            need(tok)
        if SERIALIZE or getattr(self, "cur_stage", None) in SERIAL_STAGES:
            need(self.lastop)
        for s in reads:
            need(s.w)
        for s in writes:
            need(s.w)
            for t in s.r:
                need(t, war=True)
        if dma:
            if ch is None:
                ch = self.drr
                self.drr = (self.drr + 1) % NDCH
            sem, cnt = self.dch[ch]
            if cnt[0] > 0:
                need((sem, cnt[0], "dma"))
            cnt[0] += 16
            tok = (sem, cnt[0], "dma")
            ik = 16
        else:
            sem, cnt = self.sems[eng]
            cnt[0] += 1
            tok = (sem, cnt[0], eng)
            ik = 1
        self.lastop = tok
        self.items[eng].append((waits, emit, sem, ik))
        for s in reads:
            s.r.append(tok)
        for s in writes:
            s.w = tok
            s.r = []

    def flush(self, name):
        items = self.items
        self.items = {e: [] for e in self.ENGS}
        self.barrier = (
            [(self.sems[e][0], self.sems[e][1][0], "barrier")
             for e in self.ENGS if self.sems[e][1][0] > 0] +
            [(sem, cnt[0], "barrier") for (sem, cnt) in self.dch
             if cnt[0] > 0])
        if not any(items[e] for e in self.ENGS):
            return
        with self.nc.Block(name) as block:
            def mk(eng_items):
                def body(e):
                    for waits, emit, sem_, kk in eng_items:
                        for (s, n) in waits:
                            e.wait_ge(s, n)
                        instr = emit(e)
                        instr.then_inc(sem_, kk)
                return body
            for eng in self.ENGS:
                if items[eng]:
                    getattr(block, eng)(mk(items[eng]))

    def finale(self):
        chans = [(sem, cnt[0]) for (sem, cnt) in self.dch if cnt[0] > 0]
        with self.nc.Block("finale") as block:
            @block.sync
            def _(e):
                for sem, n in chans:
                    e.wait_ge(sem, n)

    def dma(self, out_ap, in_ap, reads=(), writes=(), eng="sync", ch=None):
        self.op(eng, lambda e: e.dma_start(out=out_ap, in_=in_ap),
                reads=reads, writes=writes, dma=True, ch=ch)


def build_nc(debug=False):
    nc = bass.Bass("TRN2", target_bir_lowering=False)

    x_ext = nc.dram_tensor("x_ext", [TEXT, DM], F32, kind="ExternalInput").ap()
    wT = nc.dram_tensor("wT", [DM, PROJ], F32, kind="ExternalInput").ap()
    cw_cols = nc.dram_tensor("cw_cols", [128, 80 * 4], F32, kind="ExternalInput").ap()
    cb_row = nc.dram_tensor("cb_row", [1, CONV], F32, kind="ExternalInput").ap()
    dtb_col = nc.dram_tensor("dtb_col", [128, 1], F32, kind="ExternalInput").ap()
    d_pair = nc.dram_tensor("d_pair", [128, 64], F32, kind="ExternalInput").ap()
    normw_cols = nc.dram_tensor("normw_cols", [128, 64], F32, kind="ExternalInput").ap()
    woutT = nc.dram_tensor("woutT", [INTER, DM], F32, kind="ExternalInput").ap()
    gwT = nc.dram_tensor("gwT", [DM, 8], F32, kind="ExternalInput").ap()
    bwT = nc.dram_tensor("bwT", [DM, DM], F32, kind="ExternalInput").ap()
    bb_cols = nc.dram_tensor("bb_cols", [128, 32], F32, kind="ExternalInput").ap()
    lAT = nc.dram_tensor("lAT", [DM, 128], F32, kind="ExternalInput").ap()
    lBT = nc.dram_tensor("lBT", [128, DM], F32, kind="ExternalInput").ap()
    masks = nc.dram_tensor("masks", [128, 4], F32, kind="ExternalInput").ap()
    invmasks = nc.dram_tensor("invmasks", [128, 4], F32, kind="ExternalInput").ap()
    o_y = nc.dram_tensor("o_y", [NT, DM], F32, kind="ExternalOutput").ap()

    d_proj = nc.dram_tensor("d_proj", [144, 128, TEXT], BF16, kind="Internal").ap()
    d_dt = nc.dram_tensor("d_dt", [128, TEXT], F32, kind="Internal").ap()
    d_xbca = nc.dram_tensor("d_xbca", [80, 128, NT], BF16, kind="Internal").ap()
    d_xdt = nc.dram_tensor("d_xdt", [NCH, 128, INTER], BF16, kind="Internal").ap()
    d_prev = nc.dram_tensor("d_prev", [NCH, 128, INTER], BF16, kind="Internal").ap()
    d_ynf = nc.dram_tensor("d_ynf", [64, 128, NT], BF16, kind="Internal").ap()
    d_attn = nc.dram_tensor("d_attn", [32, 128, NT], BF16, kind="Internal").ap()
    d_x2 = nc.dram_tensor("d_x2", [16, 128, DM], BF16, kind="Internal").ap()
    d_ffn = nc.dram_tensor("d_ffn", [32, 128, NT], BF16, kind="Internal").ap()
    d_row = nc.dram_tensor("d_row", [1, 16384], F32, kind="Internal").ap()
    d_rowb = nc.dram_tensor("d_rowb", [1, 16384], BF16, kind="Internal").ap()
    cc_in_t = nc.dram_tensor("cc_in", [128, 8256], BF16, kind="Internal")
    cc_out_t = nc.dram_tensor("cc_out", [512, 8256], BF16, kind="Internal")

    eye_f = nc.inline_tensor(np.eye(128, dtype=np.float32), name="eye_f").ap()
    eye_b = nc.inline_tensor(np.eye(128, dtype=ml_dtypes.bfloat16), name="eye_b").ap()
    acol_c = nc.inline_tensor(-np.arange(1, 129, dtype=np.float32).reshape(128, 1),
                              name="acol").ap()
    ones_b = nc.inline_tensor(np.ones((1, 128), dtype=ml_dtypes.bfloat16),
                              name="ones_b").ap()
    onescol_b = nc.inline_tensor(np.ones((128, 1), dtype=ml_dtypes.bfloat16),
                                 name="onescol_b").ap()

    k = K(nc)
    DS = {n: Slot(n) for n in ["d_proj", "d_dt", "d_xbca", "d_xdt", "d_prev",
                               "d_ynf", "d_attn", "d_x2", "d_ffn", "d_row",
                               "d_rowb", "cc"]}

    with ExitStack() as es:
        E = es.enter_context

        def sb(name, shape, dt_):
            t = E(nc.sbuf_tensor(name, shape, dt_))
            return t, Slot(name)

        c_eyef, S_eyef = sb("c_eyef", [128, 128], F32)
        c_eyeb, S_eyeb = sb("c_eyeb", [128, 128], BF16)
        c_acol, S_acol = sb("c_acol", [128, 1], F32)
        c_dtb, S_dtb = sb("c_dtb", [128, 1], F32)
        c_cw, S_cw = sb("c_cw", [128, 320], F32)
        c_onesrow, S_onesrow = sb("c_onesrow", [1, 128], BF16)
        c_onescol, S_onescol = sb("c_onescol", [128, 1], BF16)
        c_dpair, S_dpair = sb("c_dpair", [128, 64], F32)
        c_normw, S_normw = sb("c_normw", [128, 64], F32)
        c_bb, S_bb = sb("c_bb", [128, 32], F32)
        c_masks, S_masks = sb("c_masks", [128, 4], F32)
        c_invm, S_invm = sb("c_invm", [128, 4], F32)
        cdbc, S_cdbc = sb("cdbc", [128, 128], F32)

        CN = dict(c_eyef=(c_eyef, S_eyef), c_eyeb=(c_eyeb, S_eyeb),
                  c_acol=(c_acol, S_acol), c_dtb=(c_dtb, S_dtb),
                  c_cw=(c_cw, S_cw), c_onesrow=(c_onesrow, S_onesrow),
                  c_onescol=(c_onescol, S_onescol), c_dpair=(c_dpair, S_dpair),
                  c_normw=(c_normw, S_normw), c_bb=(c_bb, S_bb),
                  c_masks=(c_masks, S_masks), c_invm=(c_invm, S_invm),
                  cdbc=(cdbc, S_cdbc))

        for dst, src, sl in [(c_eyef, eye_f, S_eyef), (c_eyeb, eye_b, S_eyeb),
                             (c_acol, acol_c, S_acol), (c_dtb, dtb_col, S_dtb),
                             (c_cw, cw_cols, S_cw),
                             (c_dpair, d_pair, S_dpair),
                             (c_normw, normw_cols, S_normw),
                             (c_bb, bb_cols, S_bb), (c_masks, masks, S_masks),
                             (c_invm, invmasks, S_invm),
                             (c_onesrow, ones_b, S_onesrow),
                             (c_onescol, onescol_b, S_onescol)]:
            k.dma(dst[:, :], src, writes=(sl,))
        k.flush("consts")

        with ExitStack() as es1:
            ln1_fm = es1.enter_context(
                nc.sbuf_tensor("ln1_fm", [128, 32 * TEXT], BF16))
            S_ln1 = Slot("ln1_fm")
            k.cur_stage = "s0"
            if on("s0"):
                _s0(nc, k, x_ext, ln1_fm, S_ln1, CN)
                k.flush("s0")
            k.cur_stage = "s1"
            if on("s1"):
                _s1(nc, k, wT, ln1_fm, S_ln1, d_proj, d_dt, DS)
                k.flush("s1")

        k.cur_stage = "s2a"
        if on("s2a"):
            _s2a(nc, k, DS, d_proj, d_xbca, cb_row, CN)
            k.flush("s2a")

        with ExitStack() as es2:
            E2 = es2.enter_context
            def sb2(name, shape, dt_):
                t = E2(nc.sbuf_tensor(name, shape, dt_))
                return t, Slot(name)
            da_all, S_daall = sb2("da_all", [128, NT], F32)
            expcs_all, S_ecs = sb2("expcs_all", [128, NT], F32)
            cumdec, S_cumdec = sb2("cumdec", [128, NCH], F32)
            big, S_big = sb2("big", [128, INTER], F32)
            csum, S_csum = sb2("csum", [128, 4], F32)
            P = dict(da_all=(da_all, S_daall),
                     expcs_all=(expcs_all, S_ecs),
                     cumdec=(cumdec, S_cumdec), big=(big, S_big),
                     csum=(csum, S_csum))
            k.op("vector", lambda e: e.memset(big[:, :], 0.0), writes=(S_big,))
            k.op("vector", lambda e: e.memset(csum[:, :], 0.0), writes=(S_csum,))

            k.cur_stage = "s2b"
            if on("s2b"):
                _s2b(nc, k, DS, d_xbca, d_dt, d_xdt, d_prev, d_row, CN, P)
                k.flush("s2b")
            k.cur_stage = "s3"
            if on("s3"):
                _s3(nc, k, DS, cc_in_t, cc_out_t, d_row, CN, P)
                k.flush("s3")
            k.cur_stage = "s4"
            if on("s4"):
                _s4(nc, k, DS, d_proj, d_xbca, d_xdt, d_prev, d_ynf,
                    d_row, d_rowb, CN, P)
                k.flush("s4")

        k.cur_stage = "s5"
        if on("s5"):
            _s5(nc, k, DS, woutT, d_ynf, d_attn)
            k.flush("s5")

        with ExitStack() as es3:
            ln3_fm = es3.enter_context(
                nc.sbuf_tensor("ln3_fm", [128, 32 * NT], BF16))
            S_ln3 = Slot("ln3_fm")
            k.cur_stage = "s6"
            if on("s6"):
                _s6(nc, k, DS, x_ext, d_attn, d_x2, ln3_fm, S_ln3, CN)
                k.flush("s6")
            k.cur_stage = "s7"
            if on("s7"):
                _s7(nc, k, DS, gwT, bwT, lAT, lBT, ln3_fm, S_ln3, d_ffn, CN)
                k.flush("s7")

        k.cur_stage = "s8"
        if on("s8"):
            _s8(nc, k, DS, d_x2, d_ffn, o_y, CN)
        k.flush("s8")

        if debug:
            dbg = [("dbg_proj", d_proj, [144, 128, TEXT], BF16, "s1"),
                   ("dbg_dt", d_dt, [128, TEXT], F32, "s1"),
                   ("dbg_xbca", d_xbca, [80, 128, NT], BF16, "s2a"),
                   ("dbg_xdt", d_xdt, [NCH, 128, INTER], BF16, "s2b"),
                   ("dbg_prev", d_prev, [NCH, 128, INTER], BF16, "s2b"),
                   ("dbg_ynf", d_ynf, [64, 128, NT], BF16, "s4"),
                   ("dbg_attn", d_attn, [32, 128, NT], BF16, "s5"),
                   ("dbg_x2", d_x2, [16, 128, DM], BF16, "s6"),
                   ("dbg_ffn", d_ffn, [32, 128, NT], BF16, "s7")]
            for nm, src, shape, dt_, st in dbg:
                if not on(st):
                    continue
                o = nc.dram_tensor(nm, shape, dt_, kind="ExternalOutput").ap()
                k.dma(o, src, reads=tuple(DS.values()))
            k.flush("dbg")
        k.finale()

    return nc


def _ln_stats(nc, k, xin, S_xin, stat, S_stat, ssq, S_ssq):
    k.op("vector", lambda e: e.reduce_sum(stat[:, 0:1], xin[:, :], axis=AX),
         reads=(S_xin,), writes=(S_stat,))
    k.op("scalar", lambda e: e.activation(ssq[:, :], xin[:, :], AF.Square),
         reads=(S_xin,), writes=(S_ssq,))
    k.op("vector", lambda e: e.reduce_sum(stat[:, 4:5], ssq[:, :], axis=AX),
         reads=(S_ssq,), writes=(S_stat,))
    for f in [
        lambda e: e.tensor_scalar(stat[:, 1:2], stat[:, 0:1], 1.0 / DM, None,
                                  ALU.mult),
        lambda e: e.tensor_scalar(stat[:, 2:3], stat[:, 4:5], 1.0 / DM, None,
                                  ALU.mult),
        lambda e: e.tensor_tensor(stat[:, 3:4], stat[:, 1:2], stat[:, 1:2],
                                  ALU.mult),
        lambda e: e.tensor_tensor(stat[:, 2:3], stat[:, 2:3], stat[:, 3:4],
                                  ALU.subtract),
        lambda e: e.tensor_scalar(stat[:, 2:3], stat[:, 2:3], EPS, None,
                                  ALU.add),
    ]:
        k.op("vector", f, reads=(S_stat,), writes=(S_stat,))
    k.op("scalar", lambda e: e.activation(stat[:, 5:6], stat[:, 2:3], AF.Ln),
         reads=(S_stat,), writes=(S_stat,))
    k.op("scalar", lambda e: e.activation(stat[:, 6:7], stat[:, 5:6], AF.Exp,
                                          scale=-0.5),
         reads=(S_stat,), writes=(S_stat,))
    k.op("vector", lambda e: e.tensor_tensor(stat[:, 7:8], stat[:, 1:2],
                                             stat[:, 6:7], ALU.mult),
         reads=(S_stat,), writes=(S_stat,))
    k.op("vector", lambda e: e.tensor_scalar(stat[:, 7:8], stat[:, 7:8], -1.0,
                                             None, ALU.mult),
         reads=(S_stat,), writes=(S_stat,))


def _s0(nc, k, x_ext, ln1_fm, S_ln1, CN):
    c_eyeb, S_eyeb = CN["c_eyeb"]
    NTile = TEXT // 128
    with ExitStack() as es:
        E = es.enter_context
        sx = [E(nc.sbuf_tensor(f"s0x{i}", [128, DM], F32)) for i in range(2)]
        Sx = [Slot(f"s0x{i}") for i in range(2)]
        sln = [E(nc.sbuf_tensor(f"s0l{i}", [128, DM], BF16)) for i in range(2)]
        Sl = [Slot(f"s0l{i}") for i in range(2)]
        stat = E(nc.sbuf_tensor("s0stat", [128, 8], F32))
        Sst = Slot("s0stat")
        ps = [E(nc.psum_tensor(f"s0p{i}", [128, 128], BF16)) for i in range(4)]
        Sp = [Slot(f"s0p{i}") for i in range(4)]
        for i in range(NTile):
            p = i % 2
            k.dma(sx[p][:, :], x_ext[i * 128:(i + 1) * 128, :], writes=(Sx[p],))
            _ln_stats(nc, k, sx[p], Sx[p], stat, Sst, sln[p], Sl[p])
            k.op("scalar",
                 lambda e, p=p: e.activation(sln[p][:, :], sx[p][:, :], AF.Identity,
                                             bias=stat[:, 7:8], scale=stat[:, 6:7]),
                 reads=(Sx[p], Sst), writes=(Sl[p],))
            for kt in range(32):
                b = kt % 4
                k.op("tensor",
                     lambda e, p=p, kt=kt, b=b: e.transpose(
                         ps[b][:, :], sln[p][:, kt * 128:(kt + 1) * 128],
                         c_eyeb[:, :]),
                     reads=(Sl[p], S_eyeb), writes=(Sp[b],))
                k.op("vector",
                     lambda e, i=i, kt=kt, b=b: e.tensor_copy(
                         ln1_fm[:, kt * TEXT + i * 128: kt * TEXT + (i + 1) * 128],
                         ps[b][:, :]),
                     reads=(Sp[b],), writes=(S_ln1,))


def _s1(nc, k, wT, ln1_fm, S_ln1, d_proj, d_dt, DS):
    NTC = [512, 512, 512, 512, 128]
    with ExitStack() as es:
        E = es.enter_context
        swf = E(nc.sbuf_tensor("s1wf", [128, DM], F32))
        Swf = Slot("s1wf")
        swb = [E(nc.sbuf_tensor(f"s1wb{i}", [128, DM], BF16)) for i in range(2)]
        Swb = [Slot(f"s1wb{i}") for i in range(2)]
        spj = [E(nc.sbuf_tensor(f"s1pj{i}", [128, TEXT], BF16)) for i in range(2)]
        Spj = [Slot(f"s1pj{i}") for i in range(2)]
        sdt = E(nc.sbuf_tensor("s1dt", [128, TEXT], F32))
        Sdt = Slot("s1dt")
        pss = [E(nc.psum_tensor(f"s1p{i}", [128, 512], F32)) for i in range(5)]
        Sps = [Slot(f"s1p{i}") for i in range(5)]

        for ot in range(145):
            p = ot % 2
            src = wT[:, ot * 128:(ot + 1) * 128].rearrange(
                "(t p) o -> p t o", p=128)
            k.dma(swf[:, 0:DM // 2], src[:, 0:16, :], writes=(Swf,))
            k.dma(swf[:, DM // 2:DM], src[:, 16:32, :], writes=(Swf,))
            k.op("vector", lambda e, p=p: e.tensor_copy(swb[p][:, :], swf[:, :]),
                 reads=(Swf,), writes=(Swb[p],))

            def mm(e, p=p):
                last = None
                for kt in range(32):
                    for tc, n in enumerate(NTC):
                        o0 = sum(NTC[:tc])
                        last = e.matmul(
                            pss[tc][:, 0:n],
                            swb[p][:, kt * 128:(kt + 1) * 128],
                            ln1_fm[:, kt * TEXT + o0: kt * TEXT + o0 + n],
                            start=(kt == 0), stop=(kt == 31))
                return last
            k.op("tensor", mm, reads=(Swb[p], S_ln1), writes=tuple(Sps))

            if ot < 144:
                def evac(e, p=p):
                    last = None
                    for tc, n in enumerate(NTC):
                        o0 = sum(NTC[:tc])
                        last = e.copy(spj[p][:, o0:o0 + n], pss[tc][:, 0:n])
                    return last
                k.op("scalar", evac, reads=tuple(Sps), writes=(Spj[p],))
                k.dma(d_proj[ot], spj[p][:, :], reads=(Spj[p],),
                      writes=(DS["d_proj"],))
            else:
                def evacdt(e):
                    last = None
                    for tc, n in enumerate(NTC):
                        o0 = sum(NTC[:tc])
                        last = e.copy(sdt[:, o0:o0 + n], pss[tc][:, 0:n])
                    return last
                k.op("scalar", evacdt, reads=tuple(Sps), writes=(Sdt,))
                k.dma(d_dt, sdt[:, :], reads=(Sdt,), writes=(DS["d_dt"],))


def _s2a(nc, k, DS, d_proj, d_xbca, cb_row, CN):
    c_cw, S_cw = CN["c_cw"]
    c_onesrow, S_onesrow = CN["c_onesrow"]
    with ExitStack() as es:
        E = es.enter_context
        cdg = E(nc.sbuf_tensor("cdg", [128, 40 * 128], BF16))
        Scdg = Slot("cdg")
        sxc = [E(nc.sbuf_tensor(f"s2ax{i}", [128, 10 * TEXT], BF16)) for i in range(2)]
        Sxc = [Slot(f"s2ax{i}") for i in range(2)]
        sxo = [E(nc.sbuf_tensor(f"s2ao{i}", [128, 10 * NT], BF16)) for i in range(2)]
        Sxo = [Slot(f"s2ao{i}") for i in range(2)]
        cbf = E(nc.sbuf_tensor("s2acbf", [1, 1280], F32))
        Scbf = Slot("s2acbf")
        cbb = E(nc.sbuf_tensor("s2acbb", [1, 1280], BF16))
        Scbb = Slot("s2acbb")
        psC = [E(nc.psum_tensor(f"s2ap{i}", [128, 512], F32)) for i in range(8)]
        SpC = [Slot(f"s2ap{i}") for i in range(8)]

        ones4 = bass.AP(c_onesrow[:, :].tensor, 0, [[128, 1], [0, 4], [1, 128]])
        for og in range(8):
            p = og % 2
            k.dma(cbf[:, :], cb_row[0:1, og * 1280:(og + 1) * 1280],
                  writes=(Scbf,))
            k.op("vector", lambda e: e.tensor_copy(cbb[:, :], cbf[:, :]),
                 reads=(Scbf,), writes=(Scbb,))
            for i in range(10):
                ot = og * 10 + i
                k.dma(sxc[p][:, i * TEXT:(i + 1) * TEXT], d_proj[64 + ot],
                      reads=(DS["d_proj"],), writes=(Sxc[p],))
            def build_diag(e, og=og):
                last = None
                for i in range(10):
                    for j in range(4):
                        idx = (og * 10 + i) * 4 + j
                        last = e.affine_select(
                            cdg[:, (i * 4 + j) * 128:(i * 4 + j + 1) * 128],
                            c_cw[:, idx:idx + 1].broadcast_to([128, 128]),
                            [[-1, 128]], ALU.is_equal, 0.0,
                            base=0, channel_multiplier=1)
                return last
            k.op("gpsimd", build_diag, reads=(S_cw,), writes=(Scdg,))

            for i in range(10):
                ot = og * 10 + i
                pb = (i % 2) * 4
                def convmm(e, i=i, ot=ot, pb=pb, p=p):
                    last = None
                    for tc in range(4):
                        t0 = PAD + tc * 512
                        for j in range(4):
                            last = e.matmul(
                                psC[pb + tc][:, :],
                                cdg[:, (i * 4 + j) * 128:(i * 4 + j + 1) * 128],
                                sxc[p][:, i * TEXT + t0 + j - 3 + 3:
                                       i * TEXT + t0 + j + 512],
                                start=(j == 0), stop=False)
                        last = e.matmul(
                            psC[pb + tc][:, :],
                            cbb[0:1, i * 128:(i + 1) * 128],
                            ones4,
                            start=False, stop=True)
                    return last
                k.op("tensor", convmm, reads=(Scdg, Sxc[p], S_onesrow, Scbb),
                     writes=tuple(SpC[pb + tc] for tc in range(4)))
                def silu(e, i=i, pb=pb, p=p):
                    last = None
                    for tc in range(4):
                        last = e.activation(
                            sxo[p][:, i * NT + tc * 512:i * NT + (tc + 1) * 512],
                            psC[pb + tc][:, :], AF.Silu)
                    return last
                k.op("scalar", silu, reads=tuple(SpC[pb + tc] for tc in range(4)),
                     writes=(Sxo[p],))
            for i in range(10):
                ot = og * 10 + i
                k.dma(d_xbca[ot], sxo[p][:, i * NT:(i + 1) * NT],
                      reads=(Sxo[p],), writes=(DS["d_xbca"],))


def _s2b(nc, k, DS, d_xbca, d_dt, d_xdt, d_prev, d_row, CN, P):
    c_dtb, S_dtb = CN["c_dtb"]
    c_acol, S_acol = CN["c_acol"]
    c_eyef, S_eyef = CN["c_eyef"]
    c_eyeb, S_eyeb = CN["c_eyeb"]
    cdbc, S_cdbc = CN["cdbc"]
    da_all, S_daall = P["da_all"]
    expcs_all, S_ecs = P["expcs_all"]
    cumdec, S_cumdec = P["cumdec"]
    big, S_big = P["big"]
    csum, S_csum = P["csum"]
    with ExitStack() as es:
        E = es.enter_context
        sxa = E(nc.sbuf_tensor("s2bx", [128, 72 * 128], BF16))
        Sxa = Slot("s2bx")
        sdt = E(nc.sbuf_tensor("s2bdt", [128, 128], F32))
        Sdt = Slot("s2bdt")
        dtw = E(nc.sbuf_tensor("s2bw", [128, 7 * 128], F32))
        Sdtw = Slot("s2bw")
        zz = E(nc.sbuf_tensor("s2bz", [128, 128], F32))
        Szz = Slot("s2bz")
        xdt_sb = E(nc.sbuf_tensor("s2bxd", [128, INTER], BF16))
        Sxdt = Slot("s2bxd")
        xdtw_sb = E(nc.sbuf_tensor("s2bxw", [128, INTER], BF16))
        Sxdtw = Slot("s2bxw")
        btm = E(nc.sbuf_tensor("s2bbt", [128, 8 * 128], BF16))
        Sbtm = Slot("s2bbt")
        st_all = E(nc.sbuf_tensor("s2bst", [128, INTER], BF16))
        S_stall = Slot("s2bst")
        prevb = E(nc.sbuf_tensor("s2bpv", [128, INTER], BF16))
        S_prevb = Slot("s2bpv")
        psT = [E(nc.psum_tensor(f"s2bt{i}", [128, 128], BF16)) for i in range(2)]
        SpT = [Slot(f"s2bt{i}") for i in range(2)]
        psF = E(nc.psum_tensor("s2bf", [128, 128], F32))
        SpF = Slot("s2bf")
        psS = [E(nc.psum_tensor(f"s2bs{i}", [128, 1024], F32)) for i in range(2)]
        SpS = [Slot(f"s2bs{i}") for i in range(2)]

        k.op("vector", lambda e: e.memset(zz[:, :], 0.0), writes=(Szz,))

        DT, DA, CS, DE, DTT, DET, WC = range(7)
        def col(i):
            return dtw[:, i * 128:(i + 1) * 128]

        for c in range(NCH):
            for s4 in range(4):
                k.dma(sxa[:, s4 * 18 * 128:(s4 + 1) * 18 * 128],
                      d_xbca[s4 * 18:(s4 + 1) * 18]
                      .transpose([1, 0, 2])[:, :, c * 128:(c + 1) * 128],
                      reads=(DS["d_xbca"],), writes=(Sxa,))
            k.dma(sdt[:, :], d_dt[:, 128 + c * 128:128 + (c + 1) * 128],
                  reads=(DS["d_dt"],), writes=(Sdt,))

            k.op("scalar", lambda e: e.activation(col(DT), sdt[:, :], AF.Exp,
                                                  bias=c_dtb[:, :]),
                 reads=(Sdt, S_dtb), writes=(Sdtw,))
            k.op("scalar", lambda e: e.activation(col(DT), col(DT), AF.Ln, bias=1.0),
                 reads=(Sdtw,), writes=(Sdtw,))
            k.op("vector", lambda e: e.tensor_scalar(col(DT), col(DT), 1e-3,
                                                     100.0, ALU.max, ALU.min),
                 reads=(Sdtw,), writes=(Sdtw,))
            k.op("vector", lambda e: e.tensor_scalar(col(DA), col(DT),
                                                     c_acol[:, :], -80.0,
                                                     ALU.mult, ALU.max),
                 reads=(Sdtw, S_acol), writes=(Sdtw,))
            k.op("vector", lambda e: e.tensor_tensor_scan(col(CS), col(DA),
                                                          zz[:, :], 0.0,
                                                          ALU.add, ALU.add),
                 reads=(Sdtw, Szz), writes=(Sdtw,))
            k.op("vector",
                 lambda e, c=c: e.tensor_copy(da_all[:, c * 128:(c + 1) * 128],
                                              col(DA)),
                 reads=(Sdtw,), writes=(S_daall,))
            k.op("scalar",
                 lambda e, c=c: e.activation(expcs_all[:, c * 128:(c + 1) * 128],
                                             col(CS), AF.Exp),
                 reads=(Sdtw,), writes=(S_ecs,))
            k.op("scalar", lambda e, c=c: e.activation(cumdec[:, c:c + 1],
                                                       csum[:, 0:1], AF.Exp),
                 reads=(S_csum,), writes=(S_cumdec,))
            k.op("scalar", lambda e: e.activation(col(DE), col(CS), AF.Exp,
                                                  bias=col(CS)[:, 127:128],
                                                  scale=-1.0),
                 reads=(Sdtw,), writes=(Sdtw,))
            k.op("vector", lambda e: e.tensor_tensor(csum[:, 0:1], csum[:, 0:1],
                                                     col(CS)[:, 127:128], ALU.add),
                 reads=(Sdtw,), writes=(S_csum,))
            k.op("tensor", lambda e: e.transpose(psF[:, :], col(DT), c_eyef[:, :]),
                 reads=(Sdtw, S_eyef), writes=(SpF,))
            k.op("vector", lambda e: e.tensor_copy(col(DTT), psF[:, :]),
                 reads=(SpF,), writes=(Sdtw,))
            k.op("tensor", lambda e: e.transpose(psF[:, :], col(DE), c_eyef[:, :]),
                 reads=(Sdtw, S_eyef), writes=(SpF,))
            k.op("vector", lambda e: e.tensor_copy(col(DET), psF[:, :]),
                 reads=(SpF,), writes=(Sdtw,))
            k.op("vector", lambda e: e.tensor_tensor(col(WC), col(DTT),
                                                     col(DET), ALU.mult),
                 reads=(Sdtw,), writes=(Sdtw,))

            for tp in range(64):
                pb = tp % 2
                k.op("tensor",
                     lambda e, tp=tp, pb=pb: e.transpose(
                         psT[pb][:, :], sxa[:, tp * 128:(tp + 1) * 128],
                         c_eyeb[:, :]),
                     reads=(Sxa, S_eyeb), writes=(SpT[pb],))
                def vxdt(e, tp=tp, pb=pb):
                    h0, h1 = 2 * tp, 2 * tp + 1
                    e.tensor_scalar(xdt_sb[:, tp * 128:tp * 128 + 64],
                                    psT[pb][:, 0:64], col(DTT)[:, h0:h0 + 1],
                                    None, ALU.mult)
                    e.tensor_scalar(xdt_sb[:, tp * 128 + 64:tp * 128 + 128],
                                    psT[pb][:, 64:128], col(DTT)[:, h1:h1 + 1],
                                    None, ALU.mult)
                    e.tensor_scalar(xdtw_sb[:, tp * 128:tp * 128 + 64],
                                    psT[pb][:, 0:64], col(WC)[:, h0:h0 + 1],
                                    None, ALU.mult)
                    return e.tensor_scalar(xdtw_sb[:, tp * 128 + 64:tp * 128 + 128],
                                           psT[pb][:, 64:128], col(WC)[:, h1:h1 + 1],
                                           None, ALU.mult)
                k.op("vector", vxdt, reads=(SpT[pb], Sdtw), writes=(Sxdt, Sxdtw))
            k.dma(d_xdt[c], xdt_sb[:, :], reads=(Sxdt,), writes=(DS["d_xdt"],))

            for g in range(8):
                pb = g % 2
                k.op("tensor",
                     lambda e, g=g, pb=pb: e.transpose(
                         psT[pb][:, :], sxa[:, (64 + g) * 128:(65 + g) * 128],
                         c_eyeb[:, :]),
                     reads=(Sxa, S_eyeb), writes=(SpT[pb],))
                k.op("vector",
                     lambda e, g=g, pb=pb: e.tensor_copy(
                         btm[:, g * 128:(g + 1) * 128], psT[pb][:, :]),
                     reads=(SpT[pb],), writes=(Sbtm,))

            k.op("vector", lambda e: e.tensor_copy(prevb[:, :], big[:, :]),
                 reads=(S_big,), writes=(S_prevb,))
            k.dma(d_prev[c], prevb[:, :], reads=(S_prevb,), writes=(DS["d_prev"],))

            cdcol = expcs_all[:, c * 128 + 127:c * 128 + 128]
            k.dma(d_row[0:1, 0:128], cdcol, reads=(S_ecs,), writes=(DS["d_row"],))
            k.dma(cdbc[:, :],
                  bass.AP(d_row.tensor, 0, [[16384, 1], [0, 128], [1, 128]]),
                  reads=(DS["d_row"],), writes=(S_cdbc,))
            rep = bass.AP(cdbc[:, :].tensor, 0, [[128, 128], [1, 128], [0, 64]])
            k.op("vector", lambda e, rep=rep: e.tensor_tensor(
                     big[:, :], big[:, :], rep, ALU.mult),
                 reads=(S_cdbc,), writes=(S_big,))

            for g in range(8):
                pb = g % 2
                def stmm(e, g=g, pb=pb):
                    last = None
                    for hh in range(16):
                        h = g * 16 + hh
                        last = e.matmul(
                            psS[pb][:, hh * 64:(hh + 1) * 64],
                            btm[:, g * 128:(g + 1) * 128],
                            xdtw_sb[:, h * 64:(h + 1) * 64],
                            start=True, stop=True)
                    return last
                k.op("tensor", stmm, reads=(Sbtm, Sxdtw), writes=(SpS[pb],))
                k.op("scalar",
                     lambda e, g=g, pb=pb: e.copy(
                         st_all[:, g * 1024:(g + 1) * 1024], psS[pb][:, :]),
                     reads=(SpS[pb],), writes=(S_stall,))
            k.op("vector", lambda e: e.tensor_tensor(big[:, :], big[:, :],
                                                     st_all[:, :], ALU.add),
                 reads=(S_stall,), writes=(S_big,))


def _s3(nc, k, DS, cc_in_t, cc_out_t, d_row, CN, P):
    c_masks, S_masks = CN["c_masks"]
    c_invm, S_invm = CN["c_invm"]
    cdbc, S_cdbc = CN["cdbc"]
    big, S_big = P["big"]
    csum, S_csum = P["csum"]
    with ExitStack() as es:
        E = es.enter_context
        ccsb = E(nc.sbuf_tensor("s3cc", [128, 8256], BF16))
        Scc = Slot("s3cc")
        shard = E(nc.sbuf_tensor("s3sh", [128, 8256], BF16))
        Ssh = Slot("s3sh")
        dsel = E(nc.sbuf_tensor("s3ds", [128, 4], F32))
        Sds = Slot("s3ds")
        ssel = E(nc.sbuf_tensor("s3ss", [128, INTER], BF16))
        Sss = Slot("s3ss")

        k.op("scalar", lambda e: e.activation(csum[:, 1:2], csum[:, 0:1], AF.Exp),
             reads=(S_csum,), writes=(S_csum,))
        k.op("vector", lambda e: e.tensor_copy(ccsb[:, 0:INTER], big[:, :]),
             reads=(S_big,), writes=(Scc,))
        k.op("vector", lambda e: e.memset(ccsb[:, INTER:8256], 0.0),
             writes=(Scc,))
        k.op("vector", lambda e: e.tensor_copy(ccsb[:, INTER:INTER + 1],
                                               csum[:, 1:2]),
             reads=(S_csum,), writes=(Scc,))
        k.dma(cc_in_t.ap(), ccsb[:, :], reads=(Scc,), writes=(DS["cc"],))
        k.op("gpsimd",
             lambda e: e.collective_compute(
                 "AllGather", ALU.bypass,
                 replica_groups=[[0, 1, 2, 3], [4, 5, 6, 7]],
                 ins=[cc_in_t.ap().opt()], outs=[cc_out_t.ap().opt()]),
             reads=(DS["cc"],), writes=(DS["cc"],))
        k.op("vector", lambda e: e.memset(big[:, :], 0.0),
             reads=(DS["cc"],), writes=(S_big,))
        for j in range(4):
            k.dma(shard[:, :], cc_out_t.ap()[j * 128:(j + 1) * 128, :],
                  reads=(DS["cc"],), writes=(Ssh,))
            k.op("vector",
                 lambda e, j=j: e.scalar_tensor_tensor(
                     dsel[:, j:j + 1], shard[:, INTER:INTER + 1],
                     c_masks[:, j:j + 1], c_invm[:, j:j + 1],
                     ALU.mult, ALU.add),
                 reads=(Ssh, S_masks, S_invm), writes=(Sds,))
            k.dma(d_row[0:1, 0:128], dsel[:, j:j + 1], reads=(Sds,),
                  writes=(DS["d_row"],))
            k.dma(cdbc[:, :],
                  bass.AP(d_row.tensor, 0, [[16384, 1], [0, 128], [1, 128]]),
                  reads=(DS["d_row"],), writes=(S_cdbc,))
            rep = bass.AP(cdbc[:, :].tensor, 0, [[128, 128], [1, 128], [0, 64]])
            k.op("vector", lambda e, rep=rep: e.tensor_tensor(
                     big[:, :], big[:, :], rep, ALU.mult),
                 reads=(S_cdbc,), writes=(S_big,))
            k.op("vector", lambda e, j=j: e.tensor_scalar(
                     ssel[:, :], shard[:, 0:INTER], c_masks[:, j:j + 1],
                     None, ALU.mult),
                 reads=(Ssh, S_masks), writes=(Sss,))
            k.op("vector", lambda e: e.tensor_tensor(big[:, :], big[:, :],
                                                     ssel[:, :], ALU.add),
                 reads=(Sss,), writes=(S_big,))


def _s4(nc, k, DS, d_proj, d_xbca, d_xdt, d_prev, d_ynf, d_row, d_rowb, CN, P):
    c_eyef, S_eyef = CN["c_eyef"]
    c_eyeb, S_eyeb = CN["c_eyeb"]
    c_dpair, S_dpair = CN["c_dpair"]
    c_normw, S_normw = CN["c_normw"]
    c_onescol, S_onescol = CN["c_onescol"]
    cdbc, S_cdbc = CN["cdbc"]
    da_all, S_daall = P["da_all"]
    expcs_all, S_ecs = P["expcs_all"]
    cumdec, S_cumdec = P["cumdec"]
    big, S_big = P["big"]
    with ExitStack() as es:
        E = es.enter_context
        sgate = E(nc.sbuf_tensor("s4g", [128, INTER], BF16))
        Sg = Slot("s4g")
        sxa = E(nc.sbuf_tensor("s4xa", [128, 80 * 128], BF16))
        Sxa = Slot("s4xa")
        sxdt = E(nc.sbuf_tensor("s4xd", [128, INTER], BF16))
        Sxd = Slot("s4xd")
        sprev = E(nc.sbuf_tensor("s4pv", [128, INTER], BF16))
        Spv = Slot("s4pv")
        prevf = E(nc.sbuf_tensor("s4pf", [128, INTER], BF16))
        Spf = Slot("s4pf")
        abc = E(nc.sbuf_tensor("s4abc", [128, 32 * 128], F32))
        Sabc = Slot("s4abc")
        ebc = E(nc.sbuf_tensor("s4ebc", [128, 32 * 128], BF16))
        Sebc = Slot("s4ebc")
        ebf = E(nc.sbuf_tensor("s4ebf", [128, 128], BF16))
        Sebf = Slot("s4ebf")
        scg = E(nc.sbuf_tensor("s4scg", [128, 8 * 128], BF16))
        Sscg = Slot("s4scg")
        ainv = E(nc.sbuf_tensor("s4ai", [128, 128], F32))
        Sai = Slot("s4ai")
        lm = [E(nc.sbuf_tensor(f"s4lm{i}", [128, 128], F32)) for i in range(2)]
        Slm = [Slot(f"s4lm{i}") for i in range(2)]
        scl = [E(nc.sbuf_tensor(f"s4sl{i}", [128, 128], BF16)) for i in range(4)]
        Sscl = [Slot(f"s4sl{i}") for i in range(4)]
        cfe = [E(nc.sbuf_tensor(f"s4cf{i}", [128, 128], BF16)) for i in range(4)]
        Scfe = [Slot(f"s4cf{i}") for i in range(4)]
        ych = E(nc.sbuf_tensor("s4y", [128, INTER], BF16))
        Sy = Slot("s4y")
        ynf = E(nc.sbuf_tensor("s4yn", [128, INTER], BF16))
        Syn = Slot("s4yn")
        gn = E(nc.sbuf_tensor("s4gn", [1, 1024], F32))
        Sgn = Slot("s4gn")
        gnbc = E(nc.sbuf_tensor("s4gnb", [128, 1024], F32))
        Sgnb = Slot("s4gnb")
        if DEBUG:
            d_ych = nc.dram_tensor("dbg_ych", [NCH, 128, INTER], BF16,
                                   kind="ExternalOutput").ap()
            S_dych = Slot("d_ych")
            d_ygt = nc.dram_tensor("dbg_ygt", [NCH, 128, INTER], BF16,
                                   kind="ExternalOutput").ap()
            S_dygt = Slot("d_ygt")
            d_gnr = nc.dram_tensor("dbg_gnr", [NCH, 1, 1024], F32,
                                   kind="ExternalOutput").ap()
            S_dgnr = Slot("d_gnr")
        psSc = [E(nc.psum_tensor(f"s4pc{i}", [128, 128], F32)) for i in range(2)]
        SpSc = [Slot(f"s4pc{i}") for i in range(2)]
        psY = [E(nc.psum_tensor(f"s4py{i}", [128, 128], F32)) for i in range(2)]
        SpY = [Slot(f"s4py{i}") for i in range(2)]
        psF = E(nc.psum_tensor("s4pfp", [128, 128], F32))
        SpF = Slot("s4pfp")
        psGN = E(nc.psum_tensor("s4pgn", [1, 1024], F32))
        SpGN = Slot("s4pgn")

        for c in range(NCH):
            w0 = PAD + LOOK + c * 128
            for s4 in range(4):
                k.dma(sgate[:, s4 * 16 * 128:(s4 + 1) * 16 * 128],
                      d_proj[s4 * 16:(s4 + 1) * 16]
                      .transpose([1, 0, 2])[:, :, w0:w0 + 128],
                      reads=(DS["d_proj"],), writes=(Sg,))
                k.dma(sxa[:, s4 * 20 * 128:(s4 + 1) * 20 * 128],
                      d_xbca[s4 * 20:(s4 + 1) * 20]
                      .transpose([1, 0, 2])[:, :, c * 128:(c + 1) * 128],
                      reads=(DS["d_xbca"],), writes=(Sxa,))
            k.dma(sxdt[:, :], d_xdt[c], reads=(DS["d_xdt"],), writes=(Sxd,))
            k.dma(sprev[:, :], d_prev[c], reads=(DS["d_prev"],), writes=(Spv,))

            k.dma(d_row[0:1, 0:128], cumdec[:, c:c + 1], reads=(S_cumdec,),
                  writes=(DS["d_row"],))
            k.dma(cdbc[:, :],
                  bass.AP(d_row.tensor, 0, [[16384, 1], [0, 128], [1, 128]]),
                  reads=(DS["d_row"],), writes=(S_cdbc,))
            rep = bass.AP(cdbc[:, :].tensor, 0, [[128, 128], [1, 128], [0, 64]])
            k.op("vector", lambda e, rep=rep: e.tensor_tensor(
                     prevf[:, :], big[:, :], rep, ALU.mult),
                 reads=(S_big, S_cdbc), writes=(Spf,))
            k.op("vector", lambda e: e.tensor_tensor(prevf[:, :], prevf[:, :],
                                                     sprev[:, :], ALU.add),
                 reads=(Spv, Spf), writes=(Spf,))

            for g in range(8):
                pb = g % 2
                k.op("tensor",
                     lambda e, g=g, pb=pb: e.matmul(
                         psSc[pb][:, :],
                         sxa[:, (64 + g) * 128:(65 + g) * 128],
                         sxa[:, (72 + g) * 128:(73 + g) * 128],
                         start=True, stop=True),
                     reads=(Sxa,), writes=(SpSc[pb],))
                k.op("scalar",
                     lambda e, g=g, pb=pb: e.copy(scg[:, g * 128:(g + 1) * 128],
                                                  psSc[pb][:, :]),
                     reads=(SpSc[pb],), writes=(Sscg,))

            k.dma(d_row, da_all[:, c * 128:(c + 1) * 128], reads=(S_daall,),
                  writes=(DS["d_row"],))
            k.op("vector",
                 lambda e, c=c: e.tensor_copy(ebf[:, :],
                                              expcs_all[:, c * 128:(c + 1) * 128]),
                 reads=(S_ecs,), writes=(Sebf,))
            k.dma(d_rowb, ebf[:, :], reads=(Sebf,), writes=(DS["d_rowb"],))
            k.op("tensor",
                 lambda e, c=c: e.transpose(psF[:, :],
                                            da_all[:, c * 128:(c + 1) * 128],
                                            c_eyef[:, :]),
                 reads=(S_daall, S_eyef), writes=(SpF,))
            k.op("vector", lambda e: e.tensor_copy(ainv[:, :], psF[:, :]),
                 reads=(SpF,), writes=(Sai,))
            k.op("scalar", lambda e: e.activation(ainv[:, :], ainv[:, :],
                                                  AF.Exp, scale=-1.0),
                 reads=(Sai,), writes=(Sai,))

            for q in range(4):
                k.dma(abc[:, :],
                      bass.AP(d_row.tensor, q * 4096,
                              [[16384, 1], [0, 128], [1, 4096]]),
                      reads=(DS["d_row"],), writes=(Sabc,))
                k.op("scalar", lambda e: e.activation(abc[:, :], abc[:, :],
                                                      AF.Exp),
                     reads=(Sabc,), writes=(Sabc,))
                k.dma(ebc[:, :],
                      bass.AP(d_rowb.tensor, q * 4096,
                              [[16384, 1], [0, 128], [1, 4096]]),
                      reads=(DS["d_rowb"],), writes=(Sebc,))
                for hh in range(32):
                    h = q * 32 + hh
                    g, tp, half = h // 16, h // 2, h % 2
                    sl = h % 4
                    lmp = h % 2
                    k.op("vector",
                         lambda e, hh=hh, lmp=lmp: e.tensor_tensor_scan(
                             lm[lmp][:, :], c_eyef[:, :],
                             abc[:, hh * 128:(hh + 1) * 128], 0.0,
                             ALU.add, ALU.mult),
                         reads=(S_eyef, Sabc), writes=(Slm[lmp],))
                    k.op("vector",
                         lambda e, h=h, g=g, sl=sl, lmp=lmp:
                         e.scalar_tensor_tensor(
                             scl[sl][:, :], lm[lmp][:, :], ainv[:, h:h + 1],
                             scg[:, g * 128:(g + 1) * 128], ALU.mult, ALU.mult),
                         reads=(Slm[lmp], Sai, Sscg), writes=(Sscl[sl],))
                    k.op("vector",
                         lambda e, hh=hh, g=g, sl=sl: e.tensor_tensor(
                             cfe[sl][:, :], sxa[:, (72 + g) * 128:(73 + g) * 128],
                             ebc[:, hh * 128:(hh + 1) * 128], ALU.mult),
                         reads=(Sxa, Sebc), writes=(Scfe[sl],))
                    yb = psY[tp % 2]
                    SpYb = SpY[tp % 2]
                    def ymm(e, h=h, half=half, sl=sl, yb=yb):
                        e.matmul(yb[half * 64:(half + 1) * 64, :],
                                 sxdt[:, h * 64:(h + 1) * 64],
                                 scl[sl][:, :],
                                 start=True, stop=False)
                        return e.matmul(yb[half * 64:(half + 1) * 64, :],
                                        prevf[:, h * 64:(h + 1) * 64],
                                        cfe[sl][:, :],
                                        start=False, stop=True)
                    k.op("tensor", ymm, reads=(Sxd, Sscl[sl], Spf, Scfe[sl]),
                         writes=(SpYb,))
                    if half == 1:
                        k.op("vector",
                             lambda e, tp=tp: e.tensor_scalar(
                                 ynf[:, tp * 128:(tp + 1) * 128],
                                 sxa[:, tp * 128:(tp + 1) * 128],
                                 c_dpair[:, tp:tp + 1], None, ALU.mult),
                             reads=(Sxa, S_dpair), writes=(Syn,))
                        k.op("vector",
                             lambda e, tp=tp, yb=yb: e.tensor_tensor(
                                 ych[:, tp * 128:(tp + 1) * 128], yb[:, :],
                                 ynf[:, tp * 128:(tp + 1) * 128], ALU.add),
                             reads=(SpYb, Syn), writes=(Sy,))

            if DEBUG:
                k.dma(d_ych[c], ych[:, :], reads=(Sy,), writes=(S_dych,))
            k.op("scalar", lambda e: e.activation(sgate[:, :], sgate[:, :],
                                                  AF.Silu),
                 reads=(Sg,), writes=(Sg,))
            k.op("vector", lambda e: e.tensor_tensor(ych[:, :], ych[:, :],
                                                     sgate[:, :], ALU.mult),
                 reads=(Sg, Sy), writes=(Sy,))
            if DEBUG:
                k.dma(d_ygt[c], ych[:, :], reads=(Sy,), writes=(S_dygt,))
            k.op("scalar", lambda e: e.activation(sgate[:, :], ych[:, :],
                                                  AF.Square),
                 reads=(Sy,), writes=(Sg,))
            def gnmm(e):
                last = None
                for g in range(8):
                    for t8 in range(8):
                        last = e.matmul(psGN[0:1, g * 128:(g + 1) * 128],
                                        c_onescol[:, :],
                                        sgate[:, (8 * g + t8) * 128:
                                              (8 * g + t8 + 1) * 128],
                                        start=(t8 == 0),
                                        stop=(t8 == 7))
                return last
            k.op("tensor", gnmm, reads=(Sg, S_onescol), writes=(SpGN,))
            k.op("vector", lambda e: e.tensor_scalar(
                     gn[0:1, 0:1024], psGN[0:1, :], 1.0 / 1024.0, EPS,
                     ALU.mult, ALU.add),
                 reads=(SpGN,), writes=(Sgn,))
            k.op("scalar", lambda e: e.activation(gn[0:1, 0:1024],
                                                  gn[0:1, 0:1024], AF.Ln),
                 reads=(Sgn,), writes=(Sgn,))
            k.op("scalar", lambda e: e.activation(gn[0:1, 0:1024],
                                                  gn[0:1, 0:1024], AF.Exp,
                                                  scale=-0.5),
                 reads=(Sgn,), writes=(Sgn,))
            if DEBUG:
                k.dma(d_gnr[c], gn[0:1, 0:1024], reads=(Sgn,),
                      writes=(S_dgnr,))
            k.dma(d_row[0:1, 0:1024], gn[0:1, 0:1024], reads=(Sgn,),
                  writes=(DS["d_row"],))
            k.dma(gnbc[:, :],
                  bass.AP(d_row.tensor, 0, [[16384, 1], [0, 128], [1, 1024]]),
                  reads=(DS["d_row"],), writes=(Sgnb,))
            def vynf(e):
                last = None
                for tp in range(64):
                    g = tp // 8
                    last = e.scalar_tensor_tensor(
                        ynf[:, tp * 128:(tp + 1) * 128],
                        ych[:, tp * 128:(tp + 1) * 128],
                        c_normw[:, tp:tp + 1],
                        gnbc[:, g * 128:(g + 1) * 128],
                        ALU.mult, ALU.mult)
                return last
            k.op("vector", vynf, reads=(Sy, Sgnb, S_normw), writes=(Syn,))
            for s4 in range(4):
                k.dma(d_ynf[s4 * 16:(s4 + 1) * 16]
                      .transpose([1, 0, 2])[:, :, c * 128:(c + 1) * 128],
                      ynf[:, s4 * 16 * 128:(s4 + 1) * 16 * 128],
                      reads=(Syn,), writes=(DS["d_ynf"],))


def _s5(nc, k, DS, woutT, d_ynf, d_attn):
    with ExitStack() as es:
        E = es.enter_context
        synf = E(nc.sbuf_tensor("s5yn", [128, 64 * 1024], BF16))
        Syn = Slot("s5yn")
        swf = E(nc.sbuf_tensor("s5wf", [128, INTER], F32))
        Swf = Slot("s5wf")
        swb = [E(nc.sbuf_tensor(f"s5wb{i}", [128, INTER], BF16)) for i in range(2)]
        Swb = [Slot(f"s5wb{i}") for i in range(2)]
        sat = [E(nc.sbuf_tensor(f"s5at{i}", [128, 1024], BF16)) for i in range(2)]
        Sat = [Slot(f"s5at{i}") for i in range(2)]
        pss = [E(nc.psum_tensor(f"s5p{i}", [128, 512], F32)) for i in range(4)]
        Sps = [Slot(f"s5p{i}") for i in range(4)]

        for hf in range(2):
            for s4 in range(4):
                k.dma(synf[:, s4 * 16 * 1024:(s4 + 1) * 16 * 1024],
                      d_ynf[s4 * 16:(s4 + 1) * 16]
                      .transpose([1, 0, 2])[:, :, hf * 1024:(hf + 1) * 1024],
                      reads=(DS["d_ynf"],), writes=(Syn,))
            for og in range(32):
                p = og % 2
                src = woutT[:, og * 128:(og + 1) * 128].rearrange(
                    "(t p) o -> p t o", p=128)
                k.dma(swf[:, 0:INTER // 2], src[:, 0:32, :], writes=(Swf,))
                k.dma(swf[:, INTER // 2:INTER], src[:, 32:64, :], writes=(Swf,))
                k.op("vector",
                     lambda e, p=p: e.tensor_copy(swb[p][:, :], swf[:, :]),
                     reads=(Swf,), writes=(Swb[p],))
                def mm(e, p=p):
                    last = None
                    for kt in range(64):
                        for tc in range(2):
                            last = e.matmul(
                                pss[2 * p + tc][:, :],
                                swb[p][:, kt * 128:(kt + 1) * 128],
                                synf[:, kt * 1024 + tc * 512:
                                     kt * 1024 + (tc + 1) * 512],
                                start=(kt == 0), stop=(kt == 63))
                    return last
                k.op("tensor", mm, reads=(Swb[p], Syn),
                     writes=(Sps[2 * p], Sps[2 * p + 1]))
                def evac(e, p=p):
                    last = None
                    for tc in range(2):
                        last = e.copy(sat[p][:, tc * 512:(tc + 1) * 512],
                                      pss[2 * p + tc][:, :])
                    return last
                k.op("scalar", evac, reads=(Sps[2 * p], Sps[2 * p + 1]),
                     writes=(Sat[p],))
                k.dma(d_attn[og][:, hf * 1024:(hf + 1) * 1024], sat[p][:, :],
                      reads=(Sat[p],), writes=(DS["d_attn"],))


def _s6(nc, k, DS, x_ext, d_attn, d_x2, ln3_fm, S_ln3, CN):
    c_eyeb, S_eyeb = CN["c_eyeb"]
    with ExitStack() as es:
        E = es.enter_context
        sfm = [E(nc.sbuf_tensor(f"s6f{i}", [128, 32 * 128], BF16)) for i in range(2)]
        Sf = [Slot(f"s6f{i}") for i in range(2)]
        atm = E(nc.sbuf_tensor("s6a", [128, DM], BF16))
        Sa = Slot("s6a")
        nrm = E(nc.sbuf_tensor("s6n", [128, DM], BF16))
        Sn = Slot("s6n")
        xr = E(nc.sbuf_tensor("s6x", [128, DM], F32))
        Sx = Slot("s6x")
        x2t = [E(nc.sbuf_tensor(f"s6x2{i}", [128, DM], BF16)) for i in range(2)]
        Sx2 = [Slot(f"s6x2{i}") for i in range(2)]
        ln3t = E(nc.sbuf_tensor("s6l3", [128, DM], BF16))
        Sl3 = Slot("s6l3")
        ssq, Ssq = None, None
        stat = E(nc.sbuf_tensor("s6st", [128, 8], F32))
        Sst = Slot("s6st")
        ps = [E(nc.psum_tensor(f"s6p{i}", [128, 128], BF16)) for i in range(4)]
        Sp = [Slot(f"s6p{i}") for i in range(4)]

        for tt in range(16):
            p = tt % 2
            for s2_ in range(2):
                k.dma(sfm[p][:, s2_ * 16 * 128:(s2_ + 1) * 16 * 128],
                      d_attn[s2_ * 16:(s2_ + 1) * 16]
                      .transpose([1, 0, 2])[:, :, tt * 128:(tt + 1) * 128],
                      reads=(DS["d_attn"],), writes=(Sf[p],))
            k.dma(xr[:, :],
                  x_ext[PAD + LOOK + tt * 128:PAD + LOOK + (tt + 1) * 128, :],
                  writes=(Sx,))
            for dt_ in range(32):
                b = dt_ % 4
                k.op("tensor",
                     lambda e, dt_=dt_, p=p, b=b: e.transpose(
                         ps[b][:, :], sfm[p][:, dt_ * 128:(dt_ + 1) * 128],
                         c_eyeb[:, :]),
                     reads=(Sf[p], S_eyeb), writes=(Sp[b],))
                k.op("vector",
                     lambda e, dt_=dt_, b=b: e.tensor_copy(
                         atm[:, dt_ * 128:(dt_ + 1) * 128], ps[b][:, :]),
                     reads=(Sp[b],), writes=(Sa,))
            _ln_stats(nc, k, atm, Sa, stat, Sst, nrm, Sn)
            k.op("scalar", lambda e: e.activation(nrm[:, :], atm[:, :], AF.Identity,
                                                  bias=stat[:, 7:8],
                                                  scale=stat[:, 6:7]),
                 reads=(Sa, Sst), writes=(Sn,))
            k.op("vector", lambda e, p=p: e.tensor_tensor(x2t[p][:, :], xr[:, :],
                                                          nrm[:, :], ALU.add),
                 reads=(Sx, Sn), writes=(Sx2[p],))
            k.dma(d_x2[tt], x2t[p][:, :], reads=(Sx2[p],), writes=(DS["d_x2"],))
            _ln_stats(nc, k, x2t[p], Sx2[p], stat, Sst, nrm, Sn)
            k.op("scalar", lambda e, p=p: e.activation(ln3t[:, :], x2t[p][:, :],
                                                       AF.Identity, bias=stat[:, 7:8],
                                                       scale=stat[:, 6:7]),
                 reads=(Sx2[p], Sst), writes=(Sl3,))
            for kt in range(32):
                b = kt % 4
                k.op("tensor",
                     lambda e, kt=kt, b=b: e.transpose(
                         ps[b][:, :], ln3t[:, kt * 128:(kt + 1) * 128],
                         c_eyeb[:, :]),
                     reads=(Sl3, S_eyeb), writes=(Sp[b],))
                k.op("vector",
                     lambda e, tt=tt, kt=kt, b=b: e.tensor_copy(
                         ln3_fm[:, kt * NT + tt * 128: kt * NT + (tt + 1) * 128],
                         ps[b][:, :]),
                     reads=(Sp[b],), writes=(S_ln3,))


def _s7(nc, k, DS, gwT, bwT, lAT, lBT, ln3_fm, S_ln3, d_ffn, CN):
    c_eyeb, S_eyeb = CN["c_eyeb"]
    c_bb, S_bb = CN["c_bb"]
    with ExitStack() as es:
        E = es.enter_context
        sgwf = E(nc.sbuf_tensor("s7gwf", [128, 32 * 8], F32))
        sgw = E(nc.sbuf_tensor("s7gw", [128, 32 * 8], BF16))
        Sgw = Slot("s7gw")
        slat = E(nc.sbuf_tensor("s7la", [128, DM], BF16))
        Sla = Slot("s7la")
        slbt = E(nc.sbuf_tensor("s7lb", [128, DM], BF16))
        Slb = Slot("s7lb")
        lgt = E(nc.sbuf_tensor("s7lg", [128, 48], F32))
        Slg = Slot("s7lg")
        wfb = E(nc.sbuf_tensor("s7wfb", [128, 8], BF16))
        Swfb = Slot("s7wfb")
        wfTa = E(nc.sbuf_tensor("s7wt", [8, NT], BF16))
        Swt = Slot("s7wt")
        wrep = E(nc.sbuf_tensor("s7wr", [128, NT], BF16))
        Swr = Slot("s7wr")
        lowb = E(nc.sbuf_tensor("s7lo", [128, NT], BF16))
        Slo = Slot("s7lo")
        loww = E(nc.sbuf_tensor("s7lw", [128, NT], BF16))
        Slw = Slot("s7lw")
        sbwf = E(nc.sbuf_tensor("s7bwf", [128, DM], F32))
        Sbwf = Slot("s7bwf")
        sbwb = [E(nc.sbuf_tensor(f"s7bw{i}", [128, DM], BF16)) for i in range(2)]
        Sbw = [Slot(f"s7bw{i}") for i in range(2)]
        sff = [E(nc.sbuf_tensor(f"s7ff{i}", [128, NT], BF16)) for i in range(2)]
        Sff = [Slot(f"s7ff{i}") for i in range(2)]
        psL = E(nc.psum_tensor("s7pl", [128, 8], F32))
        SpL = Slot("s7pl")
        psW = E(nc.psum_tensor("s7pw", [8, 128], BF16))
        SpW = Slot("s7pw")
        pss = [E(nc.psum_tensor(f"s7p{i}", [128, 512], F32)) for i in range(4)]
        Sps = [Slot(f"s7p{i}") for i in range(4)]

        Sc = Slot("s7consts")
        k.dma(sgwf[:, :], gwT.rearrange("(t p) o -> p t o", p=128), writes=(Sc,))
        k.op("vector", lambda e: e.tensor_copy(sgw[:, :], sgwf[:, :]),
             reads=(Sc,), writes=(Sgw,))
        k.dma(sbwf[:, :], lAT.rearrange("(t p) o -> p t o", p=128),
              writes=(Sbwf,))
        k.op("vector", lambda e: e.tensor_copy(slat[:, :], sbwf[:, :]),
             reads=(Sbwf,), writes=(Sla,))
        k.dma(sbwf[:, :], lBT, writes=(Sbwf,))
        k.op("vector", lambda e: e.tensor_copy(slbt[:, :], sbwf[:, :]),
             reads=(Sbwf,), writes=(Slb,))

        for tt in range(16):
            def gmm(e, tt=tt):
                last = None
                for kt in range(32):
                    last = e.matmul(psL[:, :],
                                    ln3_fm[:, kt * NT + tt * 128:
                                           kt * NT + (tt + 1) * 128],
                                    sgw[:, kt * 8:(kt + 1) * 8],
                                    start=(kt == 0), stop=(kt == 31))
                return last
            k.op("tensor", gmm, reads=(S_ln3, Sgw), writes=(SpL,))
            for f in [
                lambda e: e.tensor_copy(lgt[:, 0:8], psL[:, :]),
                lambda e: e.reduce_max(lgt[:, 8:9], lgt[:, 0:8], axis=AX),
                lambda e: e.tensor_tensor(lgt[:, 16:24], lgt[:, 0:8],
                                          lgt[:, 8:9].broadcast_to([128, 8]),
                                          ALU.is_equal),
                lambda e: e.scalar_tensor_tensor(lgt[:, 24:32], lgt[:, 16:24],
                                                 -1e30, lgt[:, 0:8],
                                                 ALU.mult, ALU.add),
                lambda e: e.reduce_max(lgt[:, 9:10], lgt[:, 24:32], axis=AX),
                lambda e: e.tensor_tensor(lgt[:, 12:13], lgt[:, 8:9],
                                          lgt[:, 9:10], ALU.subtract),
            ]:
                k.op("vector", f, reads=(SpL, Slg), writes=(Slg,))
            k.op("scalar", lambda e: e.activation(lgt[:, 10:11], lgt[:, 12:13],
                                                  AF.Sigmoid),
                 reads=(Slg,), writes=(Slg,))
            for f in [
                lambda e: e.tensor_scalar(lgt[:, 11:12], lgt[:, 10:11], -1.0,
                                          1.0, ALU.mult, ALU.add),
                lambda e: e.tensor_tensor(lgt[:, 32:40], lgt[:, 24:32],
                                          lgt[:, 9:10].broadcast_to([128, 8]),
                                          ALU.is_equal),
                lambda e: e.tensor_scalar(lgt[:, 32:40], lgt[:, 32:40],
                                          lgt[:, 11:12], None, ALU.mult),
                lambda e: e.scalar_tensor_tensor(lgt[:, 40:48], lgt[:, 16:24],
                                                 lgt[:, 10:11], lgt[:, 32:40],
                                                 ALU.mult, ALU.add),
            ]:
                k.op("vector", f, reads=(Slg,), writes=(Slg,))
            k.op("vector", lambda e: e.tensor_scalar(wfb[:, :], lgt[:, 40:48],
                                                     2.0, None, ALU.mult),
                 reads=(Slg,), writes=(Swfb,))
            k.op("tensor", lambda e: e.transpose(psW[:, :], wfb[:, :],
                                                 c_eyeb[:, :]),
                 reads=(Swfb, S_eyeb), writes=(SpW,))
            k.op("vector",
                 lambda e, tt=tt: e.tensor_copy(wfTa[0:8, tt * 128:(tt + 1) * 128],
                                                psW[:, :]),
                 reads=(SpW,), writes=(Swt,))
        k.dma(wrep[:, :],
              bass.AP(wfTa[:, :].tensor, 0, [[NT, 8], [0, 16], [1, NT]]),
              reads=(Swt,), writes=(Swr,))

        def lowmm(e):
            last = None
            for kt in range(32):
                for tc in range(4):
                    last = e.matmul(pss[tc][:, :],
                                    slat[:, kt * 128:(kt + 1) * 128],
                                    ln3_fm[:, kt * NT + tc * 512:
                                           kt * NT + (tc + 1) * 512],
                                    start=(kt == 0), stop=(kt == 31))
            return last
        k.op("tensor", lowmm, reads=(Sla, S_ln3), writes=tuple(Sps))
        def lowev(e):
            last = None
            for tc in range(4):
                last = e.copy(lowb[:, tc * 512:(tc + 1) * 512], pss[tc][:, :])
            return last
        k.op("scalar", lowev, reads=tuple(Sps), writes=(Slo,))
        k.op("vector", lambda e: e.tensor_tensor(loww[:, :], lowb[:, :],
                                                 wrep[:, :], ALU.mult),
             reads=(Slo, Swr), writes=(Slw,))

        for og in range(32):
            p = og % 2
            src = bwT[:, og * 128:(og + 1) * 128].rearrange(
                "(t p) o -> p t o", p=128)
            k.dma(sbwf[:, 0:DM // 2], src[:, 0:16, :], writes=(Sbwf,))
            k.dma(sbwf[:, DM // 2:DM], src[:, 16:32, :], writes=(Sbwf,))
            k.op("vector", lambda e, p=p: e.tensor_copy(sbwb[p][:, :],
                                                        sbwf[:, :]),
                 reads=(Sbwf,), writes=(Sbw[p],))
            def bmm(e, p=p, og=og):
                last = None
                for kt in range(32):
                    for tc in range(4):
                        last = e.matmul(pss[tc][:, :],
                                        sbwb[p][:, kt * 128:(kt + 1) * 128],
                                        ln3_fm[:, kt * NT + tc * 512:
                                               kt * NT + (tc + 1) * 512],
                                        start=(kt == 0), stop=False)
                for tc in range(4):
                    last = e.matmul(pss[tc][:, :],
                                    slbt[:, og * 128:(og + 1) * 128],
                                    loww[:, tc * 512:(tc + 1) * 512],
                                    start=False, stop=True)
                return last
            k.op("tensor", bmm, reads=(Sbw[p], S_ln3, Slb, Slw), writes=tuple(Sps))
            def fevac(e, p=p, og=og):
                last = None
                for tc in range(4):
                    last = e.activation(sff[p][:, tc * 512:(tc + 1) * 512],
                                        pss[tc][:, :], AF.Identity,
                                        bias=c_bb[:, og:og + 1])
                return last
            k.op("scalar", fevac, reads=tuple(Sps) + (S_bb,), writes=(Sff[p],))
            k.dma(d_ffn[og], sff[p][:, :], reads=(Sff[p],), writes=(DS["d_ffn"],))


def _s8(nc, k, DS, d_x2, d_ffn, o_y, CN):
    c_eyeb, S_eyeb = CN["c_eyeb"]
    with ExitStack() as es:
        E = es.enter_context
        sfm = [E(nc.sbuf_tensor(f"s8f{i}", [128, 32 * 128], BF16)) for i in range(2)]
        Sf = [Slot(f"s8f{i}") for i in range(2)]
        ftm = E(nc.sbuf_tensor("s8a", [128, DM], BF16))
        Sa = Slot("s8a")
        nrm = E(nc.sbuf_tensor("s8n", [128, DM], BF16))
        Sn = Slot("s8n")
        x2r = [E(nc.sbuf_tensor(f"s8x{i}", [128, DM], BF16)) for i in range(2)]
        Sx = [Slot(f"s8x{i}") for i in range(2)]
        sout = [E(nc.sbuf_tensor(f"s8o{i}", [128, DM], F32)) for i in range(2)]
        So = [Slot(f"s8o{i}") for i in range(2)]
        stat = E(nc.sbuf_tensor("s8st", [128, 8], F32))
        Sst = Slot("s8st")
        ps = [E(nc.psum_tensor(f"s8p{i}", [128, 128], BF16)) for i in range(4)]
        Sp = [Slot(f"s8p{i}") for i in range(4)]

        for tt in range(16):
            p = tt % 2
            k.dma(sfm[p][:, 0:16 * 128],
                  d_ffn[0:16].transpose([1, 0, 2])[:, :, tt * 128:(tt + 1) * 128],
                  reads=(DS["d_ffn"],), writes=(Sf[p],))
            k.dma(sfm[p][:, 16 * 128:32 * 128],
                  d_ffn[16:32].transpose([1, 0, 2])[:, :, tt * 128:(tt + 1) * 128],
                  reads=(DS["d_ffn"],), writes=(Sf[p],))
            k.dma(x2r[p][:, :], d_x2[tt], reads=(DS["d_x2"],), writes=(Sx[p],))
            for dt_ in range(32):
                b = dt_ % 4
                k.op("tensor",
                     lambda e, dt_=dt_, p=p, b=b: e.transpose(
                         ps[b][:, :], sfm[p][:, dt_ * 128:(dt_ + 1) * 128],
                         c_eyeb[:, :]),
                     reads=(Sf[p], S_eyeb), writes=(Sp[b],))
                k.op("vector",
                     lambda e, dt_=dt_, b=b: e.tensor_copy(
                         ftm[:, dt_ * 128:(dt_ + 1) * 128], ps[b][:, :]),
                     reads=(Sp[b],), writes=(Sa,))
            _ln_stats(nc, k, ftm, Sa, stat, Sst, nrm, Sn)
            k.op("scalar", lambda e: e.activation(nrm[:, :], ftm[:, :], AF.Identity,
                                                  bias=stat[:, 7:8],
                                                  scale=stat[:, 6:7]),
                 reads=(Sa, Sst), writes=(Sn,))
            k.op("vector", lambda e, p=p: e.tensor_tensor(sout[p][:, :],
                                                          x2r[p][:, :],
                                                          nrm[:, :], ALU.add),
                 reads=(Sx[p], Sn), writes=(So[p],))
            k.dma(o_y[tt * 128:(tt + 1) * 128, :], sout[p][:, :],
                  reads=(So[p],), writes=(Slot("oy"),))


def kernel(**inputs):
    x = np.ascontiguousarray(inputs["L0_attn_hidden_states"], dtype=np.float32)
    W = np.ascontiguousarray(inputs["L0_attn_in_proj_w"], dtype=np.float32)
    cw = np.ascontiguousarray(inputs["L0_attn_conv_w"], dtype=np.float32)
    cb = np.ascontiguousarray(inputs["L0_attn_conv_b"], dtype=np.float32)
    dtb = np.ascontiguousarray(inputs["L0_attn_dt_bias"], dtype=np.float32)
    Dv = np.ascontiguousarray(inputs["L0_attn_D"], dtype=np.float32)
    nw = np.ascontiguousarray(inputs["L0_attn_norm_w"], dtype=np.float32)
    Wo = np.ascontiguousarray(inputs["L0_attn_out_proj_w"], dtype=np.float32)
    gw = np.ascontiguousarray(inputs["L0_ffn_gate_w"], dtype=np.float32)
    bw = np.ascontiguousarray(inputs["L0_ffn_base_w"], dtype=np.float32)
    bb = np.ascontiguousarray(inputs["L0_ffn_base_b"], dtype=np.float32)
    lA = np.ascontiguousarray(inputs["L0_ffn_lora_A"], dtype=np.float32)
    lB = np.ascontiguousarray(inputs["L0_ffn_lora_B"], dtype=np.float32)

    wT = np.ascontiguousarray(W.T)
    cw_cols = np.ascontiguousarray(
        cw[:, 0, :].reshape(80, 128, 4).transpose(1, 0, 2).reshape(128, 320))
    cb_row = np.ascontiguousarray(cb.reshape(1, CONV))
    dtb_col = np.ascontiguousarray(dtb.reshape(128, 1))
    d_pair = np.ascontiguousarray(
        np.repeat(Dv.reshape(64, 2, 1), 64, axis=2).transpose(1, 2, 0)
        .reshape(128, 64))
    normw_cols = np.ascontiguousarray(nw.reshape(64, 128).T)
    woutT = np.ascontiguousarray(Wo.T)
    gwT = np.ascontiguousarray(gw.T)
    bwT = np.ascontiguousarray(bw.T)
    bb_cols = np.ascontiguousarray(bb.reshape(32, 128).T)
    lAT = np.ascontiguousarray(lA.reshape(128, DM).T)
    lBT = np.ascontiguousarray(lB.transpose(0, 2, 1).reshape(128, DM))

    in_maps = []
    for c in range(8):
        b, q = c // 4, c % 4
        start = q * 2048
        xs = x[b, start:start + 2048]
        look = x[b, start - 3:start] if q > 0 else np.zeros((3, DM), np.float32)
        x_ext = np.concatenate(
            [np.zeros((PAD, DM), np.float32), look, xs], axis=0)
        m = np.zeros((128, 4), np.float32)
        m[:, :q] = 1.0
        im = 1.0 - m
        in_maps.append({
            "x_ext": np.ascontiguousarray(x_ext),
            "wT": wT, "cw_cols": cw_cols, "cb_row": cb_row,
            "dtb_col": dtb_col, "d_pair": d_pair, "normw_cols": normw_cols,
            "woutT": woutT, "gwT": gwT, "bwT": bwT, "bb_cols": bb_cols,
            "lAT": lAT, "lBT": lBT,
            "masks": np.ascontiguousarray(m),
            "invmasks": np.ascontiguousarray(im),
        })

    nc = build_nc(debug=DEBUG)
    res = run_bass_kernel_spmd(nc, in_maps, core_ids=list(range(8)))
    out = np.empty((2, 8192, DM), dtype=np.float32)
    for c in range(8):
        b, q = c // 4, c % 4
        out[b, q * 2048:(q + 1) * 2048] = res.results[c]["o_y"]
    if DEBUG:
        kernel.last_results = res
    return out


# version 06
# speedup vs baseline: 1.2744x; 1.2744x over previous
import numpy as np
import ml_dtypes
from contextlib import ExitStack

import concourse.bass as bass
import concourse.mybir as mybir
from concourse.bass_utils import run_bass_kernel_spmd

F32 = mybir.dt.float32
BF16 = mybir.dt.bfloat16
AF = mybir.ActivationFunctionType
ALU = mybir.AluOpType
AX = mybir.AxisListType.X

DM = 4096
NH = 128
HD = 64
SS = 128
NG = 8
INTER = NH * HD
CONV = INTER + 2 * NG * SS
PROJ = INTER + CONV + NH
NT = 2048
PAD, LOOK = 125, 3
TEXT = PAD + LOOK + NT
NCH, CL = 16, 128
EPS = 1e-5

DEBUG = False
STAGES = None


def on(name):
    return STAGES is None or name in STAGES


class Slot:
    __slots__ = ("name", "w", "r")
    def __init__(self, name):
        self.name = name
        self.w = None
        self.r = []


SERIALIZE = False
SERIAL_STAGES = set()
NDCH = 8


class K:
    ENGS = ("sync", "tensor", "vector", "scalar", "gpsimd")

    def __init__(self, nc):
        self.nc = nc
        self.items = {e: [] for e in self.ENGS}
        self.sems = {e: (nc.alloc_semaphore("sem_" + e), [0]) for e in self.ENGS}
        self.dch = [(nc.alloc_semaphore(f"dch{i}"), [0]) for i in range(NDCH)]
        self.drr = 0
        self.seen = {e: {} for e in self.ENGS}
        self.lastop = None
        self.barrier = []

    def op(self, eng, emit, reads=(), writes=(), dma=False, ch=None):
        waits = []
        def need(tok, war=False):
            if tok is None:
                return
            sem, n, src = tok
            if war and src == eng:
                return
            if self.seen[eng].get(sem.name, 0) >= n:
                return
            self.seen[eng][sem.name] = n
            waits.append((sem, n))
        for tok in self.barrier:
            need(tok)
        if SERIALIZE or getattr(self, "cur_stage", None) in SERIAL_STAGES:
            need(self.lastop)
        for s in reads:
            need(s.w)
        for s in writes:
            need(s.w)
            for t in s.r:
                need(t, war=True)
        if dma:
            if ch is None:
                ch = self.drr
                self.drr = (self.drr + 1) % NDCH
            sem, cnt = self.dch[ch]
            if cnt[0] > 0:
                need((sem, cnt[0], "dma"))
            cnt[0] += 16
            tok = (sem, cnt[0], "dma")
            ik = 16
        else:
            sem, cnt = self.sems[eng]
            cnt[0] += 1
            tok = (sem, cnt[0], eng)
            ik = 1
        self.lastop = tok
        self.items[eng].append((waits, emit, sem, ik))
        for s in reads:
            s.r.append(tok)
        for s in writes:
            s.w = tok
            s.r = []

    def flush(self, name):
        items = self.items
        self.items = {e: [] for e in self.ENGS}
        self.barrier = (
            [(self.sems[e][0], self.sems[e][1][0], "barrier")
             for e in self.ENGS if self.sems[e][1][0] > 0] +
            [(sem, cnt[0], "barrier") for (sem, cnt) in self.dch
             if cnt[0] > 0])
        if not any(items[e] for e in self.ENGS):
            return
        with self.nc.Block(name) as block:
            def mk(eng_items):
                def body(e):
                    for waits, emit, sem_, kk in eng_items:
                        for (s, n) in waits:
                            e.wait_ge(s, n)
                        instr = emit(e)
                        instr.then_inc(sem_, kk)
                return body
            for eng in self.ENGS:
                if items[eng]:
                    getattr(block, eng)(mk(items[eng]))

    def finale(self):
        chans = [(sem, cnt[0]) for (sem, cnt) in self.dch if cnt[0] > 0]
        with self.nc.Block("finale") as block:
            @block.sync
            def _(e):
                for sem, n in chans:
                    e.wait_ge(sem, n)

    def dma(self, out_ap, in_ap, reads=(), writes=(), eng="sync", ch=None):
        self.op(eng, lambda e: e.dma_start(out=out_ap, in_=in_ap),
                reads=reads, writes=writes, dma=True, ch=ch)


def build_nc(debug=False):
    nc = bass.Bass("TRN2", target_bir_lowering=False)

    x_ext = nc.dram_tensor("x_ext", [TEXT, DM], F32, kind="ExternalInput").ap()
    wT = nc.dram_tensor("wT", [145, 128, DM], F32, kind="ExternalInput").ap()
    cw_cols = nc.dram_tensor("cw_cols", [128, 80 * 4], F32, kind="ExternalInput").ap()
    cb_row = nc.dram_tensor("cb_row", [1, CONV], F32, kind="ExternalInput").ap()
    dtb_col = nc.dram_tensor("dtb_col", [128, 1], F32, kind="ExternalInput").ap()
    d_pair = nc.dram_tensor("d_pair", [128, 64], F32, kind="ExternalInput").ap()
    normw_cols = nc.dram_tensor("normw_cols", [128, 64], F32, kind="ExternalInput").ap()
    woutT = nc.dram_tensor("woutT", [32, 128, INTER], F32,
                           kind="ExternalInput").ap()
    gwT = nc.dram_tensor("gwT", [DM, 8], F32, kind="ExternalInput").ap()
    bwT = nc.dram_tensor("bwT", [32, 128, DM], F32, kind="ExternalInput").ap()
    bb_cols = nc.dram_tensor("bb_cols", [128, 32], F32, kind="ExternalInput").ap()
    lAT = nc.dram_tensor("lAT", [128, DM], F32, kind="ExternalInput").ap()
    lBT = nc.dram_tensor("lBT", [128, DM], F32, kind="ExternalInput").ap()
    masks = nc.dram_tensor("masks", [128, 4], F32, kind="ExternalInput").ap()
    invmasks = nc.dram_tensor("invmasks", [128, 4], F32, kind="ExternalInput").ap()
    o_y = nc.dram_tensor("o_y", [NT, DM], F32, kind="ExternalOutput").ap()

    d_proj = nc.dram_tensor("d_proj", [144, 128, TEXT], BF16, kind="Internal").ap()
    d_dt = nc.dram_tensor("d_dt", [128, TEXT], F32, kind="Internal").ap()
    d_xbca = nc.dram_tensor("d_xbca", [80, 128, NT], BF16, kind="Internal").ap()
    d_xdt = nc.dram_tensor("d_xdt", [NCH, 128, INTER], BF16, kind="Internal").ap()
    d_prev = nc.dram_tensor("d_prev", [NCH, 128, INTER], BF16, kind="Internal").ap()
    d_ynf = nc.dram_tensor("d_ynf", [64, 128, NT], BF16, kind="Internal").ap()
    d_attn = nc.dram_tensor("d_attn", [32, 128, NT], BF16, kind="Internal").ap()
    d_x2 = nc.dram_tensor("d_x2", [16, 128, DM], BF16, kind="Internal").ap()
    d_ffn = nc.dram_tensor("d_ffn", [32, 128, NT], BF16, kind="Internal").ap()
    d_row = nc.dram_tensor("d_row", [1, 16384], F32, kind="Internal").ap()
    d_rowb = nc.dram_tensor("d_rowb", [1, 16384], BF16, kind="Internal").ap()
    cc_in_t = nc.dram_tensor("cc_in", [128, 8256], BF16, kind="Internal")
    cc_out_t = nc.dram_tensor("cc_out", [512, 8256], BF16, kind="Internal")

    eye_f = nc.inline_tensor(np.eye(128, dtype=np.float32), name="eye_f").ap()
    eye_b = nc.inline_tensor(np.eye(128, dtype=ml_dtypes.bfloat16), name="eye_b").ap()
    acol_c = nc.inline_tensor(-np.arange(1, 129, dtype=np.float32).reshape(128, 1),
                              name="acol").ap()
    ones_b = nc.inline_tensor(np.ones((1, 128), dtype=ml_dtypes.bfloat16),
                              name="ones_b").ap()
    onescol_b = nc.inline_tensor(np.ones((128, 1), dtype=ml_dtypes.bfloat16),
                                 name="onescol_b").ap()

    k = K(nc)
    DS = {n: Slot(n) for n in ["d_proj", "d_dt", "d_xbca", "d_xdt", "d_prev",
                               "d_ynf", "d_attn", "d_x2", "d_ffn", "d_row",
                               "d_rowb", "cc"]}

    with ExitStack() as es:
        E = es.enter_context

        def sb(name, shape, dt_):
            t = E(nc.sbuf_tensor(name, shape, dt_))
            return t, Slot(name)

        c_eyef, S_eyef = sb("c_eyef", [128, 128], F32)
        c_eyeb, S_eyeb = sb("c_eyeb", [128, 128], BF16)
        c_acol, S_acol = sb("c_acol", [128, 1], F32)
        c_dtb, S_dtb = sb("c_dtb", [128, 1], F32)
        c_cw, S_cw = sb("c_cw", [128, 320], F32)
        c_onesrow, S_onesrow = sb("c_onesrow", [1, 128], BF16)
        c_onescol, S_onescol = sb("c_onescol", [128, 1], BF16)
        c_dpair, S_dpair = sb("c_dpair", [128, 64], F32)
        c_normw, S_normw = sb("c_normw", [128, 64], F32)
        c_bb, S_bb = sb("c_bb", [128, 32], F32)
        c_masks, S_masks = sb("c_masks", [128, 4], F32)
        c_invm, S_invm = sb("c_invm", [128, 4], F32)
        cdbc, S_cdbc = sb("cdbc", [128, 128], F32)

        CN = dict(c_eyef=(c_eyef, S_eyef), c_eyeb=(c_eyeb, S_eyeb),
                  c_acol=(c_acol, S_acol), c_dtb=(c_dtb, S_dtb),
                  c_cw=(c_cw, S_cw), c_onesrow=(c_onesrow, S_onesrow),
                  c_onescol=(c_onescol, S_onescol), c_dpair=(c_dpair, S_dpair),
                  c_normw=(c_normw, S_normw), c_bb=(c_bb, S_bb),
                  c_masks=(c_masks, S_masks), c_invm=(c_invm, S_invm),
                  cdbc=(cdbc, S_cdbc))

        for dst, src, sl in [(c_eyef, eye_f, S_eyef), (c_eyeb, eye_b, S_eyeb),
                             (c_acol, acol_c, S_acol), (c_dtb, dtb_col, S_dtb),
                             (c_cw, cw_cols, S_cw),
                             (c_dpair, d_pair, S_dpair),
                             (c_normw, normw_cols, S_normw),
                             (c_bb, bb_cols, S_bb), (c_masks, masks, S_masks),
                             (c_invm, invmasks, S_invm),
                             (c_onesrow, ones_b, S_onesrow),
                             (c_onescol, onescol_b, S_onescol)]:
            k.dma(dst[:, :], src, writes=(sl,))
        k.flush("consts")

        with ExitStack() as es1:
            ln1_fm = es1.enter_context(
                nc.sbuf_tensor("ln1_fm", [128, 32 * TEXT], BF16))
            S_ln1 = Slot("ln1_fm")
            k.cur_stage = "s0"
            if on("s0"):
                _s0(nc, k, x_ext, ln1_fm, S_ln1, CN)
                k.flush("s0")
            k.cur_stage = "s1"
            if on("s1"):
                _s1(nc, k, wT, ln1_fm, S_ln1, d_proj, d_dt, DS)
                k.flush("s1")

        k.cur_stage = "s2a"
        if on("s2a"):
            _s2a(nc, k, DS, d_proj, d_xbca, cb_row, CN)
            k.flush("s2a")

        with ExitStack() as es2:
            E2 = es2.enter_context
            def sb2(name, shape, dt_):
                t = E2(nc.sbuf_tensor(name, shape, dt_))
                return t, Slot(name)
            da_all, S_daall = sb2("da_all", [128, NT], F32)
            expcs_all, S_ecs = sb2("expcs_all", [128, NT], F32)
            cumdec, S_cumdec = sb2("cumdec", [128, NCH], F32)
            big, S_big = sb2("big", [128, INTER], F32)
            csum, S_csum = sb2("csum", [128, 4], F32)
            P = dict(da_all=(da_all, S_daall),
                     expcs_all=(expcs_all, S_ecs),
                     cumdec=(cumdec, S_cumdec), big=(big, S_big),
                     csum=(csum, S_csum))
            k.op("vector", lambda e: e.memset(big[:, :], 0.0), writes=(S_big,))
            k.op("vector", lambda e: e.memset(csum[:, :], 0.0), writes=(S_csum,))

            k.cur_stage = "s2b"
            if on("s2b"):
                _s2b(nc, k, DS, d_xbca, d_dt, d_xdt, d_prev, d_row, CN, P)
                k.flush("s2b")
            k.cur_stage = "s3"
            if on("s3"):
                _s3(nc, k, DS, cc_in_t, cc_out_t, d_row, CN, P)
                k.flush("s3")
            k.cur_stage = "s4"
            if on("s4"):
                _s4(nc, k, DS, d_proj, d_xbca, d_xdt, d_prev, d_ynf,
                    d_row, d_rowb, CN, P)
                k.flush("s4")

        k.cur_stage = "s5"
        if on("s5"):
            _s5(nc, k, DS, woutT, d_ynf, d_attn)
            k.flush("s5")

        with ExitStack() as es3:
            ln3_fm = es3.enter_context(
                nc.sbuf_tensor("ln3_fm", [128, 32 * NT], BF16))
            S_ln3 = Slot("ln3_fm")
            k.cur_stage = "s6"
            if on("s6"):
                _s6(nc, k, DS, x_ext, d_attn, d_x2, ln3_fm, S_ln3, CN)
                k.flush("s6")
            k.cur_stage = "s7"
            if on("s7"):
                _s7(nc, k, DS, gwT, bwT, lAT, lBT, ln3_fm, S_ln3, d_ffn, CN)
                k.flush("s7")

        k.cur_stage = "s8"
        if on("s8"):
            _s8(nc, k, DS, d_x2, d_ffn, o_y, CN)
        k.flush("s8")

        if debug:
            dbg = [("dbg_proj", d_proj, [144, 128, TEXT], BF16, "s1"),
                   ("dbg_dt", d_dt, [128, TEXT], F32, "s1"),
                   ("dbg_xbca", d_xbca, [80, 128, NT], BF16, "s2a"),
                   ("dbg_xdt", d_xdt, [NCH, 128, INTER], BF16, "s2b"),
                   ("dbg_prev", d_prev, [NCH, 128, INTER], BF16, "s2b"),
                   ("dbg_ynf", d_ynf, [64, 128, NT], BF16, "s4"),
                   ("dbg_attn", d_attn, [32, 128, NT], BF16, "s5"),
                   ("dbg_x2", d_x2, [16, 128, DM], BF16, "s6"),
                   ("dbg_ffn", d_ffn, [32, 128, NT], BF16, "s7")]
            for nm, src, shape, dt_, st in dbg:
                if not on(st):
                    continue
                o = nc.dram_tensor(nm, shape, dt_, kind="ExternalOutput").ap()
                k.dma(o, src, reads=tuple(DS.values()))
            k.flush("dbg")
        k.finale()

    return nc


def _ln_stats(nc, k, xin, S_xin, stat, S_stat, ssq, S_ssq):
    k.op("vector", lambda e: e.reduce_sum(stat[:, 0:1], xin[:, :], axis=AX),
         reads=(S_xin,), writes=(S_stat,))
    k.op("scalar", lambda e: e.activation(ssq[:, :], xin[:, :], AF.Square),
         reads=(S_xin,), writes=(S_ssq,))
    k.op("vector", lambda e: e.reduce_sum(stat[:, 4:5], ssq[:, :], axis=AX),
         reads=(S_ssq,), writes=(S_stat,))
    for f in [
        lambda e: e.tensor_scalar(stat[:, 1:2], stat[:, 0:1], 1.0 / DM, None,
                                  ALU.mult),
        lambda e: e.tensor_scalar(stat[:, 2:3], stat[:, 4:5], 1.0 / DM, None,
                                  ALU.mult),
        lambda e: e.tensor_tensor(stat[:, 3:4], stat[:, 1:2], stat[:, 1:2],
                                  ALU.mult),
        lambda e: e.tensor_tensor(stat[:, 2:3], stat[:, 2:3], stat[:, 3:4],
                                  ALU.subtract),
        lambda e: e.tensor_scalar(stat[:, 2:3], stat[:, 2:3], EPS, None,
                                  ALU.add),
    ]:
        k.op("vector", f, reads=(S_stat,), writes=(S_stat,))
    k.op("scalar", lambda e: e.activation(stat[:, 5:6], stat[:, 2:3], AF.Ln),
         reads=(S_stat,), writes=(S_stat,))
    k.op("scalar", lambda e: e.activation(stat[:, 6:7], stat[:, 5:6], AF.Exp,
                                          scale=-0.5),
         reads=(S_stat,), writes=(S_stat,))
    k.op("vector", lambda e: e.tensor_tensor(stat[:, 7:8], stat[:, 1:2],
                                             stat[:, 6:7], ALU.mult),
         reads=(S_stat,), writes=(S_stat,))
    k.op("vector", lambda e: e.tensor_scalar(stat[:, 7:8], stat[:, 7:8], -1.0,
                                             None, ALU.mult),
         reads=(S_stat,), writes=(S_stat,))


def _s0(nc, k, x_ext, ln1_fm, S_ln1, CN):
    c_eyeb, S_eyeb = CN["c_eyeb"]
    NTile = TEXT // 128
    with ExitStack() as es:
        E = es.enter_context
        sx = [E(nc.sbuf_tensor(f"s0x{i}", [128, DM], F32)) for i in range(2)]
        Sx = [Slot(f"s0x{i}") for i in range(2)]
        sln = [E(nc.sbuf_tensor(f"s0l{i}", [128, DM], BF16)) for i in range(2)]
        Sl = [Slot(f"s0l{i}") for i in range(2)]
        stat = E(nc.sbuf_tensor("s0stat", [128, 8], F32))
        Sst = Slot("s0stat")
        ps = [E(nc.psum_tensor(f"s0p{i}", [128, 128], BF16)) for i in range(4)]
        Sp = [Slot(f"s0p{i}") for i in range(4)]
        for i in range(NTile):
            p = i % 2
            k.dma(sx[p][:, :], x_ext[i * 128:(i + 1) * 128, :], writes=(Sx[p],))
            _ln_stats(nc, k, sx[p], Sx[p], stat, Sst, sln[p], Sl[p])
            k.op("scalar",
                 lambda e, p=p: e.activation(sln[p][:, :], sx[p][:, :], AF.Identity,
                                             bias=stat[:, 7:8], scale=stat[:, 6:7]),
                 reads=(Sx[p], Sst), writes=(Sl[p],))
            for kt in range(32):
                b = kt % 4
                k.op("tensor",
                     lambda e, p=p, kt=kt, b=b: e.transpose(
                         ps[b][:, :], sln[p][:, kt * 128:(kt + 1) * 128],
                         c_eyeb[:, :]),
                     reads=(Sl[p], S_eyeb), writes=(Sp[b],))
                k.op("vector",
                     lambda e, i=i, kt=kt, b=b: e.tensor_copy(
                         ln1_fm[:, kt * TEXT + i * 128: kt * TEXT + (i + 1) * 128],
                         ps[b][:, :]),
                     reads=(Sp[b],), writes=(S_ln1,))


def _s1(nc, k, wT, ln1_fm, S_ln1, d_proj, d_dt, DS):
    NTC = [512, 512, 512, 512, 128]
    with ExitStack() as es:
        E = es.enter_context
        swf = E(nc.sbuf_tensor("s1wf", [128, DM], F32))
        Swf = Slot("s1wf")
        swb = [E(nc.sbuf_tensor(f"s1wb{i}", [128, DM], BF16)) for i in range(2)]
        Swb = [Slot(f"s1wb{i}") for i in range(2)]
        spj = [E(nc.sbuf_tensor(f"s1pj{i}", [128, TEXT], BF16)) for i in range(2)]
        Spj = [Slot(f"s1pj{i}") for i in range(2)]
        sdt = E(nc.sbuf_tensor("s1dt", [128, TEXT], F32))
        Sdt = Slot("s1dt")
        pss = [E(nc.psum_tensor(f"s1p{i}", [128, 512], F32)) for i in range(5)]
        Sps = [Slot(f"s1p{i}") for i in range(5)]

        for ot in range(145):
            p = ot % 2
            k.dma(swf[:, :], wT[ot], writes=(Swf,), ch=ot % 4)
            k.op("vector", lambda e, p=p: e.tensor_copy(swb[p][:, :], swf[:, :]),
                 reads=(Swf,), writes=(Swb[p],))

            def mm(e, p=p):
                last = None
                for kt in range(32):
                    for tc, n in enumerate(NTC):
                        o0 = sum(NTC[:tc])
                        last = e.matmul(
                            pss[tc][:, 0:n],
                            swb[p][:, kt * 128:(kt + 1) * 128],
                            ln1_fm[:, kt * TEXT + o0: kt * TEXT + o0 + n],
                            start=(kt == 0), stop=(kt == 31))
                return last
            k.op("tensor", mm, reads=(Swb[p], S_ln1), writes=tuple(Sps))

            if ot < 144:
                def evac(e, p=p):
                    last = None
                    for tc, n in enumerate(NTC):
                        o0 = sum(NTC[:tc])
                        last = e.copy(spj[p][:, o0:o0 + n], pss[tc][:, 0:n])
                    return last
                k.op("scalar", evac, reads=tuple(Sps), writes=(Spj[p],))
                k.dma(d_proj[ot], spj[p][:, :], reads=(Spj[p],),
                      writes=(DS["d_proj"],), eng="scalar", ch=6 + ot % 2)
            else:
                def evacdt(e):
                    last = None
                    for tc, n in enumerate(NTC):
                        o0 = sum(NTC[:tc])
                        last = e.copy(sdt[:, o0:o0 + n], pss[tc][:, 0:n])
                    return last
                k.op("scalar", evacdt, reads=tuple(Sps), writes=(Sdt,))
                k.dma(d_dt, sdt[:, :], reads=(Sdt,), writes=(DS["d_dt"],),
                      eng="scalar", ch=6)


def _s2a(nc, k, DS, d_proj, d_xbca, cb_row, CN):
    c_cw, S_cw = CN["c_cw"]
    c_onesrow, S_onesrow = CN["c_onesrow"]
    with ExitStack() as es:
        E = es.enter_context
        cdg = E(nc.sbuf_tensor("cdg", [128, 40 * 128], BF16))
        Scdg = Slot("cdg")
        sxc = [E(nc.sbuf_tensor(f"s2ax{i}", [128, 10 * TEXT], BF16)) for i in range(2)]
        Sxc = [Slot(f"s2ax{i}") for i in range(2)]
        sxo = [E(nc.sbuf_tensor(f"s2ao{i}", [128, 10 * NT], BF16)) for i in range(2)]
        Sxo = [Slot(f"s2ao{i}") for i in range(2)]
        cbf = E(nc.sbuf_tensor("s2acbf", [1, 1280], F32))
        Scbf = Slot("s2acbf")
        cbb = E(nc.sbuf_tensor("s2acbb", [1, 1280], BF16))
        Scbb = Slot("s2acbb")
        psC = [E(nc.psum_tensor(f"s2ap{i}", [128, 512], F32)) for i in range(8)]
        SpC = [Slot(f"s2ap{i}") for i in range(8)]

        ones4 = bass.AP(c_onesrow[:, :].tensor, 0, [[128, 1], [0, 4], [1, 128]])
        for og in range(8):
            p = og % 2
            k.dma(cbf[:, :], cb_row[0:1, og * 1280:(og + 1) * 1280],
                  writes=(Scbf,))
            k.op("vector", lambda e: e.tensor_copy(cbb[:, :], cbf[:, :]),
                 reads=(Scbf,), writes=(Scbb,))
            for i in range(10):
                ot = og * 10 + i
                k.dma(sxc[p][:, i * TEXT:(i + 1) * TEXT], d_proj[64 + ot],
                      reads=(DS["d_proj"],), writes=(Sxc[p],), ch=i % 4)
            def build_diag(e, og=og):
                last = None
                for i in range(10):
                    for j in range(4):
                        idx = (og * 10 + i) * 4 + j
                        last = e.affine_select(
                            cdg[:, (i * 4 + j) * 128:(i * 4 + j + 1) * 128],
                            c_cw[:, idx:idx + 1].broadcast_to([128, 128]),
                            [[-1, 128]], ALU.is_equal, 0.0,
                            base=0, channel_multiplier=1)
                return last
            k.op("gpsimd", build_diag, reads=(S_cw,), writes=(Scdg,))

            for i in range(10):
                ot = og * 10 + i
                pb = (i % 2) * 4
                def convmm(e, i=i, ot=ot, pb=pb, p=p):
                    last = None
                    for tc in range(4):
                        t0 = PAD + tc * 512
                        for j in range(4):
                            last = e.matmul(
                                psC[pb + tc][:, :],
                                cdg[:, (i * 4 + j) * 128:(i * 4 + j + 1) * 128],
                                sxc[p][:, i * TEXT + t0 + j - 3 + 3:
                                       i * TEXT + t0 + j + 512],
                                start=(j == 0), stop=False)
                        last = e.matmul(
                            psC[pb + tc][:, :],
                            cbb[0:1, i * 128:(i + 1) * 128],
                            ones4,
                            start=False, stop=True)
                    return last
                k.op("tensor", convmm, reads=(Scdg, Sxc[p], S_onesrow, Scbb),
                     writes=tuple(SpC[pb + tc] for tc in range(4)))
                def silu(e, i=i, pb=pb, p=p):
                    last = None
                    for tc in range(4):
                        last = e.activation(
                            sxo[p][:, i * NT + tc * 512:i * NT + (tc + 1) * 512],
                            psC[pb + tc][:, :], AF.Silu)
                    return last
                k.op("scalar", silu, reads=tuple(SpC[pb + tc] for tc in range(4)),
                     writes=(Sxo[p],))
            for i in range(10):
                ot = og * 10 + i
                k.dma(d_xbca[ot], sxo[p][:, i * NT:(i + 1) * NT],
                      reads=(Sxo[p],), writes=(DS["d_xbca"],),
                      eng="scalar", ch=6 + i % 2)


def _s2b(nc, k, DS, d_xbca, d_dt, d_xdt, d_prev, d_row, CN, P):
    c_dtb, S_dtb = CN["c_dtb"]
    c_acol, S_acol = CN["c_acol"]
    c_eyef, S_eyef = CN["c_eyef"]
    c_eyeb, S_eyeb = CN["c_eyeb"]
    cdbc, S_cdbc = CN["cdbc"]
    da_all, S_daall = P["da_all"]
    expcs_all, S_ecs = P["expcs_all"]
    cumdec, S_cumdec = P["cumdec"]
    big, S_big = P["big"]
    csum, S_csum = P["csum"]
    with ExitStack() as es:
        E = es.enter_context
        sxa = E(nc.sbuf_tensor("s2bx", [128, 72 * 128], BF16))
        Sxa = Slot("s2bx")
        sdt = E(nc.sbuf_tensor("s2bdt", [128, 128], F32))
        Sdt = Slot("s2bdt")
        dtw = E(nc.sbuf_tensor("s2bw", [128, 7 * 128], F32))
        Sdtw = Slot("s2bw")
        zz = E(nc.sbuf_tensor("s2bz", [128, 128], F32))
        Szz = Slot("s2bz")
        xdt_sb = E(nc.sbuf_tensor("s2bxd", [128, INTER], BF16))
        Sxdt = Slot("s2bxd")
        xdtw_sb = E(nc.sbuf_tensor("s2bxw", [128, INTER], BF16))
        Sxdtw = Slot("s2bxw")
        btm = E(nc.sbuf_tensor("s2bbt", [128, 8 * 128], BF16))
        Sbtm = Slot("s2bbt")
        st_all = E(nc.sbuf_tensor("s2bst", [128, INTER], BF16))
        S_stall = Slot("s2bst")
        prevb = E(nc.sbuf_tensor("s2bpv", [128, INTER], BF16))
        S_prevb = Slot("s2bpv")
        psT = [E(nc.psum_tensor(f"s2bt{i}", [128, 128], BF16)) for i in range(2)]
        SpT = [Slot(f"s2bt{i}") for i in range(2)]
        psF = E(nc.psum_tensor("s2bf", [128, 128], F32))
        SpF = Slot("s2bf")
        psS = [E(nc.psum_tensor(f"s2bs{i}", [128, 1024], F32)) for i in range(2)]
        SpS = [Slot(f"s2bs{i}") for i in range(2)]

        k.op("vector", lambda e: e.memset(zz[:, :], 0.0), writes=(Szz,))

        DT, DA, CS, DE, DTT, DET, WC = range(7)
        def col(i):
            return dtw[:, i * 128:(i + 1) * 128]

        for c in range(NCH):
            for s4 in range(4):
                k.dma(sxa[:, s4 * 18 * 128:(s4 + 1) * 18 * 128],
                      d_xbca[s4 * 18:(s4 + 1) * 18]
                      .transpose([1, 0, 2])[:, :, c * 128:(c + 1) * 128],
                      reads=(DS["d_xbca"],), writes=(Sxa,), ch=s4)
            k.dma(sdt[:, :], d_dt[:, 128 + c * 128:128 + (c + 1) * 128],
                  reads=(DS["d_dt"],), writes=(Sdt,), ch=5)

            k.op("scalar", lambda e: e.activation(col(DT), sdt[:, :], AF.Exp,
                                                  bias=c_dtb[:, :]),
                 reads=(Sdt, S_dtb), writes=(Sdtw,))
            k.op("scalar", lambda e: e.activation(col(DT), col(DT), AF.Ln, bias=1.0),
                 reads=(Sdtw,), writes=(Sdtw,))
            k.op("vector", lambda e: e.tensor_scalar(col(DT), col(DT), 1e-3,
                                                     100.0, ALU.max, ALU.min),
                 reads=(Sdtw,), writes=(Sdtw,))
            k.op("vector", lambda e: e.tensor_scalar(col(DA), col(DT),
                                                     c_acol[:, :], -80.0,
                                                     ALU.mult, ALU.max),
                 reads=(Sdtw, S_acol), writes=(Sdtw,))
            k.op("vector", lambda e: e.tensor_tensor_scan(col(CS), col(DA),
                                                          zz[:, :], 0.0,
                                                          ALU.add, ALU.add),
                 reads=(Sdtw, Szz), writes=(Sdtw,))
            k.op("vector",
                 lambda e, c=c: e.tensor_copy(da_all[:, c * 128:(c + 1) * 128],
                                              col(DA)),
                 reads=(Sdtw,), writes=(S_daall,))
            k.op("scalar",
                 lambda e, c=c: e.activation(expcs_all[:, c * 128:(c + 1) * 128],
                                             col(CS), AF.Exp),
                 reads=(Sdtw,), writes=(S_ecs,))
            k.op("scalar", lambda e, c=c: e.activation(cumdec[:, c:c + 1],
                                                       csum[:, 0:1], AF.Exp),
                 reads=(S_csum,), writes=(S_cumdec,))
            k.op("scalar", lambda e: e.activation(col(DE), col(CS), AF.Exp,
                                                  bias=col(CS)[:, 127:128],
                                                  scale=-1.0),
                 reads=(Sdtw,), writes=(Sdtw,))
            k.op("vector", lambda e: e.tensor_tensor(csum[:, 0:1], csum[:, 0:1],
                                                     col(CS)[:, 127:128], ALU.add),
                 reads=(Sdtw,), writes=(S_csum,))
            k.op("tensor", lambda e: e.transpose(psF[:, :], col(DT), c_eyef[:, :]),
                 reads=(Sdtw, S_eyef), writes=(SpF,))
            k.op("vector", lambda e: e.tensor_copy(col(DTT), psF[:, :]),
                 reads=(SpF,), writes=(Sdtw,))
            k.op("tensor", lambda e: e.transpose(psF[:, :], col(DE), c_eyef[:, :]),
                 reads=(Sdtw, S_eyef), writes=(SpF,))
            k.op("vector", lambda e: e.tensor_copy(col(DET), psF[:, :]),
                 reads=(SpF,), writes=(Sdtw,))
            k.op("vector", lambda e: e.tensor_tensor(col(WC), col(DTT),
                                                     col(DET), ALU.mult),
                 reads=(Sdtw,), writes=(Sdtw,))

            for tp in range(64):
                pb = tp % 2
                k.op("tensor",
                     lambda e, tp=tp, pb=pb: e.transpose(
                         psT[pb][:, :], sxa[:, tp * 128:(tp + 1) * 128],
                         c_eyeb[:, :]),
                     reads=(Sxa, S_eyeb), writes=(SpT[pb],))
                def vxdt(e, tp=tp, pb=pb):
                    h0, h1 = 2 * tp, 2 * tp + 1
                    e.tensor_scalar(xdt_sb[:, tp * 128:tp * 128 + 64],
                                    psT[pb][:, 0:64], col(DTT)[:, h0:h0 + 1],
                                    None, ALU.mult)
                    e.tensor_scalar(xdt_sb[:, tp * 128 + 64:tp * 128 + 128],
                                    psT[pb][:, 64:128], col(DTT)[:, h1:h1 + 1],
                                    None, ALU.mult)
                    e.tensor_scalar(xdtw_sb[:, tp * 128:tp * 128 + 64],
                                    psT[pb][:, 0:64], col(WC)[:, h0:h0 + 1],
                                    None, ALU.mult)
                    return e.tensor_scalar(xdtw_sb[:, tp * 128 + 64:tp * 128 + 128],
                                           psT[pb][:, 64:128], col(WC)[:, h1:h1 + 1],
                                           None, ALU.mult)
                k.op("vector", vxdt, reads=(SpT[pb], Sdtw), writes=(Sxdt, Sxdtw))
            k.dma(d_xdt[c], xdt_sb[:, :], reads=(Sxdt,),
                  writes=(DS["d_xdt"],), eng="gpsimd", ch=6)

            for g in range(8):
                pb = g % 2
                k.op("tensor",
                     lambda e, g=g, pb=pb: e.transpose(
                         psT[pb][:, :], sxa[:, (64 + g) * 128:(65 + g) * 128],
                         c_eyeb[:, :]),
                     reads=(Sxa, S_eyeb), writes=(SpT[pb],))
                k.op("vector",
                     lambda e, g=g, pb=pb: e.tensor_copy(
                         btm[:, g * 128:(g + 1) * 128], psT[pb][:, :]),
                     reads=(SpT[pb],), writes=(Sbtm,))

            k.op("vector", lambda e: e.tensor_copy(prevb[:, :], big[:, :]),
                 reads=(S_big,), writes=(S_prevb,))
            k.dma(d_prev[c], prevb[:, :], reads=(S_prevb,),
                  writes=(DS["d_prev"],), eng="gpsimd", ch=7)

            cdcol = expcs_all[:, c * 128 + 127:c * 128 + 128]
            k.dma(d_row[0:1, 0:128], cdcol, reads=(S_ecs,),
                  writes=(DS["d_row"],), eng="scalar", ch=4)
            k.dma(cdbc[:, :],
                  bass.AP(d_row.tensor, 0, [[16384, 1], [0, 128], [1, 128]]),
                  reads=(DS["d_row"],), writes=(S_cdbc,), eng="scalar", ch=4)
            rep = bass.AP(cdbc[:, :].tensor, 0, [[128, 128], [1, 128], [0, 64]])
            k.op("vector", lambda e, rep=rep: e.tensor_tensor(
                     big[:, :], big[:, :], rep, ALU.mult),
                 reads=(S_cdbc,), writes=(S_big,))

            for g in range(8):
                pb = g % 2
                def stmm(e, g=g, pb=pb):
                    last = None
                    for hh in range(16):
                        h = g * 16 + hh
                        last = e.matmul(
                            psS[pb][:, hh * 64:(hh + 1) * 64],
                            btm[:, g * 128:(g + 1) * 128],
                            xdtw_sb[:, h * 64:(h + 1) * 64],
                            start=True, stop=True)
                    return last
                k.op("tensor", stmm, reads=(Sbtm, Sxdtw), writes=(SpS[pb],))
                k.op("scalar",
                     lambda e, g=g, pb=pb: e.copy(
                         st_all[:, g * 1024:(g + 1) * 1024], psS[pb][:, :]),
                     reads=(SpS[pb],), writes=(S_stall,))
            k.op("vector", lambda e: e.tensor_tensor(big[:, :], big[:, :],
                                                     st_all[:, :], ALU.add),
                 reads=(S_stall,), writes=(S_big,))


def _s3(nc, k, DS, cc_in_t, cc_out_t, d_row, CN, P):
    c_masks, S_masks = CN["c_masks"]
    c_invm, S_invm = CN["c_invm"]
    cdbc, S_cdbc = CN["cdbc"]
    big, S_big = P["big"]
    csum, S_csum = P["csum"]
    with ExitStack() as es:
        E = es.enter_context
        ccsb = E(nc.sbuf_tensor("s3cc", [128, 8256], BF16))
        Scc = Slot("s3cc")
        shard = E(nc.sbuf_tensor("s3sh", [128, 8256], BF16))
        Ssh = Slot("s3sh")
        dsel = E(nc.sbuf_tensor("s3ds", [128, 4], F32))
        Sds = Slot("s3ds")
        ssel = E(nc.sbuf_tensor("s3ss", [128, INTER], BF16))
        Sss = Slot("s3ss")

        k.op("scalar", lambda e: e.activation(csum[:, 1:2], csum[:, 0:1], AF.Exp),
             reads=(S_csum,), writes=(S_csum,))
        k.op("vector", lambda e: e.tensor_copy(ccsb[:, 0:INTER], big[:, :]),
             reads=(S_big,), writes=(Scc,))
        k.op("vector", lambda e: e.memset(ccsb[:, INTER:8256], 0.0),
             writes=(Scc,))
        k.op("vector", lambda e: e.tensor_copy(ccsb[:, INTER:INTER + 1],
                                               csum[:, 1:2]),
             reads=(S_csum,), writes=(Scc,))
        k.dma(cc_in_t.ap(), ccsb[:, :], reads=(Scc,), writes=(DS["cc"],))
        k.op("gpsimd",
             lambda e: e.collective_compute(
                 "AllGather", ALU.bypass,
                 replica_groups=[[0, 1, 2, 3], [4, 5, 6, 7]],
                 ins=[cc_in_t.ap().opt()], outs=[cc_out_t.ap().opt()]),
             reads=(DS["cc"],), writes=(DS["cc"],))
        k.op("vector", lambda e: e.memset(big[:, :], 0.0),
             reads=(DS["cc"],), writes=(S_big,))
        for j in range(4):
            k.dma(shard[:, :], cc_out_t.ap()[j * 128:(j + 1) * 128, :],
                  reads=(DS["cc"],), writes=(Ssh,))
            k.op("vector",
                 lambda e, j=j: e.scalar_tensor_tensor(
                     dsel[:, j:j + 1], shard[:, INTER:INTER + 1],
                     c_masks[:, j:j + 1], c_invm[:, j:j + 1],
                     ALU.mult, ALU.add),
                 reads=(Ssh, S_masks, S_invm), writes=(Sds,))
            k.dma(d_row[0:1, 0:128], dsel[:, j:j + 1], reads=(Sds,),
                  writes=(DS["d_row"],))
            k.dma(cdbc[:, :],
                  bass.AP(d_row.tensor, 0, [[16384, 1], [0, 128], [1, 128]]),
                  reads=(DS["d_row"],), writes=(S_cdbc,))
            rep = bass.AP(cdbc[:, :].tensor, 0, [[128, 128], [1, 128], [0, 64]])
            k.op("vector", lambda e, rep=rep: e.tensor_tensor(
                     big[:, :], big[:, :], rep, ALU.mult),
                 reads=(S_cdbc,), writes=(S_big,))
            k.op("vector", lambda e, j=j: e.tensor_scalar(
                     ssel[:, :], shard[:, 0:INTER], c_masks[:, j:j + 1],
                     None, ALU.mult),
                 reads=(Ssh, S_masks), writes=(Sss,))
            k.op("vector", lambda e: e.tensor_tensor(big[:, :], big[:, :],
                                                     ssel[:, :], ALU.add),
                 reads=(Sss,), writes=(S_big,))


def _s4(nc, k, DS, d_proj, d_xbca, d_xdt, d_prev, d_ynf, d_row, d_rowb, CN, P):
    c_eyef, S_eyef = CN["c_eyef"]
    c_eyeb, S_eyeb = CN["c_eyeb"]
    c_dpair, S_dpair = CN["c_dpair"]
    c_normw, S_normw = CN["c_normw"]
    c_onescol, S_onescol = CN["c_onescol"]
    cdbc, S_cdbc = CN["cdbc"]
    da_all, S_daall = P["da_all"]
    expcs_all, S_ecs = P["expcs_all"]
    cumdec, S_cumdec = P["cumdec"]
    big, S_big = P["big"]
    with ExitStack() as es:
        E = es.enter_context
        sgate = E(nc.sbuf_tensor("s4g", [128, INTER], BF16))
        Sg = Slot("s4g")
        sxa = E(nc.sbuf_tensor("s4xa", [128, 80 * 128], BF16))
        Sxa = Slot("s4xa")
        sxdt = E(nc.sbuf_tensor("s4xd", [128, INTER], BF16))
        Sxd = Slot("s4xd")
        sprev = E(nc.sbuf_tensor("s4pv", [128, INTER], BF16))
        Spv = Slot("s4pv")
        prevf = E(nc.sbuf_tensor("s4pf", [128, INTER], BF16))
        Spf = Slot("s4pf")
        abc = E(nc.sbuf_tensor("s4abc", [128, 32 * 128], F32))
        Sabc = Slot("s4abc")
        ebc = E(nc.sbuf_tensor("s4ebc", [128, 32 * 128], BF16))
        Sebc = Slot("s4ebc")
        ebf = E(nc.sbuf_tensor("s4ebf", [128, 128], BF16))
        Sebf = Slot("s4ebf")
        scg = E(nc.sbuf_tensor("s4scg", [128, 8 * 128], BF16))
        Sscg = Slot("s4scg")
        ainv = E(nc.sbuf_tensor("s4ai", [128, 128], F32))
        Sai = Slot("s4ai")
        lm = [E(nc.sbuf_tensor(f"s4lm{i}", [128, 128], F32)) for i in range(2)]
        Slm = [Slot(f"s4lm{i}") for i in range(2)]
        scl = [E(nc.sbuf_tensor(f"s4sl{i}", [128, 128], BF16)) for i in range(4)]
        Sscl = [Slot(f"s4sl{i}") for i in range(4)]
        cfe = [E(nc.sbuf_tensor(f"s4cf{i}", [128, 128], BF16)) for i in range(4)]
        Scfe = [Slot(f"s4cf{i}") for i in range(4)]
        ych = E(nc.sbuf_tensor("s4y", [128, INTER], BF16))
        Sy = Slot("s4y")
        ynf = E(nc.sbuf_tensor("s4yn", [128, INTER], BF16))
        Syn = Slot("s4yn")
        gn = E(nc.sbuf_tensor("s4gn", [1, 1024], F32))
        Sgn = Slot("s4gn")
        gnbc = E(nc.sbuf_tensor("s4gnb", [128, 1024], F32))
        Sgnb = Slot("s4gnb")
        if DEBUG:
            d_ych = nc.dram_tensor("dbg_ych", [NCH, 128, INTER], BF16,
                                   kind="ExternalOutput").ap()
            S_dych = Slot("d_ych")
            d_ygt = nc.dram_tensor("dbg_ygt", [NCH, 128, INTER], BF16,
                                   kind="ExternalOutput").ap()
            S_dygt = Slot("d_ygt")
            d_gnr = nc.dram_tensor("dbg_gnr", [NCH, 1, 1024], F32,
                                   kind="ExternalOutput").ap()
            S_dgnr = Slot("d_gnr")
        psSc = [E(nc.psum_tensor(f"s4pc{i}", [128, 128], F32)) for i in range(2)]
        SpSc = [Slot(f"s4pc{i}") for i in range(2)]
        psY = [E(nc.psum_tensor(f"s4py{i}", [128, 128], F32)) for i in range(2)]
        SpY = [Slot(f"s4py{i}") for i in range(2)]
        psF = E(nc.psum_tensor("s4pfp", [128, 128], F32))
        SpF = Slot("s4pfp")
        psGN = E(nc.psum_tensor("s4pgn", [1, 1024], F32))
        SpGN = Slot("s4pgn")

        for c in range(NCH):
            w0 = PAD + LOOK + c * 128
            for s4 in range(4):
                k.dma(sgate[:, s4 * 16 * 128:(s4 + 1) * 16 * 128],
                      d_proj[s4 * 16:(s4 + 1) * 16]
                      .transpose([1, 0, 2])[:, :, w0:w0 + 128],
                      reads=(DS["d_proj"],), writes=(Sg,), ch=s4)
                k.dma(sxa[:, s4 * 20 * 128:(s4 + 1) * 20 * 128],
                      d_xbca[s4 * 20:(s4 + 1) * 20]
                      .transpose([1, 0, 2])[:, :, c * 128:(c + 1) * 128],
                      reads=(DS["d_xbca"],), writes=(Sxa,), ch=s4)
            k.dma(sxdt[:, :], d_xdt[c], reads=(DS["d_xdt"],), writes=(Sxd,),
                  ch=c % 4)
            k.dma(sprev[:, :], d_prev[c], reads=(DS["d_prev"],), writes=(Spv,),
                  ch=(c + 1) % 4)

            k.dma(d_row[0:1, 0:128], cumdec[:, c:c + 1], reads=(S_cumdec,),
                  writes=(DS["d_row"],), eng="scalar", ch=4)
            k.dma(cdbc[:, :],
                  bass.AP(d_row.tensor, 0, [[16384, 1], [0, 128], [1, 128]]),
                  reads=(DS["d_row"],), writes=(S_cdbc,), eng="scalar", ch=4)
            rep = bass.AP(cdbc[:, :].tensor, 0, [[128, 128], [1, 128], [0, 64]])
            k.op("vector", lambda e, rep=rep: e.tensor_tensor(
                     prevf[:, :], big[:, :], rep, ALU.mult),
                 reads=(S_big, S_cdbc), writes=(Spf,))
            k.op("vector", lambda e: e.tensor_tensor(prevf[:, :], prevf[:, :],
                                                     sprev[:, :], ALU.add),
                 reads=(Spv, Spf), writes=(Spf,))

            for g in range(8):
                pb = g % 2
                k.op("tensor",
                     lambda e, g=g, pb=pb: e.matmul(
                         psSc[pb][:, :],
                         sxa[:, (64 + g) * 128:(65 + g) * 128],
                         sxa[:, (72 + g) * 128:(73 + g) * 128],
                         start=True, stop=True),
                     reads=(Sxa,), writes=(SpSc[pb],))
                k.op("scalar",
                     lambda e, g=g, pb=pb: e.copy(scg[:, g * 128:(g + 1) * 128],
                                                  psSc[pb][:, :]),
                     reads=(SpSc[pb],), writes=(Sscg,))

            k.dma(d_row, da_all[:, c * 128:(c + 1) * 128], reads=(S_daall,),
                  writes=(DS["d_row"],), eng="scalar", ch=5)
            k.op("vector",
                 lambda e, c=c: e.tensor_copy(ebf[:, :],
                                              expcs_all[:, c * 128:(c + 1) * 128]),
                 reads=(S_ecs,), writes=(Sebf,))
            k.dma(d_rowb, ebf[:, :], reads=(Sebf,), writes=(DS["d_rowb"],),
                  eng="scalar", ch=5)
            k.op("tensor",
                 lambda e, c=c: e.transpose(psF[:, :],
                                            da_all[:, c * 128:(c + 1) * 128],
                                            c_eyef[:, :]),
                 reads=(S_daall, S_eyef), writes=(SpF,))
            k.op("vector", lambda e: e.tensor_copy(ainv[:, :], psF[:, :]),
                 reads=(SpF,), writes=(Sai,))
            k.op("scalar", lambda e: e.activation(ainv[:, :], ainv[:, :],
                                                  AF.Exp, scale=-1.0),
                 reads=(Sai,), writes=(Sai,))

            for q in range(4):
                k.dma(abc[:, :],
                      bass.AP(d_row.tensor, q * 4096,
                              [[16384, 1], [0, 128], [1, 4096]]),
                      reads=(DS["d_row"],), writes=(Sabc,), eng="scalar", ch=5)
                k.op("scalar", lambda e: e.activation(abc[:, :], abc[:, :],
                                                      AF.Exp),
                     reads=(Sabc,), writes=(Sabc,))
                k.dma(ebc[:, :],
                      bass.AP(d_rowb.tensor, q * 4096,
                              [[16384, 1], [0, 128], [1, 4096]]),
                      reads=(DS["d_rowb"],), writes=(Sebc,), eng="scalar", ch=5)
                for hh in range(32):
                    h = q * 32 + hh
                    g, tp, half = h // 16, h // 2, h % 2
                    sl = h % 4
                    lmp = h % 2
                    k.op("vector",
                         lambda e, hh=hh, lmp=lmp: e.tensor_tensor_scan(
                             lm[lmp][:, :], c_eyef[:, :],
                             abc[:, hh * 128:(hh + 1) * 128], 0.0,
                             ALU.add, ALU.mult),
                         reads=(S_eyef, Sabc), writes=(Slm[lmp],))
                    k.op("vector",
                         lambda e, h=h, g=g, sl=sl, lmp=lmp:
                         e.scalar_tensor_tensor(
                             scl[sl][:, :], lm[lmp][:, :], ainv[:, h:h + 1],
                             scg[:, g * 128:(g + 1) * 128], ALU.mult, ALU.mult),
                         reads=(Slm[lmp], Sai, Sscg), writes=(Sscl[sl],))
                    k.op("vector",
                         lambda e, hh=hh, g=g, sl=sl: e.tensor_tensor(
                             cfe[sl][:, :], sxa[:, (72 + g) * 128:(73 + g) * 128],
                             ebc[:, hh * 128:(hh + 1) * 128], ALU.mult),
                         reads=(Sxa, Sebc), writes=(Scfe[sl],))
                    yb = psY[tp % 2]
                    SpYb = SpY[tp % 2]
                    def ymm(e, h=h, half=half, sl=sl, yb=yb):
                        e.matmul(yb[half * 64:(half + 1) * 64, :],
                                 sxdt[:, h * 64:(h + 1) * 64],
                                 scl[sl][:, :],
                                 start=True, stop=False)
                        return e.matmul(yb[half * 64:(half + 1) * 64, :],
                                        prevf[:, h * 64:(h + 1) * 64],
                                        cfe[sl][:, :],
                                        start=False, stop=True)
                    k.op("tensor", ymm, reads=(Sxd, Sscl[sl], Spf, Scfe[sl]),
                         writes=(SpYb,))
                    if half == 1:
                        k.op("vector",
                             lambda e, tp=tp: e.tensor_scalar(
                                 ynf[:, tp * 128:(tp + 1) * 128],
                                 sxa[:, tp * 128:(tp + 1) * 128],
                                 c_dpair[:, tp:tp + 1], None, ALU.mult),
                             reads=(Sxa, S_dpair), writes=(Syn,))
                        k.op("vector",
                             lambda e, tp=tp, yb=yb: e.tensor_tensor(
                                 ych[:, tp * 128:(tp + 1) * 128], yb[:, :],
                                 ynf[:, tp * 128:(tp + 1) * 128], ALU.add),
                             reads=(SpYb, Syn), writes=(Sy,))

            if DEBUG:
                k.dma(d_ych[c], ych[:, :], reads=(Sy,), writes=(S_dych,))
            k.op("scalar", lambda e: e.activation(sgate[:, :], sgate[:, :],
                                                  AF.Silu),
                 reads=(Sg,), writes=(Sg,))
            k.op("vector", lambda e: e.tensor_tensor(ych[:, :], ych[:, :],
                                                     sgate[:, :], ALU.mult),
                 reads=(Sg, Sy), writes=(Sy,))
            if DEBUG:
                k.dma(d_ygt[c], ych[:, :], reads=(Sy,), writes=(S_dygt,))
            k.op("scalar", lambda e: e.activation(sgate[:, :], ych[:, :],
                                                  AF.Square),
                 reads=(Sy,), writes=(Sg,))
            def gnmm(e):
                last = None
                for g in range(8):
                    for t8 in range(8):
                        last = e.matmul(psGN[0:1, g * 128:(g + 1) * 128],
                                        c_onescol[:, :],
                                        sgate[:, (8 * g + t8) * 128:
                                              (8 * g + t8 + 1) * 128],
                                        start=(t8 == 0),
                                        stop=(t8 == 7))
                return last
            k.op("tensor", gnmm, reads=(Sg, S_onescol), writes=(SpGN,))
            k.op("vector", lambda e: e.tensor_scalar(
                     gn[0:1, 0:1024], psGN[0:1, :], 1.0 / 1024.0, EPS,
                     ALU.mult, ALU.add),
                 reads=(SpGN,), writes=(Sgn,))
            k.op("scalar", lambda e: e.activation(gn[0:1, 0:1024],
                                                  gn[0:1, 0:1024], AF.Ln),
                 reads=(Sgn,), writes=(Sgn,))
            k.op("scalar", lambda e: e.activation(gn[0:1, 0:1024],
                                                  gn[0:1, 0:1024], AF.Exp,
                                                  scale=-0.5),
                 reads=(Sgn,), writes=(Sgn,))
            if DEBUG:
                k.dma(d_gnr[c], gn[0:1, 0:1024], reads=(Sgn,),
                      writes=(S_dgnr,))
            k.dma(d_row[0:1, 0:1024], gn[0:1, 0:1024], reads=(Sgn,),
                  writes=(DS["d_row"],), eng="scalar", ch=4)
            k.dma(gnbc[:, :],
                  bass.AP(d_row.tensor, 0, [[16384, 1], [0, 128], [1, 1024]]),
                  reads=(DS["d_row"],), writes=(Sgnb,), eng="scalar", ch=4)
            def vynf(e):
                last = None
                for tp in range(64):
                    g = tp // 8
                    last = e.scalar_tensor_tensor(
                        ynf[:, tp * 128:(tp + 1) * 128],
                        ych[:, tp * 128:(tp + 1) * 128],
                        c_normw[:, tp:tp + 1],
                        gnbc[:, g * 128:(g + 1) * 128],
                        ALU.mult, ALU.mult)
                return last
            k.op("vector", vynf, reads=(Sy, Sgnb, S_normw), writes=(Syn,))
            for s4 in range(4):
                k.dma(d_ynf[s4 * 16:(s4 + 1) * 16]
                      .transpose([1, 0, 2])[:, :, c * 128:(c + 1) * 128],
                      ynf[:, s4 * 16 * 128:(s4 + 1) * 16 * 128],
                      reads=(Syn,), writes=(DS["d_ynf"],),
                      eng="gpsimd", ch=6 + s4 % 2)


def _s5(nc, k, DS, woutT, d_ynf, d_attn):
    with ExitStack() as es:
        E = es.enter_context
        synf = E(nc.sbuf_tensor("s5yn", [128, 64 * 1024], BF16))
        Syn = Slot("s5yn")
        swf = E(nc.sbuf_tensor("s5wf", [128, INTER], F32))
        Swf = Slot("s5wf")
        swb = [E(nc.sbuf_tensor(f"s5wb{i}", [128, INTER], BF16)) for i in range(2)]
        Swb = [Slot(f"s5wb{i}") for i in range(2)]
        sat = [E(nc.sbuf_tensor(f"s5at{i}", [128, 1024], BF16)) for i in range(2)]
        Sat = [Slot(f"s5at{i}") for i in range(2)]
        pss = [E(nc.psum_tensor(f"s5p{i}", [128, 512], F32)) for i in range(4)]
        Sps = [Slot(f"s5p{i}") for i in range(4)]

        for hf in range(2):
            for s4 in range(4):
                k.dma(synf[:, s4 * 16 * 1024:(s4 + 1) * 16 * 1024],
                      d_ynf[s4 * 16:(s4 + 1) * 16]
                      .transpose([1, 0, 2])[:, :, hf * 1024:(hf + 1) * 1024],
                      reads=(DS["d_ynf"],), writes=(Syn,), ch=4 + s4 % 2)
            for og in range(32):
                p = og % 2
                k.dma(swf[:, :], woutT[og], writes=(Swf,), ch=og % 4)
                k.op("vector",
                     lambda e, p=p: e.tensor_copy(swb[p][:, :], swf[:, :]),
                     reads=(Swf,), writes=(Swb[p],))
                def mm(e, p=p):
                    last = None
                    for kt in range(64):
                        for tc in range(2):
                            last = e.matmul(
                                pss[2 * p + tc][:, :],
                                swb[p][:, kt * 128:(kt + 1) * 128],
                                synf[:, kt * 1024 + tc * 512:
                                     kt * 1024 + (tc + 1) * 512],
                                start=(kt == 0), stop=(kt == 63))
                    return last
                k.op("tensor", mm, reads=(Swb[p], Syn),
                     writes=(Sps[2 * p], Sps[2 * p + 1]))
                def evac(e, p=p):
                    last = None
                    for tc in range(2):
                        last = e.copy(sat[p][:, tc * 512:(tc + 1) * 512],
                                      pss[2 * p + tc][:, :])
                    return last
                k.op("scalar", evac, reads=(Sps[2 * p], Sps[2 * p + 1]),
                     writes=(Sat[p],))
                k.dma(d_attn[og][:, hf * 1024:(hf + 1) * 1024], sat[p][:, :],
                      reads=(Sat[p],), writes=(DS["d_attn"],),
                      eng="scalar", ch=6 + og % 2)


def _s6(nc, k, DS, x_ext, d_attn, d_x2, ln3_fm, S_ln3, CN):
    c_eyeb, S_eyeb = CN["c_eyeb"]
    with ExitStack() as es:
        E = es.enter_context
        sfm = [E(nc.sbuf_tensor(f"s6f{i}", [128, 32 * 128], BF16)) for i in range(2)]
        Sf = [Slot(f"s6f{i}") for i in range(2)]
        atm = E(nc.sbuf_tensor("s6a", [128, DM], BF16))
        Sa = Slot("s6a")
        nrm = E(nc.sbuf_tensor("s6n", [128, DM], BF16))
        Sn = Slot("s6n")
        xr = E(nc.sbuf_tensor("s6x", [128, DM], F32))
        Sx = Slot("s6x")
        x2t = [E(nc.sbuf_tensor(f"s6x2{i}", [128, DM], BF16)) for i in range(2)]
        Sx2 = [Slot(f"s6x2{i}") for i in range(2)]
        ln3t = E(nc.sbuf_tensor("s6l3", [128, DM], BF16))
        Sl3 = Slot("s6l3")
        ssq, Ssq = None, None
        stat = E(nc.sbuf_tensor("s6st", [128, 8], F32))
        Sst = Slot("s6st")
        ps = [E(nc.psum_tensor(f"s6p{i}", [128, 128], BF16)) for i in range(4)]
        Sp = [Slot(f"s6p{i}") for i in range(4)]

        for tt in range(16):
            p = tt % 2
            for s2_ in range(2):
                k.dma(sfm[p][:, s2_ * 16 * 128:(s2_ + 1) * 16 * 128],
                      d_attn[s2_ * 16:(s2_ + 1) * 16]
                      .transpose([1, 0, 2])[:, :, tt * 128:(tt + 1) * 128],
                      reads=(DS["d_attn"],), writes=(Sf[p],), ch=s2_)
            k.dma(xr[:, :],
                  x_ext[PAD + LOOK + tt * 128:PAD + LOOK + (tt + 1) * 128, :],
                  writes=(Sx,), ch=2)
            for dt_ in range(32):
                b = dt_ % 4
                k.op("tensor",
                     lambda e, dt_=dt_, p=p, b=b: e.transpose(
                         ps[b][:, :], sfm[p][:, dt_ * 128:(dt_ + 1) * 128],
                         c_eyeb[:, :]),
                     reads=(Sf[p], S_eyeb), writes=(Sp[b],))
                k.op("vector",
                     lambda e, dt_=dt_, b=b: e.tensor_copy(
                         atm[:, dt_ * 128:(dt_ + 1) * 128], ps[b][:, :]),
                     reads=(Sp[b],), writes=(Sa,))
            _ln_stats(nc, k, atm, Sa, stat, Sst, nrm, Sn)
            k.op("scalar", lambda e: e.activation(nrm[:, :], atm[:, :], AF.Identity,
                                                  bias=stat[:, 7:8],
                                                  scale=stat[:, 6:7]),
                 reads=(Sa, Sst), writes=(Sn,))
            k.op("vector", lambda e, p=p: e.tensor_tensor(x2t[p][:, :], xr[:, :],
                                                          nrm[:, :], ALU.add),
                 reads=(Sx, Sn), writes=(Sx2[p],))
            k.dma(d_x2[tt], x2t[p][:, :], reads=(Sx2[p],),
                  writes=(DS["d_x2"],), eng="gpsimd", ch=6)
            _ln_stats(nc, k, x2t[p], Sx2[p], stat, Sst, nrm, Sn)
            k.op("scalar", lambda e, p=p: e.activation(ln3t[:, :], x2t[p][:, :],
                                                       AF.Identity, bias=stat[:, 7:8],
                                                       scale=stat[:, 6:7]),
                 reads=(Sx2[p], Sst), writes=(Sl3,))
            for kt in range(32):
                b = kt % 4
                k.op("tensor",
                     lambda e, kt=kt, b=b: e.transpose(
                         ps[b][:, :], ln3t[:, kt * 128:(kt + 1) * 128],
                         c_eyeb[:, :]),
                     reads=(Sl3, S_eyeb), writes=(Sp[b],))
                k.op("vector",
                     lambda e, tt=tt, kt=kt, b=b: e.tensor_copy(
                         ln3_fm[:, kt * NT + tt * 128: kt * NT + (tt + 1) * 128],
                         ps[b][:, :]),
                     reads=(Sp[b],), writes=(S_ln3,))


def _s7(nc, k, DS, gwT, bwT, lAT, lBT, ln3_fm, S_ln3, d_ffn, CN):
    c_eyeb, S_eyeb = CN["c_eyeb"]
    c_bb, S_bb = CN["c_bb"]
    with ExitStack() as es:
        E = es.enter_context
        sgwf = E(nc.sbuf_tensor("s7gwf", [128, 32 * 8], F32))
        sgw = E(nc.sbuf_tensor("s7gw", [128, 32 * 8], BF16))
        Sgw = Slot("s7gw")
        slat = E(nc.sbuf_tensor("s7la", [128, DM], BF16))
        Sla = Slot("s7la")
        slbt = E(nc.sbuf_tensor("s7lb", [128, DM], BF16))
        Slb = Slot("s7lb")
        lgt = E(nc.sbuf_tensor("s7lg", [128, 48], F32))
        Slg = Slot("s7lg")
        wfb = E(nc.sbuf_tensor("s7wfb", [128, 8], BF16))
        Swfb = Slot("s7wfb")
        wfTa = E(nc.sbuf_tensor("s7wt", [8, NT], BF16))
        Swt = Slot("s7wt")
        wrep = E(nc.sbuf_tensor("s7wr", [128, NT], BF16))
        Swr = Slot("s7wr")
        lowb = E(nc.sbuf_tensor("s7lo", [128, NT], BF16))
        Slo = Slot("s7lo")
        loww = E(nc.sbuf_tensor("s7lw", [128, NT], BF16))
        Slw = Slot("s7lw")
        sbwf = E(nc.sbuf_tensor("s7bwf", [128, DM], F32))
        Sbwf = Slot("s7bwf")
        sbwb = [E(nc.sbuf_tensor(f"s7bw{i}", [128, DM], BF16)) for i in range(2)]
        Sbw = [Slot(f"s7bw{i}") for i in range(2)]
        sff = [E(nc.sbuf_tensor(f"s7ff{i}", [128, NT], BF16)) for i in range(2)]
        Sff = [Slot(f"s7ff{i}") for i in range(2)]
        psL = E(nc.psum_tensor("s7pl", [128, 8], F32))
        SpL = Slot("s7pl")
        psW = E(nc.psum_tensor("s7pw", [8, 128], BF16))
        SpW = Slot("s7pw")
        pss = [E(nc.psum_tensor(f"s7p{i}", [128, 512], F32)) for i in range(4)]
        Sps = [Slot(f"s7p{i}") for i in range(4)]

        Sc = Slot("s7consts")
        k.dma(sgwf[:, :], gwT.rearrange("(t p) o -> p t o", p=128), writes=(Sc,))
        k.op("vector", lambda e: e.tensor_copy(sgw[:, :], sgwf[:, :]),
             reads=(Sc,), writes=(Sgw,))
        k.dma(sbwf[:, :], lAT, writes=(Sbwf,))
        k.op("vector", lambda e: e.tensor_copy(slat[:, :], sbwf[:, :]),
             reads=(Sbwf,), writes=(Sla,))
        k.dma(sbwf[:, :], lBT, writes=(Sbwf,))
        k.op("vector", lambda e: e.tensor_copy(slbt[:, :], sbwf[:, :]),
             reads=(Sbwf,), writes=(Slb,))

        for tt in range(16):
            def gmm(e, tt=tt):
                last = None
                for kt in range(32):
                    last = e.matmul(psL[:, :],
                                    ln3_fm[:, kt * NT + tt * 128:
                                           kt * NT + (tt + 1) * 128],
                                    sgw[:, kt * 8:(kt + 1) * 8],
                                    start=(kt == 0), stop=(kt == 31))
                return last
            k.op("tensor", gmm, reads=(S_ln3, Sgw), writes=(SpL,))
            for f in [
                lambda e: e.tensor_copy(lgt[:, 0:8], psL[:, :]),
                lambda e: e.reduce_max(lgt[:, 8:9], lgt[:, 0:8], axis=AX),
                lambda e: e.tensor_tensor(lgt[:, 16:24], lgt[:, 0:8],
                                          lgt[:, 8:9].broadcast_to([128, 8]),
                                          ALU.is_equal),
                lambda e: e.scalar_tensor_tensor(lgt[:, 24:32], lgt[:, 16:24],
                                                 -1e30, lgt[:, 0:8],
                                                 ALU.mult, ALU.add),
                lambda e: e.reduce_max(lgt[:, 9:10], lgt[:, 24:32], axis=AX),
                lambda e: e.tensor_tensor(lgt[:, 12:13], lgt[:, 8:9],
                                          lgt[:, 9:10], ALU.subtract),
            ]:
                k.op("vector", f, reads=(SpL, Slg), writes=(Slg,))
            k.op("scalar", lambda e: e.activation(lgt[:, 10:11], lgt[:, 12:13],
                                                  AF.Sigmoid),
                 reads=(Slg,), writes=(Slg,))
            for f in [
                lambda e: e.tensor_scalar(lgt[:, 11:12], lgt[:, 10:11], -1.0,
                                          1.0, ALU.mult, ALU.add),
                lambda e: e.tensor_tensor(lgt[:, 32:40], lgt[:, 24:32],
                                          lgt[:, 9:10].broadcast_to([128, 8]),
                                          ALU.is_equal),
                lambda e: e.tensor_scalar(lgt[:, 32:40], lgt[:, 32:40],
                                          lgt[:, 11:12], None, ALU.mult),
                lambda e: e.scalar_tensor_tensor(lgt[:, 40:48], lgt[:, 16:24],
                                                 lgt[:, 10:11], lgt[:, 32:40],
                                                 ALU.mult, ALU.add),
            ]:
                k.op("vector", f, reads=(Slg,), writes=(Slg,))
            k.op("vector", lambda e: e.tensor_scalar(wfb[:, :], lgt[:, 40:48],
                                                     2.0, None, ALU.mult),
                 reads=(Slg,), writes=(Swfb,))
            k.op("tensor", lambda e: e.transpose(psW[:, :], wfb[:, :],
                                                 c_eyeb[:, :]),
                 reads=(Swfb, S_eyeb), writes=(SpW,))
            k.op("vector",
                 lambda e, tt=tt: e.tensor_copy(wfTa[0:8, tt * 128:(tt + 1) * 128],
                                                psW[:, :]),
                 reads=(SpW,), writes=(Swt,))
        k.dma(wrep[:, :],
              bass.AP(wfTa[:, :].tensor, 0, [[NT, 8], [0, 16], [1, NT]]),
              reads=(Swt,), writes=(Swr,), eng="gpsimd", ch=5)

        def lowmm(e):
            last = None
            for kt in range(32):
                for tc in range(4):
                    last = e.matmul(pss[tc][:, :],
                                    slat[:, kt * 128:(kt + 1) * 128],
                                    ln3_fm[:, kt * NT + tc * 512:
                                           kt * NT + (tc + 1) * 512],
                                    start=(kt == 0), stop=(kt == 31))
            return last
        k.op("tensor", lowmm, reads=(Sla, S_ln3), writes=tuple(Sps))
        def lowev(e):
            last = None
            for tc in range(4):
                last = e.copy(lowb[:, tc * 512:(tc + 1) * 512], pss[tc][:, :])
            return last
        k.op("scalar", lowev, reads=tuple(Sps), writes=(Slo,))
        k.op("vector", lambda e: e.tensor_tensor(loww[:, :], lowb[:, :],
                                                 wrep[:, :], ALU.mult),
             reads=(Slo, Swr), writes=(Slw,))

        for og in range(32):
            p = og % 2
            k.dma(sbwf[:, :], bwT[og], writes=(Sbwf,), ch=og % 4)
            k.op("vector", lambda e, p=p: e.tensor_copy(sbwb[p][:, :],
                                                        sbwf[:, :]),
                 reads=(Sbwf,), writes=(Sbw[p],))
            def bmm(e, p=p, og=og):
                last = None
                for kt in range(32):
                    for tc in range(4):
                        last = e.matmul(pss[tc][:, :],
                                        sbwb[p][:, kt * 128:(kt + 1) * 128],
                                        ln3_fm[:, kt * NT + tc * 512:
                                               kt * NT + (tc + 1) * 512],
                                        start=(kt == 0), stop=False)
                for tc in range(4):
                    last = e.matmul(pss[tc][:, :],
                                    slbt[:, og * 128:(og + 1) * 128],
                                    loww[:, tc * 512:(tc + 1) * 512],
                                    start=False, stop=True)
                return last
            k.op("tensor", bmm, reads=(Sbw[p], S_ln3, Slb, Slw), writes=tuple(Sps))
            def fevac(e, p=p, og=og):
                last = None
                for tc in range(4):
                    last = e.activation(sff[p][:, tc * 512:(tc + 1) * 512],
                                        pss[tc][:, :], AF.Identity,
                                        bias=c_bb[:, og:og + 1])
                return last
            k.op("scalar", fevac, reads=tuple(Sps) + (S_bb,), writes=(Sff[p],))
            k.dma(d_ffn[og], sff[p][:, :], reads=(Sff[p],),
                  writes=(DS["d_ffn"],), eng="scalar", ch=6 + og % 2)


def _s8(nc, k, DS, d_x2, d_ffn, o_y, CN):
    c_eyeb, S_eyeb = CN["c_eyeb"]
    with ExitStack() as es:
        E = es.enter_context
        sfm = [E(nc.sbuf_tensor(f"s8f{i}", [128, 32 * 128], BF16)) for i in range(2)]
        Sf = [Slot(f"s8f{i}") for i in range(2)]
        ftm = E(nc.sbuf_tensor("s8a", [128, DM], BF16))
        Sa = Slot("s8a")
        nrm = E(nc.sbuf_tensor("s8n", [128, DM], BF16))
        Sn = Slot("s8n")
        x2r = [E(nc.sbuf_tensor(f"s8x{i}", [128, DM], BF16)) for i in range(2)]
        Sx = [Slot(f"s8x{i}") for i in range(2)]
        sout = [E(nc.sbuf_tensor(f"s8o{i}", [128, DM], F32)) for i in range(2)]
        So = [Slot(f"s8o{i}") for i in range(2)]
        stat = E(nc.sbuf_tensor("s8st", [128, 8], F32))
        Sst = Slot("s8st")
        ps = [E(nc.psum_tensor(f"s8p{i}", [128, 128], BF16)) for i in range(4)]
        Sp = [Slot(f"s8p{i}") for i in range(4)]

        for tt in range(16):
            p = tt % 2
            k.dma(sfm[p][:, 0:16 * 128],
                  d_ffn[0:16].transpose([1, 0, 2])[:, :, tt * 128:(tt + 1) * 128],
                  reads=(DS["d_ffn"],), writes=(Sf[p],), ch=0)
            k.dma(sfm[p][:, 16 * 128:32 * 128],
                  d_ffn[16:32].transpose([1, 0, 2])[:, :, tt * 128:(tt + 1) * 128],
                  reads=(DS["d_ffn"],), writes=(Sf[p],), ch=1)
            k.dma(x2r[p][:, :], d_x2[tt], reads=(DS["d_x2"],), writes=(Sx[p],), ch=2)
            for dt_ in range(32):
                b = dt_ % 4
                k.op("tensor",
                     lambda e, dt_=dt_, p=p, b=b: e.transpose(
                         ps[b][:, :], sfm[p][:, dt_ * 128:(dt_ + 1) * 128],
                         c_eyeb[:, :]),
                     reads=(Sf[p], S_eyeb), writes=(Sp[b],))
                k.op("vector",
                     lambda e, dt_=dt_, b=b: e.tensor_copy(
                         ftm[:, dt_ * 128:(dt_ + 1) * 128], ps[b][:, :]),
                     reads=(Sp[b],), writes=(Sa,))
            _ln_stats(nc, k, ftm, Sa, stat, Sst, nrm, Sn)
            k.op("scalar", lambda e: e.activation(nrm[:, :], ftm[:, :], AF.Identity,
                                                  bias=stat[:, 7:8],
                                                  scale=stat[:, 6:7]),
                 reads=(Sa, Sst), writes=(Sn,))
            k.op("vector", lambda e, p=p: e.tensor_tensor(sout[p][:, :],
                                                          x2r[p][:, :],
                                                          nrm[:, :], ALU.add),
                 reads=(Sx[p], Sn), writes=(So[p],))
            k.dma(o_y[tt * 128:(tt + 1) * 128, :], sout[p][:, :],
                  reads=(So[p],), writes=(Slot("oy"),), eng="gpsimd",
                  ch=6 + tt % 2)


def kernel(**inputs):
    x = np.ascontiguousarray(inputs["L0_attn_hidden_states"], dtype=np.float32)
    W = np.ascontiguousarray(inputs["L0_attn_in_proj_w"], dtype=np.float32)
    cw = np.ascontiguousarray(inputs["L0_attn_conv_w"], dtype=np.float32)
    cb = np.ascontiguousarray(inputs["L0_attn_conv_b"], dtype=np.float32)
    dtb = np.ascontiguousarray(inputs["L0_attn_dt_bias"], dtype=np.float32)
    Dv = np.ascontiguousarray(inputs["L0_attn_D"], dtype=np.float32)
    nw = np.ascontiguousarray(inputs["L0_attn_norm_w"], dtype=np.float32)
    Wo = np.ascontiguousarray(inputs["L0_attn_out_proj_w"], dtype=np.float32)
    gw = np.ascontiguousarray(inputs["L0_ffn_gate_w"], dtype=np.float32)
    bw = np.ascontiguousarray(inputs["L0_ffn_base_w"], dtype=np.float32)
    bb = np.ascontiguousarray(inputs["L0_ffn_base_b"], dtype=np.float32)
    lA = np.ascontiguousarray(inputs["L0_ffn_lora_A"], dtype=np.float32)
    lB = np.ascontiguousarray(inputs["L0_ffn_lora_B"], dtype=np.float32)

    wT = np.ascontiguousarray(
        np.pad(W, ((0, 0), (0, 0))).reshape(145, 128, 32, 128)
        .transpose(0, 3, 2, 1).reshape(145, 128, DM))
    cw_cols = np.ascontiguousarray(
        cw[:, 0, :].reshape(80, 128, 4).transpose(1, 0, 2).reshape(128, 320))
    cb_row = np.ascontiguousarray(cb.reshape(1, CONV))
    dtb_col = np.ascontiguousarray(dtb.reshape(128, 1))
    d_pair = np.ascontiguousarray(
        np.repeat(Dv.reshape(64, 2, 1), 64, axis=2).transpose(1, 2, 0)
        .reshape(128, 64))
    normw_cols = np.ascontiguousarray(nw.reshape(64, 128).T)
    woutT = np.ascontiguousarray(
        Wo.reshape(32, 128, 64, 128).transpose(0, 3, 2, 1)
        .reshape(32, 128, INTER))
    gwT = np.ascontiguousarray(gw.T)
    bwT = np.ascontiguousarray(
        bw.reshape(32, 128, 32, 128).transpose(0, 3, 2, 1)
        .reshape(32, 128, DM))
    bb_cols = np.ascontiguousarray(bb.reshape(32, 128).T)
    lAT = np.ascontiguousarray(
        lA.reshape(128, 32, 128).transpose(2, 1, 0).reshape(128, DM))
    lBT = np.ascontiguousarray(lB.transpose(0, 2, 1).reshape(128, DM))

    in_maps = []
    for c in range(8):
        b, q = c // 4, c % 4
        start = q * 2048
        xs = x[b, start:start + 2048]
        look = x[b, start - 3:start] if q > 0 else np.zeros((3, DM), np.float32)
        x_ext = np.concatenate(
            [np.zeros((PAD, DM), np.float32), look, xs], axis=0)
        m = np.zeros((128, 4), np.float32)
        m[:, :q] = 1.0
        im = 1.0 - m
        in_maps.append({
            "x_ext": np.ascontiguousarray(x_ext),
            "wT": wT, "cw_cols": cw_cols, "cb_row": cb_row,
            "dtb_col": dtb_col, "d_pair": d_pair, "normw_cols": normw_cols,
            "woutT": woutT, "gwT": gwT, "bwT": bwT, "bb_cols": bb_cols,
            "lAT": lAT, "lBT": lBT,
            "masks": np.ascontiguousarray(m),
            "invmasks": np.ascontiguousarray(im),
        })

    nc = build_nc(debug=DEBUG)
    res = run_bass_kernel_spmd(nc, in_maps, core_ids=list(range(8)))
    out = np.empty((2, 8192, DM), dtype=np.float32)
    for c in range(8):
        b, q = c // 4, c % 4
        out[b, q * 2048:(q + 1) * 2048] = res.results[c]["o_y"]
    if DEBUG:
        kernel.last_results = res
    return out


# version 07 (final)
# speedup vs baseline: 1.2860x; 1.0091x over previous
import numpy as np
import ml_dtypes
from contextlib import ExitStack

import concourse.bass as bass
import concourse.mybir as mybir
from concourse.bass_utils import run_bass_kernel_spmd

F32 = mybir.dt.float32
BF16 = mybir.dt.bfloat16
AF = mybir.ActivationFunctionType
ALU = mybir.AluOpType
AX = mybir.AxisListType.X

DM = 4096
NH = 128
HD = 64
SS = 128
NG = 8
INTER = NH * HD
CONV = INTER + 2 * NG * SS
PROJ = INTER + CONV + NH
NT = 2048
PAD, LOOK = 125, 3
TEXT = PAD + LOOK + NT
NCH, CL = 16, 128
EPS = 1e-5

DEBUG = False
STAGES = None


def on(name):
    return STAGES is None or name in STAGES


class Slot:
    __slots__ = ("name", "w", "r")
    def __init__(self, name):
        self.name = name
        self.w = None
        self.r = []


SERIALIZE = False
SERIAL_STAGES = set()
NDCH = 16


class K:
    ENGS = ("sync", "tensor", "vector", "scalar", "gpsimd")

    def __init__(self, nc):
        self.nc = nc
        self.items = {e: [] for e in self.ENGS}
        self.sems = {e: (nc.alloc_semaphore("sem_" + e), [0]) for e in self.ENGS}
        self.dch = [(nc.alloc_semaphore(f"dch{i}"), [0]) for i in range(NDCH)]
        self.drr = 0
        self.seen = {e: {} for e in self.ENGS}
        self.lastop = None
        self.barrier = []

    def op(self, eng, emit, reads=(), writes=(), dma=False, ch=None):
        waits = []
        def need(tok, war=False):
            if tok is None:
                return
            sem, n, src = tok
            if war and src == eng:
                return
            if self.seen[eng].get(sem.name, 0) >= n:
                return
            self.seen[eng][sem.name] = n
            waits.append((sem, n))
        for tok in self.barrier:
            need(tok)
        if SERIALIZE or getattr(self, "cur_stage", None) in SERIAL_STAGES:
            need(self.lastop)
        for s in reads:
            need(s.w)
        for s in writes:
            need(s.w)
            for t in s.r:
                need(t, war=True)
        if dma:
            if ch is None:
                ch = self.drr
                self.drr = (self.drr + 1) % NDCH
            sem, cnt = self.dch[ch]
            if cnt[0] > 0:
                need((sem, cnt[0], "dma"))
            cnt[0] += 16
            tok = (sem, cnt[0], "dma")
            ik = 16
        else:
            sem, cnt = self.sems[eng]
            cnt[0] += 1
            tok = (sem, cnt[0], eng)
            ik = 1
        self.lastop = tok
        self.items[eng].append((waits, emit, sem, ik))
        for s in reads:
            s.r.append(tok)
        for s in writes:
            s.w = tok
            s.r = []

    def flush(self, name):
        items = self.items
        self.items = {e: [] for e in self.ENGS}
        self.barrier = (
            [(self.sems[e][0], self.sems[e][1][0], "barrier")
             for e in self.ENGS if self.sems[e][1][0] > 0] +
            [(sem, cnt[0], "barrier") for (sem, cnt) in self.dch
             if cnt[0] > 0])
        if not any(items[e] for e in self.ENGS):
            return
        with self.nc.Block(name) as block:
            def mk(eng_items):
                def body(e):
                    for waits, emit, sem_, kk in eng_items:
                        for (s, n) in waits:
                            e.wait_ge(s, n)
                        instr = emit(e)
                        instr.then_inc(sem_, kk)
                return body
            for eng in self.ENGS:
                if items[eng]:
                    getattr(block, eng)(mk(items[eng]))

    def finale(self):
        chans = [(sem, cnt[0]) for (sem, cnt) in self.dch if cnt[0] > 0]
        with self.nc.Block("finale") as block:
            @block.sync
            def _(e):
                for sem, n in chans:
                    e.wait_ge(sem, n)

    def dma(self, out_ap, in_ap, reads=(), writes=(), eng="sync", ch=None):
        self.op(eng, lambda e: e.dma_start(out=out_ap, in_=in_ap),
                reads=reads, writes=writes, dma=True, ch=ch)


def build_nc(debug=False):
    nc = bass.Bass("TRN2", target_bir_lowering=False)

    x_ext = nc.dram_tensor("x_ext", [TEXT, DM], F32, kind="ExternalInput").ap()
    wT = nc.dram_tensor("wT", [145, 128, DM], F32, kind="ExternalInput").ap()
    cw_cols = nc.dram_tensor("cw_cols", [128, 80 * 4], F32, kind="ExternalInput").ap()
    cb_row = nc.dram_tensor("cb_row", [1, CONV], F32, kind="ExternalInput").ap()
    dtb_col = nc.dram_tensor("dtb_col", [128, 1], F32, kind="ExternalInput").ap()
    d_pair = nc.dram_tensor("d_pair", [128, 64], F32, kind="ExternalInput").ap()
    normw_cols = nc.dram_tensor("normw_cols", [128, 64], F32, kind="ExternalInput").ap()
    woutT = nc.dram_tensor("woutT", [32, 128, INTER], F32,
                           kind="ExternalInput").ap()
    gwT = nc.dram_tensor("gwT", [DM, 8], F32, kind="ExternalInput").ap()
    bwT = nc.dram_tensor("bwT", [32, 128, DM], F32, kind="ExternalInput").ap()
    bb_cols = nc.dram_tensor("bb_cols", [128, 32], F32, kind="ExternalInput").ap()
    lAT = nc.dram_tensor("lAT", [128, DM], F32, kind="ExternalInput").ap()
    lBT = nc.dram_tensor("lBT", [128, DM], F32, kind="ExternalInput").ap()
    masks = nc.dram_tensor("masks", [128, 4], F32, kind="ExternalInput").ap()
    invmasks = nc.dram_tensor("invmasks", [128, 4], F32, kind="ExternalInput").ap()
    o_y = nc.dram_tensor("o_y", [NT, DM], F32, kind="ExternalOutput").ap()

    d_proj = nc.dram_tensor("d_proj", [144, 128, TEXT], BF16, kind="Internal").ap()
    d_dt = nc.dram_tensor("d_dt", [128, TEXT], F32, kind="Internal").ap()
    d_xbca = nc.dram_tensor("d_xbca", [80, 128, NT], BF16, kind="Internal").ap()
    d_xdt = nc.dram_tensor("d_xdt", [NCH, 128, INTER], BF16, kind="Internal").ap()
    d_prev = nc.dram_tensor("d_prev", [NCH, 128, INTER], BF16, kind="Internal").ap()
    d_ynf = nc.dram_tensor("d_ynf", [64, 128, NT], BF16, kind="Internal").ap()
    d_attn = nc.dram_tensor("d_attn", [32, 128, NT], BF16, kind="Internal").ap()
    d_x2 = nc.dram_tensor("d_x2", [16, 128, DM], BF16, kind="Internal").ap()
    d_ffn = nc.dram_tensor("d_ffn", [32, 128, NT], BF16, kind="Internal").ap()
    d_row = nc.dram_tensor("d_row", [1, 16384], F32, kind="Internal").ap()
    d_rowb = nc.dram_tensor("d_rowb", [1, 16384], BF16, kind="Internal").ap()
    cc_in_t = nc.dram_tensor("cc_in", [128, 8256], BF16, kind="Internal")
    cc_out_t = nc.dram_tensor("cc_out", [512, 8256], BF16, kind="Internal")

    eye_f = nc.inline_tensor(np.eye(128, dtype=np.float32), name="eye_f").ap()
    eye_b = nc.inline_tensor(np.eye(128, dtype=ml_dtypes.bfloat16), name="eye_b").ap()
    acol_c = nc.inline_tensor(-np.arange(1, 129, dtype=np.float32).reshape(128, 1),
                              name="acol").ap()
    ones_b = nc.inline_tensor(np.ones((1, 128), dtype=ml_dtypes.bfloat16),
                              name="ones_b").ap()
    onescol_b = nc.inline_tensor(np.ones((128, 1), dtype=ml_dtypes.bfloat16),
                                 name="onescol_b").ap()

    k = K(nc)
    DS = {n: Slot(n) for n in ["d_proj", "d_dt", "d_xbca", "d_xdt", "d_prev",
                               "d_ynf", "d_attn", "d_x2", "d_ffn", "d_row",
                               "d_rowb", "cc"]}

    with ExitStack() as es:
        E = es.enter_context

        def sb(name, shape, dt_):
            t = E(nc.sbuf_tensor(name, shape, dt_))
            return t, Slot(name)

        c_eyef, S_eyef = sb("c_eyef", [128, 128], F32)
        c_eyeb, S_eyeb = sb("c_eyeb", [128, 128], BF16)
        c_acol, S_acol = sb("c_acol", [128, 1], F32)
        c_dtb, S_dtb = sb("c_dtb", [128, 1], F32)
        c_cw, S_cw = sb("c_cw", [128, 320], F32)
        c_onesrow, S_onesrow = sb("c_onesrow", [1, 128], BF16)
        c_onescol, S_onescol = sb("c_onescol", [128, 1], BF16)
        c_dpair, S_dpair = sb("c_dpair", [128, 64], F32)
        c_normw, S_normw = sb("c_normw", [128, 64], F32)
        c_bb, S_bb = sb("c_bb", [128, 32], F32)
        c_masks, S_masks = sb("c_masks", [128, 4], F32)
        c_invm, S_invm = sb("c_invm", [128, 4], F32)
        cdbc, S_cdbc = sb("cdbc", [128, 128], F32)

        CN = dict(c_eyef=(c_eyef, S_eyef), c_eyeb=(c_eyeb, S_eyeb),
                  c_acol=(c_acol, S_acol), c_dtb=(c_dtb, S_dtb),
                  c_cw=(c_cw, S_cw), c_onesrow=(c_onesrow, S_onesrow),
                  c_onescol=(c_onescol, S_onescol), c_dpair=(c_dpair, S_dpair),
                  c_normw=(c_normw, S_normw), c_bb=(c_bb, S_bb),
                  c_masks=(c_masks, S_masks), c_invm=(c_invm, S_invm),
                  cdbc=(cdbc, S_cdbc))

        for dst, src, sl in [(c_eyef, eye_f, S_eyef), (c_eyeb, eye_b, S_eyeb),
                             (c_acol, acol_c, S_acol), (c_dtb, dtb_col, S_dtb),
                             (c_cw, cw_cols, S_cw),
                             (c_dpair, d_pair, S_dpair),
                             (c_normw, normw_cols, S_normw),
                             (c_bb, bb_cols, S_bb), (c_masks, masks, S_masks),
                             (c_invm, invmasks, S_invm),
                             (c_onesrow, ones_b, S_onesrow),
                             (c_onescol, onescol_b, S_onescol)]:
            k.dma(dst[:, :], src, writes=(sl,))
        k.flush("consts")

        with ExitStack() as es1:
            ln1_fm = es1.enter_context(
                nc.sbuf_tensor("ln1_fm", [128, 32 * TEXT], BF16))
            S_ln1 = Slot("ln1_fm")
            k.cur_stage = "s0"
            if on("s0"):
                _s0(nc, k, x_ext, ln1_fm, S_ln1, CN)
                k.flush("s0")
            k.cur_stage = "s1"
            if on("s1"):
                _s1(nc, k, wT, ln1_fm, S_ln1, d_proj, d_dt, DS)
                k.flush("s1")

        k.cur_stage = "s2a"
        if on("s2a"):
            _s2a(nc, k, DS, d_proj, d_xbca, cb_row, CN)
            k.flush("s2a")

        with ExitStack() as es2:
            E2 = es2.enter_context
            def sb2(name, shape, dt_):
                t = E2(nc.sbuf_tensor(name, shape, dt_))
                return t, Slot(name)
            da_all, S_daall = sb2("da_all", [128, NT], F32)
            expcs_all, S_ecs = sb2("expcs_all", [128, NT], F32)
            cumdec, S_cumdec = sb2("cumdec", [128, NCH], F32)
            big, S_big = sb2("big", [128, INTER], F32)
            csum, S_csum = sb2("csum", [128, 4], F32)
            P = dict(da_all=(da_all, S_daall),
                     expcs_all=(expcs_all, S_ecs),
                     cumdec=(cumdec, S_cumdec), big=(big, S_big),
                     csum=(csum, S_csum))
            k.op("vector", lambda e: e.memset(big[:, :], 0.0), writes=(S_big,))
            k.op("vector", lambda e: e.memset(csum[:, :], 0.0), writes=(S_csum,))

            k.cur_stage = "s2b"
            if on("s2b"):
                _s2b(nc, k, DS, d_xbca, d_dt, d_xdt, d_prev, d_row, CN, P)
                k.flush("s2b")
            k.cur_stage = "s3"
            if on("s3"):
                _s3(nc, k, DS, cc_in_t, cc_out_t, d_row, CN, P)
                k.flush("s3")
            k.cur_stage = "s4"
            if on("s4"):
                _s4(nc, k, DS, d_proj, d_xbca, d_xdt, d_prev, d_ynf,
                    d_row, d_rowb, CN, P)
                k.flush("s4")

        k.cur_stage = "s5"
        if on("s5"):
            _s5(nc, k, DS, woutT, d_ynf, d_attn)
            k.flush("s5")

        with ExitStack() as es3:
            ln3_fm = es3.enter_context(
                nc.sbuf_tensor("ln3_fm", [128, 32 * NT], BF16))
            S_ln3 = Slot("ln3_fm")
            k.cur_stage = "s6"
            if on("s6"):
                _s6(nc, k, DS, x_ext, d_attn, d_x2, ln3_fm, S_ln3, CN)
                k.flush("s6")
            k.cur_stage = "s7"
            if on("s7"):
                _s7(nc, k, DS, gwT, bwT, lAT, lBT, ln3_fm, S_ln3, d_ffn, CN)
                k.flush("s7")

        k.cur_stage = "s8"
        if on("s8"):
            _s8(nc, k, DS, d_x2, d_ffn, o_y, CN)
        k.flush("s8")

        if debug:
            dbg = [("dbg_proj", d_proj, [144, 128, TEXT], BF16, "s1"),
                   ("dbg_dt", d_dt, [128, TEXT], F32, "s1"),
                   ("dbg_xbca", d_xbca, [80, 128, NT], BF16, "s2a"),
                   ("dbg_xdt", d_xdt, [NCH, 128, INTER], BF16, "s2b"),
                   ("dbg_prev", d_prev, [NCH, 128, INTER], BF16, "s2b"),
                   ("dbg_ynf", d_ynf, [64, 128, NT], BF16, "s4"),
                   ("dbg_attn", d_attn, [32, 128, NT], BF16, "s5"),
                   ("dbg_x2", d_x2, [16, 128, DM], BF16, "s6"),
                   ("dbg_ffn", d_ffn, [32, 128, NT], BF16, "s7")]
            for nm, src, shape, dt_, st in dbg:
                if not on(st):
                    continue
                o = nc.dram_tensor(nm, shape, dt_, kind="ExternalOutput").ap()
                k.dma(o, src, reads=tuple(DS.values()))
            k.flush("dbg")
        k.finale()

    return nc


def _ln_stats(nc, k, xin, S_xin, stat, S_stat, ssq, S_ssq):
    k.op("vector", lambda e: e.reduce_sum(stat[:, 0:1], xin[:, :], axis=AX),
         reads=(S_xin,), writes=(S_stat,))
    k.op("scalar", lambda e: e.activation(ssq[:, :], xin[:, :], AF.Square),
         reads=(S_xin,), writes=(S_ssq,))
    k.op("vector", lambda e: e.reduce_sum(stat[:, 4:5], ssq[:, :], axis=AX),
         reads=(S_ssq,), writes=(S_stat,))
    for f in [
        lambda e: e.tensor_scalar(stat[:, 1:2], stat[:, 0:1], 1.0 / DM, None,
                                  ALU.mult),
        lambda e: e.tensor_scalar(stat[:, 2:3], stat[:, 4:5], 1.0 / DM, None,
                                  ALU.mult),
        lambda e: e.tensor_tensor(stat[:, 3:4], stat[:, 1:2], stat[:, 1:2],
                                  ALU.mult),
        lambda e: e.tensor_tensor(stat[:, 2:3], stat[:, 2:3], stat[:, 3:4],
                                  ALU.subtract),
        lambda e: e.tensor_scalar(stat[:, 2:3], stat[:, 2:3], EPS, None,
                                  ALU.add),
    ]:
        k.op("vector", f, reads=(S_stat,), writes=(S_stat,))
    k.op("scalar", lambda e: e.activation(stat[:, 5:6], stat[:, 2:3], AF.Ln),
         reads=(S_stat,), writes=(S_stat,))
    k.op("scalar", lambda e: e.activation(stat[:, 6:7], stat[:, 5:6], AF.Exp,
                                          scale=-0.5),
         reads=(S_stat,), writes=(S_stat,))
    k.op("vector", lambda e: e.tensor_tensor(stat[:, 7:8], stat[:, 1:2],
                                             stat[:, 6:7], ALU.mult),
         reads=(S_stat,), writes=(S_stat,))
    k.op("vector", lambda e: e.tensor_scalar(stat[:, 7:8], stat[:, 7:8], -1.0,
                                             None, ALU.mult),
         reads=(S_stat,), writes=(S_stat,))


def _s0(nc, k, x_ext, ln1_fm, S_ln1, CN):
    c_eyeb, S_eyeb = CN["c_eyeb"]
    NTile = TEXT // 128
    with ExitStack() as es:
        E = es.enter_context
        sx = [E(nc.sbuf_tensor(f"s0x{i}", [128, DM], F32)) for i in range(2)]
        Sx = [Slot(f"s0x{i}") for i in range(2)]
        sln = [E(nc.sbuf_tensor(f"s0l{i}", [128, DM], BF16)) for i in range(2)]
        Sl = [Slot(f"s0l{i}") for i in range(2)]
        stat = E(nc.sbuf_tensor("s0stat", [128, 8], F32))
        Sst = Slot("s0stat")
        ps = [E(nc.psum_tensor(f"s0p{i}", [128, 128], BF16)) for i in range(4)]
        Sp = [Slot(f"s0p{i}") for i in range(4)]
        for i in range(NTile):
            p = i % 2
            k.dma(sx[p][:, :], x_ext[i * 128:(i + 1) * 128, :], writes=(Sx[p],))
            _ln_stats(nc, k, sx[p], Sx[p], stat, Sst, sln[p], Sl[p])
            k.op("scalar",
                 lambda e, p=p: e.activation(sln[p][:, :], sx[p][:, :], AF.Identity,
                                             bias=stat[:, 7:8], scale=stat[:, 6:7]),
                 reads=(Sx[p], Sst), writes=(Sl[p],))
            for kt in range(32):
                b = kt % 4
                k.op("tensor",
                     lambda e, p=p, kt=kt, b=b: e.transpose(
                         ps[b][:, :], sln[p][:, kt * 128:(kt + 1) * 128],
                         c_eyeb[:, :]),
                     reads=(Sl[p], S_eyeb), writes=(Sp[b],))
                k.op("vector",
                     lambda e, i=i, kt=kt, b=b: e.tensor_copy(
                         ln1_fm[:, kt * TEXT + i * 128: kt * TEXT + (i + 1) * 128],
                         ps[b][:, :]),
                     reads=(Sp[b],), writes=(S_ln1,))


def _s1(nc, k, wT, ln1_fm, S_ln1, d_proj, d_dt, DS):
    NTC = [512, 512, 512, 512, 128]
    with ExitStack() as es:
        E = es.enter_context
        swf = E(nc.sbuf_tensor("s1wf", [128, DM], F32))
        Swf = Slot("s1wf")
        swb = [E(nc.sbuf_tensor(f"s1wb{i}", [128, DM], BF16)) for i in range(2)]
        Swb = [Slot(f"s1wb{i}") for i in range(2)]
        spj = [E(nc.sbuf_tensor(f"s1pj{i}", [128, TEXT], BF16)) for i in range(2)]
        Spj = [Slot(f"s1pj{i}") for i in range(2)]
        sdt = E(nc.sbuf_tensor("s1dt", [128, TEXT], F32))
        Sdt = Slot("s1dt")
        pss = [E(nc.psum_tensor(f"s1p{i}", [128, 512], F32)) for i in range(5)]
        Sps = [Slot(f"s1p{i}") for i in range(5)]

        for ot in range(145):
            p = ot % 2
            k.dma(swf[:, :], wT[ot], writes=(Swf,), ch=ot % 4)
            k.op("vector", lambda e, p=p: e.tensor_copy(swb[p][:, :], swf[:, :]),
                 reads=(Swf,), writes=(Swb[p],))

            def mm(e, p=p):
                last = None
                for kt in range(32):
                    for tc, n in enumerate(NTC):
                        o0 = sum(NTC[:tc])
                        last = e.matmul(
                            pss[tc][:, 0:n],
                            swb[p][:, kt * 128:(kt + 1) * 128],
                            ln1_fm[:, kt * TEXT + o0: kt * TEXT + o0 + n],
                            start=(kt == 0), stop=(kt == 31))
                return last
            k.op("tensor", mm, reads=(Swb[p], S_ln1), writes=tuple(Sps))

            if ot < 144:
                def evac(e, p=p):
                    last = None
                    for tc, n in enumerate(NTC):
                        o0 = sum(NTC[:tc])
                        last = e.copy(spj[p][:, o0:o0 + n], pss[tc][:, 0:n])
                    return last
                k.op("scalar", evac, reads=tuple(Sps), writes=(Spj[p],))
                k.dma(d_proj[ot], spj[p][:, :], reads=(Spj[p],),
                      writes=(DS["d_proj"],), eng="scalar", ch=6 + ot % 2)
            else:
                def evacdt(e):
                    last = None
                    for tc, n in enumerate(NTC):
                        o0 = sum(NTC[:tc])
                        last = e.copy(sdt[:, o0:o0 + n], pss[tc][:, 0:n])
                    return last
                k.op("scalar", evacdt, reads=tuple(Sps), writes=(Sdt,))
                k.dma(d_dt, sdt[:, :], reads=(Sdt,), writes=(DS["d_dt"],),
                      eng="scalar", ch=6)


def _s2a(nc, k, DS, d_proj, d_xbca, cb_row, CN):
    c_cw, S_cw = CN["c_cw"]
    c_onesrow, S_onesrow = CN["c_onesrow"]
    with ExitStack() as es:
        E = es.enter_context
        cdg = E(nc.sbuf_tensor("cdg", [128, 40 * 128], BF16))
        Scdg = Slot("cdg")
        sxc = [E(nc.sbuf_tensor(f"s2ax{i}", [128, 10 * TEXT], BF16)) for i in range(2)]
        Sxc = [Slot(f"s2ax{i}") for i in range(2)]
        sxo = [E(nc.sbuf_tensor(f"s2ao{i}", [128, 10 * NT], BF16)) for i in range(2)]
        Sxo = [Slot(f"s2ao{i}") for i in range(2)]
        cbf = E(nc.sbuf_tensor("s2acbf", [1, 1280], F32))
        Scbf = Slot("s2acbf")
        cbb = E(nc.sbuf_tensor("s2acbb", [1, 1280], BF16))
        Scbb = Slot("s2acbb")
        psC = [E(nc.psum_tensor(f"s2ap{i}", [128, 512], F32)) for i in range(8)]
        SpC = [Slot(f"s2ap{i}") for i in range(8)]

        ones4 = bass.AP(c_onesrow[:, :].tensor, 0, [[128, 1], [0, 4], [1, 128]])
        for og in range(8):
            p = og % 2
            k.dma(cbf[:, :], cb_row[0:1, og * 1280:(og + 1) * 1280],
                  writes=(Scbf,))
            k.op("vector", lambda e: e.tensor_copy(cbb[:, :], cbf[:, :]),
                 reads=(Scbf,), writes=(Scbb,))
            for i in range(10):
                ot = og * 10 + i
                k.dma(sxc[p][:, i * TEXT:(i + 1) * TEXT], d_proj[64 + ot],
                      reads=(DS["d_proj"],), writes=(Sxc[p],), ch=i % 4)
            def build_diag(e, og=og):
                last = None
                for i in range(10):
                    for j in range(4):
                        idx = (og * 10 + i) * 4 + j
                        last = e.affine_select(
                            cdg[:, (i * 4 + j) * 128:(i * 4 + j + 1) * 128],
                            c_cw[:, idx:idx + 1].broadcast_to([128, 128]),
                            [[-1, 128]], ALU.is_equal, 0.0,
                            base=0, channel_multiplier=1)
                return last
            k.op("gpsimd", build_diag, reads=(S_cw,), writes=(Scdg,))

            for i in range(10):
                ot = og * 10 + i
                pb = (i % 2) * 4
                def convmm(e, i=i, ot=ot, pb=pb, p=p):
                    last = None
                    for tc in range(4):
                        t0 = PAD + tc * 512
                        for j in range(4):
                            last = e.matmul(
                                psC[pb + tc][:, :],
                                cdg[:, (i * 4 + j) * 128:(i * 4 + j + 1) * 128],
                                sxc[p][:, i * TEXT + t0 + j - 3 + 3:
                                       i * TEXT + t0 + j + 512],
                                start=(j == 0), stop=False)
                        last = e.matmul(
                            psC[pb + tc][:, :],
                            cbb[0:1, i * 128:(i + 1) * 128],
                            ones4,
                            start=False, stop=True)
                    return last
                k.op("tensor", convmm, reads=(Scdg, Sxc[p], S_onesrow, Scbb),
                     writes=tuple(SpC[pb + tc] for tc in range(4)))
                def silu(e, i=i, pb=pb, p=p):
                    last = None
                    for tc in range(4):
                        last = e.activation(
                            sxo[p][:, i * NT + tc * 512:i * NT + (tc + 1) * 512],
                            psC[pb + tc][:, :], AF.Silu)
                    return last
                k.op("scalar", silu, reads=tuple(SpC[pb + tc] for tc in range(4)),
                     writes=(Sxo[p],))
            for i in range(10):
                ot = og * 10 + i
                k.dma(d_xbca[ot], sxo[p][:, i * NT:(i + 1) * NT],
                      reads=(Sxo[p],), writes=(DS["d_xbca"],),
                      eng="scalar", ch=6 + i % 2)


def _s2b(nc, k, DS, d_xbca, d_dt, d_xdt, d_prev, d_row, CN, P):
    c_dtb, S_dtb = CN["c_dtb"]
    c_acol, S_acol = CN["c_acol"]
    c_eyef, S_eyef = CN["c_eyef"]
    c_eyeb, S_eyeb = CN["c_eyeb"]
    cdbc, S_cdbc = CN["cdbc"]
    da_all, S_daall = P["da_all"]
    expcs_all, S_ecs = P["expcs_all"]
    cumdec, S_cumdec = P["cumdec"]
    big, S_big = P["big"]
    csum, S_csum = P["csum"]
    with ExitStack() as es:
        E = es.enter_context
        sxa = [E(nc.sbuf_tensor(f"s2bx{i}", [128, 72 * 128], BF16))
               for i in range(2)]
        Sxa = [Slot(f"s2bx{i}") for i in range(2)]
        sdt = E(nc.sbuf_tensor("s2bdt", [128, 128], F32))
        Sdt = Slot("s2bdt")
        dtw = E(nc.sbuf_tensor("s2bw", [128, 7 * 128], F32))
        Sdtw = Slot("s2bw")
        zz = E(nc.sbuf_tensor("s2bz", [128, 128], F32))
        Szz = Slot("s2bz")
        xdt_sb = E(nc.sbuf_tensor("s2bxd", [128, INTER], BF16))
        Sxdt = Slot("s2bxd")
        xdtw_sb = E(nc.sbuf_tensor("s2bxw", [128, INTER], BF16))
        Sxdtw = Slot("s2bxw")
        btm = E(nc.sbuf_tensor("s2bbt", [128, 8 * 128], BF16))
        Sbtm = Slot("s2bbt")
        st_all = E(nc.sbuf_tensor("s2bst", [128, INTER], BF16))
        S_stall = Slot("s2bst")
        prevb = E(nc.sbuf_tensor("s2bpv", [128, INTER], BF16))
        S_prevb = Slot("s2bpv")
        psT = [E(nc.psum_tensor(f"s2bt{i}", [128, 128], BF16)) for i in range(2)]
        SpT = [Slot(f"s2bt{i}") for i in range(2)]
        psF = E(nc.psum_tensor("s2bf", [128, 128], F32))
        SpF = Slot("s2bf")
        psS = [E(nc.psum_tensor(f"s2bs{i}", [128, 1024], F32)) for i in range(2)]
        SpS = [Slot(f"s2bs{i}") for i in range(2)]

        k.op("vector", lambda e: e.memset(zz[:, :], 0.0), writes=(Szz,))

        DT, DA, CS, DE, DTT, DET, WC = range(7)
        def col(i):
            return dtw[:, i * 128:(i + 1) * 128]

        for c in range(NCH):
            pb4 = c % 2
            for s4 in range(4):
                k.dma(sxa[pb4][:, s4 * 18 * 128:(s4 + 1) * 18 * 128],
                      d_xbca[s4 * 18:(s4 + 1) * 18]
                      .transpose([1, 0, 2])[:, :, c * 128:(c + 1) * 128],
                      reads=(DS["d_xbca"],), writes=(Sxa[pb4],), ch=s4)
            k.dma(sdt[:, :], d_dt[:, 128 + c * 128:128 + (c + 1) * 128],
                  reads=(DS["d_dt"],), writes=(Sdt,), ch=5)

            k.op("scalar", lambda e: e.activation(col(DT), sdt[:, :], AF.Exp,
                                                  bias=c_dtb[:, :]),
                 reads=(Sdt, S_dtb), writes=(Sdtw,))
            k.op("scalar", lambda e: e.activation(col(DT), col(DT), AF.Ln, bias=1.0),
                 reads=(Sdtw,), writes=(Sdtw,))
            k.op("vector", lambda e: e.tensor_scalar(col(DT), col(DT), 1e-3,
                                                     100.0, ALU.max, ALU.min),
                 reads=(Sdtw,), writes=(Sdtw,))
            k.op("vector", lambda e: e.tensor_scalar(col(DA), col(DT),
                                                     c_acol[:, :], -80.0,
                                                     ALU.mult, ALU.max),
                 reads=(Sdtw, S_acol), writes=(Sdtw,))
            k.op("vector", lambda e: e.tensor_tensor_scan(col(CS), col(DA),
                                                          zz[:, :], 0.0,
                                                          ALU.add, ALU.add),
                 reads=(Sdtw, Szz), writes=(Sdtw,))
            k.op("vector",
                 lambda e, c=c: e.tensor_copy(da_all[:, c * 128:(c + 1) * 128],
                                              col(DA)),
                 reads=(Sdtw,), writes=(S_daall,))
            k.op("scalar",
                 lambda e, c=c: e.activation(expcs_all[:, c * 128:(c + 1) * 128],
                                             col(CS), AF.Exp),
                 reads=(Sdtw,), writes=(S_ecs,))
            k.op("scalar", lambda e, c=c: e.activation(cumdec[:, c:c + 1],
                                                       csum[:, 0:1], AF.Exp),
                 reads=(S_csum,), writes=(S_cumdec,))
            k.op("scalar", lambda e: e.activation(col(DE), col(CS), AF.Exp,
                                                  bias=col(CS)[:, 127:128],
                                                  scale=-1.0),
                 reads=(Sdtw,), writes=(Sdtw,))
            k.op("vector", lambda e: e.tensor_tensor(csum[:, 0:1], csum[:, 0:1],
                                                     col(CS)[:, 127:128], ALU.add),
                 reads=(Sdtw,), writes=(S_csum,))
            k.op("tensor", lambda e: e.transpose(psF[:, :], col(DT), c_eyef[:, :]),
                 reads=(Sdtw, S_eyef), writes=(SpF,))
            k.op("vector", lambda e: e.tensor_copy(col(DTT), psF[:, :]),
                 reads=(SpF,), writes=(Sdtw,))
            k.op("tensor", lambda e: e.transpose(psF[:, :], col(DE), c_eyef[:, :]),
                 reads=(Sdtw, S_eyef), writes=(SpF,))
            k.op("vector", lambda e: e.tensor_copy(col(DET), psF[:, :]),
                 reads=(SpF,), writes=(Sdtw,))
            k.op("vector", lambda e: e.tensor_tensor(col(WC), col(DTT),
                                                     col(DET), ALU.mult),
                 reads=(Sdtw,), writes=(Sdtw,))

            for tp in range(64):
                pb = tp % 2
                k.op("tensor",
                     lambda e, tp=tp, pb=pb, pb4=pb4: e.transpose(
                         psT[pb][:, :], sxa[pb4][:, tp * 128:(tp + 1) * 128],
                         c_eyeb[:, :]),
                     reads=(Sxa[pb4], S_eyeb), writes=(SpT[pb],))
                def vxdt(e, tp=tp, pb=pb):
                    h0, h1 = 2 * tp, 2 * tp + 1
                    e.tensor_scalar(xdt_sb[:, tp * 128:tp * 128 + 64],
                                    psT[pb][:, 0:64], col(DTT)[:, h0:h0 + 1],
                                    None, ALU.mult)
                    e.tensor_scalar(xdt_sb[:, tp * 128 + 64:tp * 128 + 128],
                                    psT[pb][:, 64:128], col(DTT)[:, h1:h1 + 1],
                                    None, ALU.mult)
                    e.tensor_scalar(xdtw_sb[:, tp * 128:tp * 128 + 64],
                                    psT[pb][:, 0:64], col(WC)[:, h0:h0 + 1],
                                    None, ALU.mult)
                    return e.tensor_scalar(xdtw_sb[:, tp * 128 + 64:tp * 128 + 128],
                                           psT[pb][:, 64:128], col(WC)[:, h1:h1 + 1],
                                           None, ALU.mult)
                k.op("vector", vxdt, reads=(SpT[pb], Sdtw), writes=(Sxdt, Sxdtw))
            k.dma(d_xdt[c], xdt_sb[:, :], reads=(Sxdt,),
                  writes=(DS["d_xdt"],), eng="gpsimd", ch=6)

            for g in range(8):
                pb = g % 2
                k.op("tensor",
                     lambda e, g=g, pb=pb, pb4=pb4: e.transpose(
                         psT[pb][:, :], sxa[pb4][:, (64 + g) * 128:(65 + g) * 128],
                         c_eyeb[:, :]),
                     reads=(Sxa[pb4], S_eyeb), writes=(SpT[pb],))
                k.op("vector",
                     lambda e, g=g, pb=pb: e.tensor_copy(
                         btm[:, g * 128:(g + 1) * 128], psT[pb][:, :]),
                     reads=(SpT[pb],), writes=(Sbtm,))

            k.op("vector", lambda e: e.tensor_copy(prevb[:, :], big[:, :]),
                 reads=(S_big,), writes=(S_prevb,))
            k.dma(d_prev[c], prevb[:, :], reads=(S_prevb,),
                  writes=(DS["d_prev"],), eng="gpsimd", ch=7)

            cdcol = expcs_all[:, c * 128 + 127:c * 128 + 128]
            k.dma(d_row[0:1, 0:128], cdcol, reads=(S_ecs,),
                  writes=(DS["d_row"],), eng="scalar", ch=8)
            k.dma(cdbc[:, :],
                  bass.AP(d_row.tensor, 0, [[16384, 1], [0, 128], [1, 128]]),
                  reads=(DS["d_row"],), writes=(S_cdbc,), eng="scalar", ch=8)
            rep = bass.AP(cdbc[:, :].tensor, 0, [[128, 128], [1, 128], [0, 64]])
            k.op("vector", lambda e, rep=rep: e.tensor_tensor(
                     big[:, :], big[:, :], rep, ALU.mult),
                 reads=(S_cdbc,), writes=(S_big,))

            for g in range(8):
                pb = g % 2
                def stmm(e, g=g, pb=pb):
                    last = None
                    for hh in range(16):
                        h = g * 16 + hh
                        last = e.matmul(
                            psS[pb][:, hh * 64:(hh + 1) * 64],
                            btm[:, g * 128:(g + 1) * 128],
                            xdtw_sb[:, h * 64:(h + 1) * 64],
                            start=True, stop=True)
                    return last
                k.op("tensor", stmm, reads=(Sbtm, Sxdtw), writes=(SpS[pb],))
                k.op("scalar",
                     lambda e, g=g, pb=pb: e.copy(
                         st_all[:, g * 1024:(g + 1) * 1024], psS[pb][:, :]),
                     reads=(SpS[pb],), writes=(S_stall,))
            k.op("vector", lambda e: e.tensor_tensor(big[:, :], big[:, :],
                                                     st_all[:, :], ALU.add),
                 reads=(S_stall,), writes=(S_big,))


def _s3(nc, k, DS, cc_in_t, cc_out_t, d_row, CN, P):
    c_masks, S_masks = CN["c_masks"]
    c_invm, S_invm = CN["c_invm"]
    cdbc, S_cdbc = CN["cdbc"]
    big, S_big = P["big"]
    csum, S_csum = P["csum"]
    with ExitStack() as es:
        E = es.enter_context
        ccsb = E(nc.sbuf_tensor("s3cc", [128, 8256], BF16))
        Scc = Slot("s3cc")
        shard = E(nc.sbuf_tensor("s3sh", [128, 8256], BF16))
        Ssh = Slot("s3sh")
        dsel = E(nc.sbuf_tensor("s3ds", [128, 4], F32))
        Sds = Slot("s3ds")
        ssel = E(nc.sbuf_tensor("s3ss", [128, INTER], BF16))
        Sss = Slot("s3ss")

        k.op("scalar", lambda e: e.activation(csum[:, 1:2], csum[:, 0:1], AF.Exp),
             reads=(S_csum,), writes=(S_csum,))
        k.op("vector", lambda e: e.tensor_copy(ccsb[:, 0:INTER], big[:, :]),
             reads=(S_big,), writes=(Scc,))
        k.op("vector", lambda e: e.memset(ccsb[:, INTER:8256], 0.0),
             writes=(Scc,))
        k.op("vector", lambda e: e.tensor_copy(ccsb[:, INTER:INTER + 1],
                                               csum[:, 1:2]),
             reads=(S_csum,), writes=(Scc,))
        k.dma(cc_in_t.ap(), ccsb[:, :], reads=(Scc,), writes=(DS["cc"],))
        k.op("gpsimd",
             lambda e: e.collective_compute(
                 "AllGather", ALU.bypass,
                 replica_groups=[[0, 1, 2, 3], [4, 5, 6, 7]],
                 ins=[cc_in_t.ap().opt()], outs=[cc_out_t.ap().opt()]),
             reads=(DS["cc"],), writes=(DS["cc"],))
        k.op("vector", lambda e: e.memset(big[:, :], 0.0),
             reads=(DS["cc"],), writes=(S_big,))
        for j in range(4):
            k.dma(shard[:, :], cc_out_t.ap()[j * 128:(j + 1) * 128, :],
                  reads=(DS["cc"],), writes=(Ssh,))
            k.op("vector",
                 lambda e, j=j: e.scalar_tensor_tensor(
                     dsel[:, j:j + 1], shard[:, INTER:INTER + 1],
                     c_masks[:, j:j + 1], c_invm[:, j:j + 1],
                     ALU.mult, ALU.add),
                 reads=(Ssh, S_masks, S_invm), writes=(Sds,))
            k.dma(d_row[0:1, 0:128], dsel[:, j:j + 1], reads=(Sds,),
                  writes=(DS["d_row"],))
            k.dma(cdbc[:, :],
                  bass.AP(d_row.tensor, 0, [[16384, 1], [0, 128], [1, 128]]),
                  reads=(DS["d_row"],), writes=(S_cdbc,))
            rep = bass.AP(cdbc[:, :].tensor, 0, [[128, 128], [1, 128], [0, 64]])
            k.op("vector", lambda e, rep=rep: e.tensor_tensor(
                     big[:, :], big[:, :], rep, ALU.mult),
                 reads=(S_cdbc,), writes=(S_big,))
            k.op("vector", lambda e, j=j: e.tensor_scalar(
                     ssel[:, :], shard[:, 0:INTER], c_masks[:, j:j + 1],
                     None, ALU.mult),
                 reads=(Ssh, S_masks), writes=(Sss,))
            k.op("vector", lambda e: e.tensor_tensor(big[:, :], big[:, :],
                                                     ssel[:, :], ALU.add),
                 reads=(Sss,), writes=(S_big,))


def _s4(nc, k, DS, d_proj, d_xbca, d_xdt, d_prev, d_ynf, d_row, d_rowb, CN, P):
    c_eyef, S_eyef = CN["c_eyef"]
    c_eyeb, S_eyeb = CN["c_eyeb"]
    c_dpair, S_dpair = CN["c_dpair"]
    c_normw, S_normw = CN["c_normw"]
    c_onescol, S_onescol = CN["c_onescol"]
    cdbc, S_cdbc = CN["cdbc"]
    da_all, S_daall = P["da_all"]
    expcs_all, S_ecs = P["expcs_all"]
    cumdec, S_cumdec = P["cumdec"]
    big, S_big = P["big"]
    with ExitStack() as es:
        E = es.enter_context
        sgate = E(nc.sbuf_tensor("s4g", [128, INTER], BF16))
        Sg = Slot("s4g")
        sxa = [E(nc.sbuf_tensor(f"s4xa{i}", [128, 80 * 128], BF16))
               for i in range(2)]
        Sxa = [Slot(f"s4xa{i}") for i in range(2)]
        sxdt = E(nc.sbuf_tensor("s4xd", [128, INTER], BF16))
        Sxd = Slot("s4xd")
        sprev = E(nc.sbuf_tensor("s4pv", [128, INTER], BF16))
        Spv = Slot("s4pv")
        abc = E(nc.sbuf_tensor("s4abc", [128, 32 * 128], F32))
        Sabc = Slot("s4abc")
        ebc = E(nc.sbuf_tensor("s4ebc", [128, 32 * 128], BF16))
        Sebc = Slot("s4ebc")
        ebf = E(nc.sbuf_tensor("s4ebf", [128, 128], BF16))
        Sebf = Slot("s4ebf")
        scg = E(nc.sbuf_tensor("s4scg", [128, 8 * 128], BF16))
        Sscg = Slot("s4scg")
        ainv = E(nc.sbuf_tensor("s4ai", [128, 128], F32))
        Sai = Slot("s4ai")
        lm = [E(nc.sbuf_tensor(f"s4lm{i}", [128, 128], F32)) for i in range(2)]
        Slm = [Slot(f"s4lm{i}") for i in range(2)]
        scl = [E(nc.sbuf_tensor(f"s4sl{i}", [128, 128], BF16)) for i in range(4)]
        Sscl = [Slot(f"s4sl{i}") for i in range(4)]
        cfe = [E(nc.sbuf_tensor(f"s4cf{i}", [128, 128], BF16)) for i in range(4)]
        Scfe = [Slot(f"s4cf{i}") for i in range(4)]
        ych = E(nc.sbuf_tensor("s4y", [128, INTER], BF16))
        Sy = Slot("s4y")
        gn = E(nc.sbuf_tensor("s4gn", [1, 1024], F32))
        Sgn = Slot("s4gn")
        gnbc = E(nc.sbuf_tensor("s4gnb", [128, 1024], F32))
        Sgnb = Slot("s4gnb")
        if DEBUG:
            d_ych = nc.dram_tensor("dbg_ych", [NCH, 128, INTER], BF16,
                                   kind="ExternalOutput").ap()
            S_dych = Slot("d_ych")
            d_ygt = nc.dram_tensor("dbg_ygt", [NCH, 128, INTER], BF16,
                                   kind="ExternalOutput").ap()
            S_dygt = Slot("d_ygt")
            d_gnr = nc.dram_tensor("dbg_gnr", [NCH, 1, 1024], F32,
                                   kind="ExternalOutput").ap()
            S_dgnr = Slot("d_gnr")
        psSc = [E(nc.psum_tensor(f"s4pc{i}", [128, 128], F32)) for i in range(2)]
        SpSc = [Slot(f"s4pc{i}") for i in range(2)]
        psY = [E(nc.psum_tensor(f"s4py{i}", [128, 128], F32)) for i in range(2)]
        SpY = [Slot(f"s4py{i}") for i in range(2)]
        psF = E(nc.psum_tensor("s4pfp", [128, 128], F32))
        SpF = Slot("s4pfp")
        psGN = E(nc.psum_tensor("s4pgn", [1, 1024], F32))
        SpGN = Slot("s4pgn")

        for c in range(NCH):
            w0 = PAD + LOOK + c * 128
            pb4 = c % 2
            for s4 in range(4):
                k.dma(sxa[pb4][:, s4 * 20 * 128:(s4 + 1) * 20 * 128],
                      d_xbca[s4 * 20:(s4 + 1) * 20]
                      .transpose([1, 0, 2])[:, :, c * 128:(c + 1) * 128],
                      reads=(DS["d_xbca"],), writes=(Sxa[pb4],), ch=4 + s4)
            for s4 in range(4):
                k.dma(sgate[:, s4 * 16 * 128:(s4 + 1) * 16 * 128],
                      d_proj[s4 * 16:(s4 + 1) * 16]
                      .transpose([1, 0, 2])[:, :, w0:w0 + 128],
                      reads=(DS["d_proj"],), writes=(Sg,), ch=s4)
            k.dma(sxdt[:, :], d_xdt[c], reads=(DS["d_xdt"],), writes=(Sxd,),
                  eng="gpsimd", ch=10)
            k.dma(sprev[:, :], d_prev[c], reads=(DS["d_prev"],), writes=(Spv,),
                  eng="gpsimd", ch=11)

            k.dma(d_row[0:1, 0:128], cumdec[:, c:c + 1], reads=(S_cumdec,),
                  writes=(DS["d_row"],), eng="scalar", ch=8)
            k.dma(cdbc[:, :],
                  bass.AP(d_row.tensor, 0, [[16384, 1], [0, 128], [1, 128]]),
                  reads=(DS["d_row"],), writes=(S_cdbc,), eng="scalar", ch=8)
            rep = bass.AP(cdbc[:, :].tensor, 0, [[128, 128], [1, 128], [0, 64]])
            k.op("vector", lambda e, rep=rep: e.tensor_tensor(
                     ych[:, :], big[:, :], rep, ALU.mult),
                 reads=(S_big, S_cdbc), writes=(Sy,))
            k.op("vector", lambda e: e.tensor_tensor(sprev[:, :], ych[:, :],
                                                     sprev[:, :], ALU.add),
                 reads=(Sy, Spv), writes=(Spv,))

            for g in range(8):
                pb = g % 2
                k.op("tensor",
                     lambda e, g=g, pb=pb, pb4=pb4: e.matmul(
                         psSc[pb][:, :],
                         sxa[pb4][:, (64 + g) * 128:(65 + g) * 128],
                         sxa[pb4][:, (72 + g) * 128:(73 + g) * 128],
                         start=True, stop=True),
                     reads=(Sxa[pb4],), writes=(SpSc[pb],))
                k.op("scalar",
                     lambda e, g=g, pb=pb: e.copy(scg[:, g * 128:(g + 1) * 128],
                                                  psSc[pb][:, :]),
                     reads=(SpSc[pb],), writes=(Sscg,))

            k.dma(d_row, da_all[:, c * 128:(c + 1) * 128], reads=(S_daall,),
                  writes=(DS["d_row"],), eng="scalar", ch=9)
            k.op("vector",
                 lambda e, c=c: e.tensor_copy(ebf[:, :],
                                              expcs_all[:, c * 128:(c + 1) * 128]),
                 reads=(S_ecs,), writes=(Sebf,))
            k.dma(d_rowb, ebf[:, :], reads=(Sebf,), writes=(DS["d_rowb"],),
                  eng="scalar", ch=9)
            k.op("tensor",
                 lambda e, c=c: e.transpose(psF[:, :],
                                            da_all[:, c * 128:(c + 1) * 128],
                                            c_eyef[:, :]),
                 reads=(S_daall, S_eyef), writes=(SpF,))
            k.op("vector", lambda e: e.tensor_copy(ainv[:, :], psF[:, :]),
                 reads=(SpF,), writes=(Sai,))
            k.op("scalar", lambda e: e.activation(ainv[:, :], ainv[:, :],
                                                  AF.Exp, scale=-1.0),
                 reads=(Sai,), writes=(Sai,))

            for q in range(4):
                k.dma(abc[:, :],
                      bass.AP(d_row.tensor, q * 4096,
                              [[16384, 1], [0, 128], [1, 4096]]),
                      reads=(DS["d_row"],), writes=(Sabc,), eng="scalar", ch=9)
                k.op("scalar", lambda e: e.activation(abc[:, :], abc[:, :],
                                                      AF.Exp),
                     reads=(Sabc,), writes=(Sabc,))
                k.dma(ebc[:, :],
                      bass.AP(d_rowb.tensor, q * 4096,
                              [[16384, 1], [0, 128], [1, 4096]]),
                      reads=(DS["d_rowb"],), writes=(Sebc,), eng="scalar", ch=9)
                for hh in range(32):
                    h = q * 32 + hh
                    g, tp, half = h // 16, h // 2, h % 2
                    sl = h % 4
                    lmp = h % 2
                    k.op("vector",
                         lambda e, hh=hh, lmp=lmp: e.tensor_tensor_scan(
                             lm[lmp][:, :], c_eyef[:, :],
                             abc[:, hh * 128:(hh + 1) * 128], 0.0,
                             ALU.add, ALU.mult),
                         reads=(S_eyef, Sabc), writes=(Slm[lmp],))
                    k.op("vector",
                         lambda e, h=h, g=g, sl=sl, lmp=lmp:
                         e.scalar_tensor_tensor(
                             scl[sl][:, :], lm[lmp][:, :], ainv[:, h:h + 1],
                             scg[:, g * 128:(g + 1) * 128], ALU.mult, ALU.mult),
                         reads=(Slm[lmp], Sai, Sscg), writes=(Sscl[sl],))
                    k.op("vector",
                         lambda e, hh=hh, g=g, sl=sl, pb4=pb4: e.tensor_tensor(
                             cfe[sl][:, :],
                             sxa[pb4][:, (72 + g) * 128:(73 + g) * 128],
                             ebc[:, hh * 128:(hh + 1) * 128], ALU.mult),
                         reads=(Sxa[pb4], Sebc), writes=(Scfe[sl],))
                    yb = psY[tp % 2]
                    SpYb = SpY[tp % 2]
                    def ymm(e, h=h, half=half, sl=sl, yb=yb):
                        e.matmul(yb[half * 64:(half + 1) * 64, :],
                                 sxdt[:, h * 64:(h + 1) * 64],
                                 scl[sl][:, :],
                                 start=True, stop=False)
                        return e.matmul(yb[half * 64:(half + 1) * 64, :],
                                        sprev[:, h * 64:(h + 1) * 64],
                                        cfe[sl][:, :],
                                        start=False, stop=True)
                    k.op("tensor", ymm, reads=(Sxd, Sscl[sl], Spv, Scfe[sl]),
                         writes=(SpYb,))
                    if half == 1:
                        k.op("vector",
                             lambda e, tp=tp, pb4=pb4: e.tensor_scalar(
                                 ych[:, tp * 128:(tp + 1) * 128],
                                 sxa[pb4][:, tp * 128:(tp + 1) * 128],
                                 c_dpair[:, tp:tp + 1], None, ALU.mult),
                             reads=(Sxa[pb4], S_dpair), writes=(Sy,))
                        k.op("vector",
                             lambda e, tp=tp, yb=yb: e.tensor_tensor(
                                 ych[:, tp * 128:(tp + 1) * 128], yb[:, :],
                                 ych[:, tp * 128:(tp + 1) * 128], ALU.add),
                             reads=(SpYb, Sy), writes=(Sy,))

            if DEBUG:
                k.dma(d_ych[c], ych[:, :], reads=(Sy,), writes=(S_dych,))
            k.op("scalar", lambda e: e.activation(sgate[:, :], sgate[:, :],
                                                  AF.Silu),
                 reads=(Sg,), writes=(Sg,))
            k.op("vector", lambda e: e.tensor_tensor(ych[:, :], ych[:, :],
                                                     sgate[:, :], ALU.mult),
                 reads=(Sg, Sy), writes=(Sy,))
            if DEBUG:
                k.dma(d_ygt[c], ych[:, :], reads=(Sy,), writes=(S_dygt,))
            k.op("scalar", lambda e: e.activation(sgate[:, :], ych[:, :],
                                                  AF.Square),
                 reads=(Sy,), writes=(Sg,))
            def gnmm(e):
                last = None
                for g in range(8):
                    for t8 in range(8):
                        last = e.matmul(psGN[0:1, g * 128:(g + 1) * 128],
                                        c_onescol[:, :],
                                        sgate[:, (8 * g + t8) * 128:
                                              (8 * g + t8 + 1) * 128],
                                        start=(t8 == 0),
                                        stop=(t8 == 7))
                return last
            k.op("tensor", gnmm, reads=(Sg, S_onescol), writes=(SpGN,))
            k.op("vector", lambda e: e.tensor_scalar(
                     gn[0:1, 0:1024], psGN[0:1, :], 1.0 / 1024.0, EPS,
                     ALU.mult, ALU.add),
                 reads=(SpGN,), writes=(Sgn,))
            k.op("scalar", lambda e: e.activation(gn[0:1, 0:1024],
                                                  gn[0:1, 0:1024], AF.Ln),
                 reads=(Sgn,), writes=(Sgn,))
            k.op("scalar", lambda e: e.activation(gn[0:1, 0:1024],
                                                  gn[0:1, 0:1024], AF.Exp,
                                                  scale=-0.5),
                 reads=(Sgn,), writes=(Sgn,))
            if DEBUG:
                k.dma(d_gnr[c], gn[0:1, 0:1024], reads=(Sgn,),
                      writes=(S_dgnr,))
            k.dma(d_row[0:1, 0:1024], gn[0:1, 0:1024], reads=(Sgn,),
                  writes=(DS["d_row"],), eng="scalar", ch=8)
            k.dma(gnbc[:, :],
                  bass.AP(d_row.tensor, 0, [[16384, 1], [0, 128], [1, 1024]]),
                  reads=(DS["d_row"],), writes=(Sgnb,), eng="scalar", ch=8)
            def vynf(e):
                last = None
                for tp in range(64):
                    g = tp // 8
                    last = e.scalar_tensor_tensor(
                        ych[:, tp * 128:(tp + 1) * 128],
                        ych[:, tp * 128:(tp + 1) * 128],
                        c_normw[:, tp:tp + 1],
                        gnbc[:, g * 128:(g + 1) * 128],
                        ALU.mult, ALU.mult)
                return last
            k.op("vector", vynf, reads=(Sy, Sgnb, S_normw), writes=(Sy,))
            for s4 in range(4):
                k.dma(d_ynf[s4 * 16:(s4 + 1) * 16]
                      .transpose([1, 0, 2])[:, :, c * 128:(c + 1) * 128],
                      ych[:, s4 * 16 * 128:(s4 + 1) * 16 * 128],
                      reads=(Sy,), writes=(DS["d_ynf"],),
                      eng="gpsimd", ch=12 + s4 % 2)


def _s5(nc, k, DS, woutT, d_ynf, d_attn):
    with ExitStack() as es:
        E = es.enter_context
        synf = E(nc.sbuf_tensor("s5yn", [128, 64 * 1024], BF16))
        Syn = Slot("s5yn")
        swf = E(nc.sbuf_tensor("s5wf", [128, INTER], F32))
        Swf = Slot("s5wf")
        swb = [E(nc.sbuf_tensor(f"s5wb{i}", [128, INTER], BF16)) for i in range(2)]
        Swb = [Slot(f"s5wb{i}") for i in range(2)]
        sat = [E(nc.sbuf_tensor(f"s5at{i}", [128, 1024], BF16)) for i in range(2)]
        Sat = [Slot(f"s5at{i}") for i in range(2)]
        pss = [E(nc.psum_tensor(f"s5p{i}", [128, 512], F32)) for i in range(4)]
        Sps = [Slot(f"s5p{i}") for i in range(4)]

        for hf in range(2):
            for s4 in range(4):
                k.dma(synf[:, s4 * 16 * 1024:(s4 + 1) * 16 * 1024],
                      d_ynf[s4 * 16:(s4 + 1) * 16]
                      .transpose([1, 0, 2])[:, :, hf * 1024:(hf + 1) * 1024],
                      reads=(DS["d_ynf"],), writes=(Syn,), ch=4 + s4 % 2)
            for og in range(32):
                p = og % 2
                k.dma(swf[:, :], woutT[og], writes=(Swf,), ch=og % 4)
                k.op("vector",
                     lambda e, p=p: e.tensor_copy(swb[p][:, :], swf[:, :]),
                     reads=(Swf,), writes=(Swb[p],))
                def mm(e, p=p):
                    last = None
                    for kt in range(64):
                        for tc in range(2):
                            last = e.matmul(
                                pss[2 * p + tc][:, :],
                                swb[p][:, kt * 128:(kt + 1) * 128],
                                synf[:, kt * 1024 + tc * 512:
                                     kt * 1024 + (tc + 1) * 512],
                                start=(kt == 0), stop=(kt == 63))
                    return last
                k.op("tensor", mm, reads=(Swb[p], Syn),
                     writes=(Sps[2 * p], Sps[2 * p + 1]))
                def evac(e, p=p):
                    last = None
                    for tc in range(2):
                        last = e.copy(sat[p][:, tc * 512:(tc + 1) * 512],
                                      pss[2 * p + tc][:, :])
                    return last
                k.op("scalar", evac, reads=(Sps[2 * p], Sps[2 * p + 1]),
                     writes=(Sat[p],))
                k.dma(d_attn[og][:, hf * 1024:(hf + 1) * 1024], sat[p][:, :],
                      reads=(Sat[p],), writes=(DS["d_attn"],),
                      eng="scalar", ch=6 + og % 2)


def _s6(nc, k, DS, x_ext, d_attn, d_x2, ln3_fm, S_ln3, CN):
    c_eyeb, S_eyeb = CN["c_eyeb"]
    with ExitStack() as es:
        E = es.enter_context
        sfm = [E(nc.sbuf_tensor(f"s6f{i}", [128, 32 * 128], BF16)) for i in range(2)]
        Sf = [Slot(f"s6f{i}") for i in range(2)]
        atm = E(nc.sbuf_tensor("s6a", [128, DM], BF16))
        Sa = Slot("s6a")
        nrm = E(nc.sbuf_tensor("s6n", [128, DM], BF16))
        Sn = Slot("s6n")
        xr = E(nc.sbuf_tensor("s6x", [128, DM], F32))
        Sx = Slot("s6x")
        x2t = [E(nc.sbuf_tensor(f"s6x2{i}", [128, DM], BF16)) for i in range(2)]
        Sx2 = [Slot(f"s6x2{i}") for i in range(2)]
        ln3t = E(nc.sbuf_tensor("s6l3", [128, DM], BF16))
        Sl3 = Slot("s6l3")
        ssq, Ssq = None, None
        stat = E(nc.sbuf_tensor("s6st", [128, 8], F32))
        Sst = Slot("s6st")
        ps = [E(nc.psum_tensor(f"s6p{i}", [128, 128], BF16)) for i in range(4)]
        Sp = [Slot(f"s6p{i}") for i in range(4)]

        for tt in range(16):
            p = tt % 2
            for s2_ in range(2):
                k.dma(sfm[p][:, s2_ * 16 * 128:(s2_ + 1) * 16 * 128],
                      d_attn[s2_ * 16:(s2_ + 1) * 16]
                      .transpose([1, 0, 2])[:, :, tt * 128:(tt + 1) * 128],
                      reads=(DS["d_attn"],), writes=(Sf[p],), ch=s2_)
            k.dma(xr[:, :],
                  x_ext[PAD + LOOK + tt * 128:PAD + LOOK + (tt + 1) * 128, :],
                  writes=(Sx,), ch=2)
            for dt_ in range(32):
                b = dt_ % 4
                k.op("tensor",
                     lambda e, dt_=dt_, p=p, b=b: e.transpose(
                         ps[b][:, :], sfm[p][:, dt_ * 128:(dt_ + 1) * 128],
                         c_eyeb[:, :]),
                     reads=(Sf[p], S_eyeb), writes=(Sp[b],))
                k.op("vector",
                     lambda e, dt_=dt_, b=b: e.tensor_copy(
                         atm[:, dt_ * 128:(dt_ + 1) * 128], ps[b][:, :]),
                     reads=(Sp[b],), writes=(Sa,))
            _ln_stats(nc, k, atm, Sa, stat, Sst, nrm, Sn)
            k.op("scalar", lambda e: e.activation(nrm[:, :], atm[:, :], AF.Identity,
                                                  bias=stat[:, 7:8],
                                                  scale=stat[:, 6:7]),
                 reads=(Sa, Sst), writes=(Sn,))
            k.op("vector", lambda e, p=p: e.tensor_tensor(x2t[p][:, :], xr[:, :],
                                                          nrm[:, :], ALU.add),
                 reads=(Sx, Sn), writes=(Sx2[p],))
            k.dma(d_x2[tt], x2t[p][:, :], reads=(Sx2[p],),
                  writes=(DS["d_x2"],), eng="gpsimd", ch=6)
            _ln_stats(nc, k, x2t[p], Sx2[p], stat, Sst, nrm, Sn)
            k.op("scalar", lambda e, p=p: e.activation(ln3t[:, :], x2t[p][:, :],
                                                       AF.Identity, bias=stat[:, 7:8],
                                                       scale=stat[:, 6:7]),
                 reads=(Sx2[p], Sst), writes=(Sl3,))
            for kt in range(32):
                b = kt % 4
                k.op("tensor",
                     lambda e, kt=kt, b=b: e.transpose(
                         ps[b][:, :], ln3t[:, kt * 128:(kt + 1) * 128],
                         c_eyeb[:, :]),
                     reads=(Sl3, S_eyeb), writes=(Sp[b],))
                k.op("vector",
                     lambda e, tt=tt, kt=kt, b=b: e.tensor_copy(
                         ln3_fm[:, kt * NT + tt * 128: kt * NT + (tt + 1) * 128],
                         ps[b][:, :]),
                     reads=(Sp[b],), writes=(S_ln3,))


def _s7(nc, k, DS, gwT, bwT, lAT, lBT, ln3_fm, S_ln3, d_ffn, CN):
    c_eyeb, S_eyeb = CN["c_eyeb"]
    c_bb, S_bb = CN["c_bb"]
    with ExitStack() as es:
        E = es.enter_context
        sgwf = E(nc.sbuf_tensor("s7gwf", [128, 32 * 8], F32))
        sgw = E(nc.sbuf_tensor("s7gw", [128, 32 * 8], BF16))
        Sgw = Slot("s7gw")
        slat = E(nc.sbuf_tensor("s7la", [128, DM], BF16))
        Sla = Slot("s7la")
        slbt = E(nc.sbuf_tensor("s7lb", [128, DM], BF16))
        Slb = Slot("s7lb")
        lgt = E(nc.sbuf_tensor("s7lg", [128, 48], F32))
        Slg = Slot("s7lg")
        wfb = E(nc.sbuf_tensor("s7wfb", [128, 8], BF16))
        Swfb = Slot("s7wfb")
        wfTa = E(nc.sbuf_tensor("s7wt", [8, NT], BF16))
        Swt = Slot("s7wt")
        wrep = E(nc.sbuf_tensor("s7wr", [128, NT], BF16))
        Swr = Slot("s7wr")
        lowb = E(nc.sbuf_tensor("s7lo", [128, NT], BF16))
        Slo = Slot("s7lo")
        loww = E(nc.sbuf_tensor("s7lw", [128, NT], BF16))
        Slw = Slot("s7lw")
        sbwf = E(nc.sbuf_tensor("s7bwf", [128, DM], F32))
        Sbwf = Slot("s7bwf")
        sbwb = [E(nc.sbuf_tensor(f"s7bw{i}", [128, DM], BF16)) for i in range(2)]
        Sbw = [Slot(f"s7bw{i}") for i in range(2)]
        sff = [E(nc.sbuf_tensor(f"s7ff{i}", [128, NT], BF16)) for i in range(2)]
        Sff = [Slot(f"s7ff{i}") for i in range(2)]
        psL = E(nc.psum_tensor("s7pl", [128, 8], F32))
        SpL = Slot("s7pl")
        psW = E(nc.psum_tensor("s7pw", [8, 128], BF16))
        SpW = Slot("s7pw")
        pss = [E(nc.psum_tensor(f"s7p{i}", [128, 512], F32)) for i in range(4)]
        Sps = [Slot(f"s7p{i}") for i in range(4)]

        Sc = Slot("s7consts")
        k.dma(sgwf[:, :], gwT.rearrange("(t p) o -> p t o", p=128), writes=(Sc,))
        k.op("vector", lambda e: e.tensor_copy(sgw[:, :], sgwf[:, :]),
             reads=(Sc,), writes=(Sgw,))
        k.dma(sbwf[:, :], lAT, writes=(Sbwf,))
        k.op("vector", lambda e: e.tensor_copy(slat[:, :], sbwf[:, :]),
             reads=(Sbwf,), writes=(Sla,))
        k.dma(sbwf[:, :], lBT, writes=(Sbwf,))
        k.op("vector", lambda e: e.tensor_copy(slbt[:, :], sbwf[:, :]),
             reads=(Sbwf,), writes=(Slb,))

        for tt in range(16):
            def gmm(e, tt=tt):
                last = None
                for kt in range(32):
                    last = e.matmul(psL[:, :],
                                    ln3_fm[:, kt * NT + tt * 128:
                                           kt * NT + (tt + 1) * 128],
                                    sgw[:, kt * 8:(kt + 1) * 8],
                                    start=(kt == 0), stop=(kt == 31))
                return last
            k.op("tensor", gmm, reads=(S_ln3, Sgw), writes=(SpL,))
            for f in [
                lambda e: e.tensor_copy(lgt[:, 0:8], psL[:, :]),
                lambda e: e.reduce_max(lgt[:, 8:9], lgt[:, 0:8], axis=AX),
                lambda e: e.tensor_tensor(lgt[:, 16:24], lgt[:, 0:8],
                                          lgt[:, 8:9].broadcast_to([128, 8]),
                                          ALU.is_equal),
                lambda e: e.scalar_tensor_tensor(lgt[:, 24:32], lgt[:, 16:24],
                                                 -1e30, lgt[:, 0:8],
                                                 ALU.mult, ALU.add),
                lambda e: e.reduce_max(lgt[:, 9:10], lgt[:, 24:32], axis=AX),
                lambda e: e.tensor_tensor(lgt[:, 12:13], lgt[:, 8:9],
                                          lgt[:, 9:10], ALU.subtract),
            ]:
                k.op("vector", f, reads=(SpL, Slg), writes=(Slg,))
            k.op("scalar", lambda e: e.activation(lgt[:, 10:11], lgt[:, 12:13],
                                                  AF.Sigmoid),
                 reads=(Slg,), writes=(Slg,))
            for f in [
                lambda e: e.tensor_scalar(lgt[:, 11:12], lgt[:, 10:11], -1.0,
                                          1.0, ALU.mult, ALU.add),
                lambda e: e.tensor_tensor(lgt[:, 32:40], lgt[:, 24:32],
                                          lgt[:, 9:10].broadcast_to([128, 8]),
                                          ALU.is_equal),
                lambda e: e.tensor_scalar(lgt[:, 32:40], lgt[:, 32:40],
                                          lgt[:, 11:12], None, ALU.mult),
                lambda e: e.scalar_tensor_tensor(lgt[:, 40:48], lgt[:, 16:24],
                                                 lgt[:, 10:11], lgt[:, 32:40],
                                                 ALU.mult, ALU.add),
            ]:
                k.op("vector", f, reads=(Slg,), writes=(Slg,))
            k.op("vector", lambda e: e.tensor_scalar(wfb[:, :], lgt[:, 40:48],
                                                     2.0, None, ALU.mult),
                 reads=(Slg,), writes=(Swfb,))
            k.op("tensor", lambda e: e.transpose(psW[:, :], wfb[:, :],
                                                 c_eyeb[:, :]),
                 reads=(Swfb, S_eyeb), writes=(SpW,))
            k.op("vector",
                 lambda e, tt=tt: e.tensor_copy(wfTa[0:8, tt * 128:(tt + 1) * 128],
                                                psW[:, :]),
                 reads=(SpW,), writes=(Swt,))
        k.dma(wrep[:, :],
              bass.AP(wfTa[:, :].tensor, 0, [[NT, 8], [0, 16], [1, NT]]),
              reads=(Swt,), writes=(Swr,), eng="gpsimd", ch=5)

        def lowmm(e):
            last = None
            for kt in range(32):
                for tc in range(4):
                    last = e.matmul(pss[tc][:, :],
                                    slat[:, kt * 128:(kt + 1) * 128],
                                    ln3_fm[:, kt * NT + tc * 512:
                                           kt * NT + (tc + 1) * 512],
                                    start=(kt == 0), stop=(kt == 31))
            return last
        k.op("tensor", lowmm, reads=(Sla, S_ln3), writes=tuple(Sps))
        def lowev(e):
            last = None
            for tc in range(4):
                last = e.copy(lowb[:, tc * 512:(tc + 1) * 512], pss[tc][:, :])
            return last
        k.op("scalar", lowev, reads=tuple(Sps), writes=(Slo,))
        k.op("vector", lambda e: e.tensor_tensor(loww[:, :], lowb[:, :],
                                                 wrep[:, :], ALU.mult),
             reads=(Slo, Swr), writes=(Slw,))

        for og in range(32):
            p = og % 2
            k.dma(sbwf[:, :], bwT[og], writes=(Sbwf,), ch=og % 4)
            k.op("vector", lambda e, p=p: e.tensor_copy(sbwb[p][:, :],
                                                        sbwf[:, :]),
                 reads=(Sbwf,), writes=(Sbw[p],))
            def bmm(e, p=p, og=og):
                last = None
                for kt in range(32):
                    for tc in range(4):
                        last = e.matmul(pss[tc][:, :],
                                        sbwb[p][:, kt * 128:(kt + 1) * 128],
                                        ln3_fm[:, kt * NT + tc * 512:
                                               kt * NT + (tc + 1) * 512],
                                        start=(kt == 0), stop=False)
                for tc in range(4):
                    last = e.matmul(pss[tc][:, :],
                                    slbt[:, og * 128:(og + 1) * 128],
                                    loww[:, tc * 512:(tc + 1) * 512],
                                    start=False, stop=True)
                return last
            k.op("tensor", bmm, reads=(Sbw[p], S_ln3, Slb, Slw), writes=tuple(Sps))
            def fevac(e, p=p, og=og):
                last = None
                for tc in range(4):
                    last = e.activation(sff[p][:, tc * 512:(tc + 1) * 512],
                                        pss[tc][:, :], AF.Identity,
                                        bias=c_bb[:, og:og + 1])
                return last
            k.op("scalar", fevac, reads=tuple(Sps) + (S_bb,), writes=(Sff[p],))
            k.dma(d_ffn[og], sff[p][:, :], reads=(Sff[p],),
                  writes=(DS["d_ffn"],), eng="scalar", ch=6 + og % 2)


def _s8(nc, k, DS, d_x2, d_ffn, o_y, CN):
    c_eyeb, S_eyeb = CN["c_eyeb"]
    with ExitStack() as es:
        E = es.enter_context
        sfm = [E(nc.sbuf_tensor(f"s8f{i}", [128, 32 * 128], BF16)) for i in range(2)]
        Sf = [Slot(f"s8f{i}") for i in range(2)]
        ftm = E(nc.sbuf_tensor("s8a", [128, DM], BF16))
        Sa = Slot("s8a")
        nrm = E(nc.sbuf_tensor("s8n", [128, DM], BF16))
        Sn = Slot("s8n")
        x2r = [E(nc.sbuf_tensor(f"s8x{i}", [128, DM], BF16)) for i in range(2)]
        Sx = [Slot(f"s8x{i}") for i in range(2)]
        sout = [E(nc.sbuf_tensor(f"s8o{i}", [128, DM], F32)) for i in range(2)]
        So = [Slot(f"s8o{i}") for i in range(2)]
        stat = E(nc.sbuf_tensor("s8st", [128, 8], F32))
        Sst = Slot("s8st")
        ps = [E(nc.psum_tensor(f"s8p{i}", [128, 128], BF16)) for i in range(4)]
        Sp = [Slot(f"s8p{i}") for i in range(4)]

        for tt in range(16):
            p = tt % 2
            k.dma(sfm[p][:, 0:16 * 128],
                  d_ffn[0:16].transpose([1, 0, 2])[:, :, tt * 128:(tt + 1) * 128],
                  reads=(DS["d_ffn"],), writes=(Sf[p],), ch=0)
            k.dma(sfm[p][:, 16 * 128:32 * 128],
                  d_ffn[16:32].transpose([1, 0, 2])[:, :, tt * 128:(tt + 1) * 128],
                  reads=(DS["d_ffn"],), writes=(Sf[p],), ch=1)
            k.dma(x2r[p][:, :], d_x2[tt], reads=(DS["d_x2"],), writes=(Sx[p],), ch=2)
            for dt_ in range(32):
                b = dt_ % 4
                k.op("tensor",
                     lambda e, dt_=dt_, p=p, b=b: e.transpose(
                         ps[b][:, :], sfm[p][:, dt_ * 128:(dt_ + 1) * 128],
                         c_eyeb[:, :]),
                     reads=(Sf[p], S_eyeb), writes=(Sp[b],))
                k.op("vector",
                     lambda e, dt_=dt_, b=b: e.tensor_copy(
                         ftm[:, dt_ * 128:(dt_ + 1) * 128], ps[b][:, :]),
                     reads=(Sp[b],), writes=(Sa,))
            _ln_stats(nc, k, ftm, Sa, stat, Sst, nrm, Sn)
            k.op("scalar", lambda e: e.activation(nrm[:, :], ftm[:, :], AF.Identity,
                                                  bias=stat[:, 7:8],
                                                  scale=stat[:, 6:7]),
                 reads=(Sa, Sst), writes=(Sn,))
            k.op("vector", lambda e, p=p: e.tensor_tensor(sout[p][:, :],
                                                          x2r[p][:, :],
                                                          nrm[:, :], ALU.add),
                 reads=(Sx[p], Sn), writes=(So[p],))
            k.dma(o_y[tt * 128:(tt + 1) * 128, :], sout[p][:, :],
                  reads=(So[p],), writes=(Slot("oy"),), eng="gpsimd",
                  ch=6 + tt % 2)


def kernel(**inputs):
    x = np.ascontiguousarray(inputs["L0_attn_hidden_states"], dtype=np.float32)
    W = np.ascontiguousarray(inputs["L0_attn_in_proj_w"], dtype=np.float32)
    cw = np.ascontiguousarray(inputs["L0_attn_conv_w"], dtype=np.float32)
    cb = np.ascontiguousarray(inputs["L0_attn_conv_b"], dtype=np.float32)
    dtb = np.ascontiguousarray(inputs["L0_attn_dt_bias"], dtype=np.float32)
    Dv = np.ascontiguousarray(inputs["L0_attn_D"], dtype=np.float32)
    nw = np.ascontiguousarray(inputs["L0_attn_norm_w"], dtype=np.float32)
    Wo = np.ascontiguousarray(inputs["L0_attn_out_proj_w"], dtype=np.float32)
    gw = np.ascontiguousarray(inputs["L0_ffn_gate_w"], dtype=np.float32)
    bw = np.ascontiguousarray(inputs["L0_ffn_base_w"], dtype=np.float32)
    bb = np.ascontiguousarray(inputs["L0_ffn_base_b"], dtype=np.float32)
    lA = np.ascontiguousarray(inputs["L0_ffn_lora_A"], dtype=np.float32)
    lB = np.ascontiguousarray(inputs["L0_ffn_lora_B"], dtype=np.float32)

    wT = np.ascontiguousarray(
        np.pad(W, ((0, 0), (0, 0))).reshape(145, 128, 32, 128)
        .transpose(0, 3, 2, 1).reshape(145, 128, DM))
    cw_cols = np.ascontiguousarray(
        cw[:, 0, :].reshape(80, 128, 4).transpose(1, 0, 2).reshape(128, 320))
    cb_row = np.ascontiguousarray(cb.reshape(1, CONV))
    dtb_col = np.ascontiguousarray(dtb.reshape(128, 1))
    d_pair = np.ascontiguousarray(
        np.repeat(Dv.reshape(64, 2, 1), 64, axis=2).transpose(1, 2, 0)
        .reshape(128, 64))
    normw_cols = np.ascontiguousarray(nw.reshape(64, 128).T)
    woutT = np.ascontiguousarray(
        Wo.reshape(32, 128, 64, 128).transpose(0, 3, 2, 1)
        .reshape(32, 128, INTER))
    gwT = np.ascontiguousarray(gw.T)
    bwT = np.ascontiguousarray(
        bw.reshape(32, 128, 32, 128).transpose(0, 3, 2, 1)
        .reshape(32, 128, DM))
    bb_cols = np.ascontiguousarray(bb.reshape(32, 128).T)
    lAT = np.ascontiguousarray(
        lA.reshape(128, 32, 128).transpose(2, 1, 0).reshape(128, DM))
    lBT = np.ascontiguousarray(lB.transpose(0, 2, 1).reshape(128, DM))

    in_maps = []
    for c in range(8):
        b, q = c // 4, c % 4
        start = q * 2048
        xs = x[b, start:start + 2048]
        look = x[b, start - 3:start] if q > 0 else np.zeros((3, DM), np.float32)
        x_ext = np.concatenate(
            [np.zeros((PAD, DM), np.float32), look, xs], axis=0)
        m = np.zeros((128, 4), np.float32)
        m[:, :q] = 1.0
        im = 1.0 - m
        in_maps.append({
            "x_ext": np.ascontiguousarray(x_ext),
            "wT": wT, "cw_cols": cw_cols, "cb_row": cb_row,
            "dtb_col": dtb_col, "d_pair": d_pair, "normw_cols": normw_cols,
            "woutT": woutT, "gwT": gwT, "bwT": bwT, "bb_cols": bb_cols,
            "lAT": lAT, "lBT": lBT,
            "masks": np.ascontiguousarray(m),
            "invmasks": np.ascontiguousarray(im),
        })

    nc = build_nc(debug=DEBUG)
    res = run_bass_kernel_spmd(nc, in_maps, core_ids=list(range(8)))
    out = np.empty((2, 8192, DM), dtype=np.float32)
    for c in range(8):
        b, q = c // 4, c % 4
        out[b, q * 2048:(q + 1) * 2048] = res.results[c]["o_y"]
    if DEBUG:
        kernel.last_results = res
    return out


# version 08
# speedup vs baseline: 1.3275x; 1.0322x over previous
import numpy as np
import ml_dtypes
from contextlib import ExitStack

import concourse.bass as bass
import concourse.mybir as mybir
from concourse.bass_utils import run_bass_kernel_spmd

F32 = mybir.dt.float32
BF16 = mybir.dt.bfloat16
AF = mybir.ActivationFunctionType
ALU = mybir.AluOpType
AX = mybir.AxisListType.X

DM = 4096
NH = 128
HD = 64
SS = 128
NG = 8
INTER = NH * HD
CONV = INTER + 2 * NG * SS
PROJ = INTER + CONV + NH
NT = 2048
PAD, LOOK = 125, 3
TEXT = PAD + LOOK + NT
NCH, CL = 16, 128
EPS = 1e-5

DEBUG = False
STAGES = None


def on(name):
    return STAGES is None or name in STAGES


class Slot:
    __slots__ = ("name", "w", "r")
    def __init__(self, name):
        self.name = name
        self.w = None
        self.r = []


SERIALIZE = False
SERIAL_STAGES = set()
NDCH = 16


class K:
    ENGS = ("sync", "tensor", "vector", "scalar", "gpsimd")

    def __init__(self, nc):
        self.nc = nc
        self.items = {e: [] for e in self.ENGS}
        self.sems = {e: (nc.alloc_semaphore("sem_" + e), [0]) for e in self.ENGS}
        self.dch = [(nc.alloc_semaphore(f"dch{i}"), [0]) for i in range(NDCH)]
        self.drr = 0
        self.seen = {e: {} for e in self.ENGS}
        self.lastop = None
        self.barrier = []

    def op(self, eng, emit, reads=(), writes=(), dma=False, ch=None):
        waits = []
        def need(tok, war=False):
            if tok is None:
                return
            sem, n, src = tok
            if war and src == eng:
                return
            if self.seen[eng].get(sem.name, 0) >= n:
                return
            self.seen[eng][sem.name] = n
            waits.append((sem, n))
        for tok in self.barrier:
            need(tok)
        if SERIALIZE or getattr(self, "cur_stage", None) in SERIAL_STAGES:
            need(self.lastop)
        for s in reads:
            need(s.w)
        for s in writes:
            need(s.w)
            for t in s.r:
                need(t, war=True)
        if dma:
            if ch is None:
                ch = self.drr
                self.drr = (self.drr + 1) % NDCH
            sem, cnt = self.dch[ch]
            if cnt[0] > 0:
                need((sem, cnt[0], "dma"))
            cnt[0] += 16
            tok = (sem, cnt[0], "dma")
            ik = 16
        else:
            sem, cnt = self.sems[eng]
            cnt[0] += 1
            tok = (sem, cnt[0], eng)
            ik = 1
        self.lastop = tok
        self.items[eng].append((waits, emit, sem, ik))
        for s in reads:
            s.r.append(tok)
        for s in writes:
            s.w = tok
            s.r = []

    def flush(self, name):
        items = self.items
        self.items = {e: [] for e in self.ENGS}
        self.barrier = (
            [(self.sems[e][0], self.sems[e][1][0], "barrier")
             for e in self.ENGS if self.sems[e][1][0] > 0] +
            [(sem, cnt[0], "barrier") for (sem, cnt) in self.dch
             if cnt[0] > 0])
        if not any(items[e] for e in self.ENGS):
            return
        with self.nc.Block(name) as block:
            def mk(eng_items):
                def body(e):
                    for waits, emit, sem_, kk in eng_items:
                        for (s, n) in waits:
                            e.wait_ge(s, n)
                        instr = emit(e)
                        instr.then_inc(sem_, kk)
                return body
            for eng in self.ENGS:
                if items[eng]:
                    getattr(block, eng)(mk(items[eng]))

    def finale(self):
        chans = [(sem, cnt[0]) for (sem, cnt) in self.dch if cnt[0] > 0]
        with self.nc.Block("finale") as block:
            @block.sync
            def _(e):
                for sem, n in chans:
                    e.wait_ge(sem, n)

    def dma(self, out_ap, in_ap, reads=(), writes=(), eng="sync", ch=None):
        self.op(eng, lambda e: e.dma_start(out=out_ap, in_=in_ap),
                reads=reads, writes=writes, dma=True, ch=ch)


def build_nc(debug=False):
    nc = bass.Bass("TRN2", target_bir_lowering=False)

    x_ext = nc.dram_tensor("x_ext", [TEXT, DM], F32, kind="ExternalInput").ap()
    wT = nc.dram_tensor("wT", [145, 128, DM], F32, kind="ExternalInput").ap()
    cw_cols = nc.dram_tensor("cw_cols", [128, 80 * 4], F32, kind="ExternalInput").ap()
    cb_row = nc.dram_tensor("cb_row", [1, CONV], F32, kind="ExternalInput").ap()
    dtb_col = nc.dram_tensor("dtb_col", [128, 1], F32, kind="ExternalInput").ap()
    d_pair = nc.dram_tensor("d_pair", [128, 64], F32, kind="ExternalInput").ap()
    normw_cols = nc.dram_tensor("normw_cols", [128, 64], F32, kind="ExternalInput").ap()
    woutT = nc.dram_tensor("woutT", [32, 128, INTER], F32,
                           kind="ExternalInput").ap()
    gwT = nc.dram_tensor("gwT", [DM, 8], F32, kind="ExternalInput").ap()
    bwT = nc.dram_tensor("bwT", [32, 128, DM], F32, kind="ExternalInput").ap()
    bb_cols = nc.dram_tensor("bb_cols", [128, 32], F32, kind="ExternalInput").ap()
    lAT = nc.dram_tensor("lAT", [128, DM], F32, kind="ExternalInput").ap()
    lBT = nc.dram_tensor("lBT", [128, DM], F32, kind="ExternalInput").ap()
    masks = nc.dram_tensor("masks", [128, 4], F32, kind="ExternalInput").ap()
    invmasks = nc.dram_tensor("invmasks", [128, 4], F32, kind="ExternalInput").ap()
    o_y = nc.dram_tensor("o_y", [NT, DM], F32, kind="ExternalOutput").ap()

    d_proj = nc.dram_tensor("d_proj", [144, 128, TEXT], BF16, kind="Internal").ap()
    d_dt = nc.dram_tensor("d_dt", [128, TEXT], F32, kind="Internal").ap()
    d_xbca = nc.dram_tensor("d_xbca", [80, 128, NT], BF16, kind="Internal").ap()
    d_xdt = nc.dram_tensor("d_xdt", [NCH, 128, INTER], BF16, kind="Internal").ap()
    d_prev = nc.dram_tensor("d_prev", [NCH, 128, INTER], BF16, kind="Internal").ap()
    d_ynf = nc.dram_tensor("d_ynf", [64, 128, NT], BF16, kind="Internal").ap()
    d_attn = nc.dram_tensor("d_attn", [32, 128, NT], BF16, kind="Internal").ap()
    d_x2 = nc.dram_tensor("d_x2", [16, 128, DM], BF16, kind="Internal").ap()
    d_ffn = nc.dram_tensor("d_ffn", [32, 128, NT], BF16, kind="Internal").ap()
    d_row = nc.dram_tensor("d_row", [1, 16384], F32, kind="Internal").ap()
    d_rowb = nc.dram_tensor("d_rowb", [1, 16384], BF16, kind="Internal").ap()
    cc_in_t = nc.dram_tensor("cc_in", [128, 8256], BF16, kind="Internal")
    cc_out_t = nc.dram_tensor("cc_out", [512, 8256], BF16, kind="Internal")

    eye_f = nc.inline_tensor(np.eye(128, dtype=np.float32), name="eye_f").ap()
    eye_b = nc.inline_tensor(np.eye(128, dtype=ml_dtypes.bfloat16), name="eye_b").ap()
    acol_c = nc.inline_tensor(-np.arange(1, 129, dtype=np.float32).reshape(128, 1),
                              name="acol").ap()
    ones_b = nc.inline_tensor(np.ones((1, 128), dtype=ml_dtypes.bfloat16),
                              name="ones_b").ap()
    onescol_b = nc.inline_tensor(np.ones((128, 1), dtype=ml_dtypes.bfloat16),
                                 name="onescol_b").ap()

    k = K(nc)
    DS = {n: Slot(n) for n in ["d_proj", "d_dt", "d_xbca", "d_xdt", "d_prev",
                               "d_ynf", "d_attn", "d_x2", "d_ffn", "d_row",
                               "d_rowb", "cc"]}

    with ExitStack() as es:
        E = es.enter_context

        def sb(name, shape, dt_):
            t = E(nc.sbuf_tensor(name, shape, dt_))
            return t, Slot(name)

        c_eyef, S_eyef = sb("c_eyef", [128, 128], F32)
        c_eyeb, S_eyeb = sb("c_eyeb", [128, 128], BF16)
        c_acol, S_acol = sb("c_acol", [128, 1], F32)
        c_dtb, S_dtb = sb("c_dtb", [128, 1], F32)
        c_cw, S_cw = sb("c_cw", [128, 320], F32)
        c_onesrow, S_onesrow = sb("c_onesrow", [1, 128], BF16)
        c_onescol, S_onescol = sb("c_onescol", [128, 1], BF16)
        c_dpair, S_dpair = sb("c_dpair", [128, 64], F32)
        c_normw, S_normw = sb("c_normw", [128, 64], F32)
        c_bb, S_bb = sb("c_bb", [128, 32], F32)
        c_masks, S_masks = sb("c_masks", [128, 4], F32)
        c_invm, S_invm = sb("c_invm", [128, 4], F32)
        cdbc, S_cdbc = sb("cdbc", [128, 128], F32)

        CN = dict(c_eyef=(c_eyef, S_eyef), c_eyeb=(c_eyeb, S_eyeb),
                  c_acol=(c_acol, S_acol), c_dtb=(c_dtb, S_dtb),
                  c_cw=(c_cw, S_cw), c_onesrow=(c_onesrow, S_onesrow),
                  c_onescol=(c_onescol, S_onescol), c_dpair=(c_dpair, S_dpair),
                  c_normw=(c_normw, S_normw), c_bb=(c_bb, S_bb),
                  c_masks=(c_masks, S_masks), c_invm=(c_invm, S_invm),
                  cdbc=(cdbc, S_cdbc))

        for dst, src, sl in [(c_eyef, eye_f, S_eyef), (c_eyeb, eye_b, S_eyeb),
                             (c_acol, acol_c, S_acol), (c_dtb, dtb_col, S_dtb),
                             (c_cw, cw_cols, S_cw),
                             (c_dpair, d_pair, S_dpair),
                             (c_normw, normw_cols, S_normw),
                             (c_bb, bb_cols, S_bb), (c_masks, masks, S_masks),
                             (c_invm, invmasks, S_invm),
                             (c_onesrow, ones_b, S_onesrow),
                             (c_onescol, onescol_b, S_onescol)]:
            k.dma(dst[:, :], src, writes=(sl,))
        k.flush("consts")

        with ExitStack() as es1:
            ln1_fm = es1.enter_context(
                nc.sbuf_tensor("ln1_fm", [128, 32 * TEXT], BF16))
            S_ln1 = Slot("ln1_fm")
            k.cur_stage = "s0"
            if on("s0"):
                _s0(nc, k, x_ext, ln1_fm, S_ln1, CN)
                k.flush("s0")
            k.cur_stage = "s1"
            if on("s1"):
                _s1(nc, k, wT, ln1_fm, S_ln1, d_proj, d_dt, DS)
                k.flush("s1")

        k.cur_stage = "s2a"
        if on("s2a"):
            _s2a(nc, k, DS, d_proj, d_xbca, cb_row, CN)
            k.flush("s2a")

        with ExitStack() as es2:
            E2 = es2.enter_context
            def sb2(name, shape, dt_):
                t = E2(nc.sbuf_tensor(name, shape, dt_))
                return t, Slot(name)
            da_all, S_daall = sb2("da_all", [128, NT], F32)
            expcs_all, S_ecs = sb2("expcs_all", [128, NT], F32)
            cumdec, S_cumdec = sb2("cumdec", [128, NCH], F32)
            big, S_big = sb2("big", [128, INTER], F32)
            csum, S_csum = sb2("csum", [128, 4], F32)
            P = dict(da_all=(da_all, S_daall),
                     expcs_all=(expcs_all, S_ecs),
                     cumdec=(cumdec, S_cumdec), big=(big, S_big),
                     csum=(csum, S_csum))
            k.op("vector", lambda e: e.memset(big[:, :], 0.0), writes=(S_big,))
            k.op("vector", lambda e: e.memset(csum[:, :], 0.0), writes=(S_csum,))

            k.cur_stage = "s2b"
            if on("s2b"):
                _s2b(nc, k, DS, d_xbca, d_dt, d_xdt, d_prev, d_row, CN, P)
                k.flush("s2b")
            k.cur_stage = "s3"
            if on("s3"):
                _s3(nc, k, DS, cc_in_t, cc_out_t, d_row, CN, P)
                k.flush("s3")
            k.cur_stage = "s4"
            if on("s4"):
                _s4(nc, k, DS, d_proj, d_xbca, d_xdt, d_prev, d_ynf,
                    d_row, d_rowb, CN, P)
                k.flush("s4")

        k.cur_stage = "s5"
        if on("s5"):
            _s5(nc, k, DS, woutT, d_ynf, d_attn)
            k.flush("s5")

        with ExitStack() as es3:
            ln3_fm = es3.enter_context(
                nc.sbuf_tensor("ln3_fm", [128, 32 * NT], BF16))
            S_ln3 = Slot("ln3_fm")
            k.cur_stage = "s6"
            if on("s6"):
                _s6(nc, k, DS, x_ext, d_attn, d_x2, ln3_fm, S_ln3, CN)
                k.flush("s6")
            k.cur_stage = "s7"
            if on("s7"):
                _s7(nc, k, DS, gwT, bwT, lAT, lBT, ln3_fm, S_ln3, d_ffn, CN)
                k.flush("s7")

        k.cur_stage = "s8"
        if on("s8"):
            _s8(nc, k, DS, d_x2, d_ffn, o_y, CN)
        k.flush("s8")

        if debug:
            dbg = [("dbg_proj", d_proj, [144, 128, TEXT], BF16, "s1"),
                   ("dbg_dt", d_dt, [128, TEXT], F32, "s1"),
                   ("dbg_xbca", d_xbca, [80, 128, NT], BF16, "s2a"),
                   ("dbg_xdt", d_xdt, [NCH, 128, INTER], BF16, "s2b"),
                   ("dbg_prev", d_prev, [NCH, 128, INTER], BF16, "s2b"),
                   ("dbg_ynf", d_ynf, [64, 128, NT], BF16, "s4"),
                   ("dbg_attn", d_attn, [32, 128, NT], BF16, "s5"),
                   ("dbg_x2", d_x2, [16, 128, DM], BF16, "s6"),
                   ("dbg_ffn", d_ffn, [32, 128, NT], BF16, "s7")]
            for nm, src, shape, dt_, st in dbg:
                if not on(st):
                    continue
                o = nc.dram_tensor(nm, shape, dt_, kind="ExternalOutput").ap()
                k.dma(o, src, reads=tuple(DS.values()))
            k.flush("dbg")
        k.finale()

    return nc


def _ln_stats(nc, k, xin, S_xin, stat, S_stat, ssq, S_ssq):
    k.op("vector", lambda e: e.reduce_sum(stat[:, 0:1], xin[:, :], axis=AX),
         reads=(S_xin,), writes=(S_stat,))
    k.op("scalar", lambda e: e.activation(ssq[:, :], xin[:, :], AF.Square),
         reads=(S_xin,), writes=(S_ssq,))
    k.op("vector", lambda e: e.reduce_sum(stat[:, 4:5], ssq[:, :], axis=AX),
         reads=(S_ssq,), writes=(S_stat,))
    for f in [
        lambda e: e.tensor_scalar(stat[:, 1:2], stat[:, 0:1], 1.0 / DM, None,
                                  ALU.mult),
        lambda e: e.tensor_scalar(stat[:, 2:3], stat[:, 4:5], 1.0 / DM, None,
                                  ALU.mult),
        lambda e: e.tensor_tensor(stat[:, 3:4], stat[:, 1:2], stat[:, 1:2],
                                  ALU.mult),
        lambda e: e.tensor_tensor(stat[:, 2:3], stat[:, 2:3], stat[:, 3:4],
                                  ALU.subtract),
        lambda e: e.tensor_scalar(stat[:, 2:3], stat[:, 2:3], EPS, None,
                                  ALU.add),
    ]:
        k.op("vector", f, reads=(S_stat,), writes=(S_stat,))
    k.op("scalar", lambda e: e.activation(stat[:, 5:6], stat[:, 2:3], AF.Ln),
         reads=(S_stat,), writes=(S_stat,))
    k.op("scalar", lambda e: e.activation(stat[:, 6:7], stat[:, 5:6], AF.Exp,
                                          scale=-0.5),
         reads=(S_stat,), writes=(S_stat,))
    k.op("vector", lambda e: e.tensor_tensor(stat[:, 7:8], stat[:, 1:2],
                                             stat[:, 6:7], ALU.mult),
         reads=(S_stat,), writes=(S_stat,))
    k.op("vector", lambda e: e.tensor_scalar(stat[:, 7:8], stat[:, 7:8], -1.0,
                                             None, ALU.mult),
         reads=(S_stat,), writes=(S_stat,))


def _s0(nc, k, x_ext, ln1_fm, S_ln1, CN):
    c_eyeb, S_eyeb = CN["c_eyeb"]
    NTile = TEXT // 128
    with ExitStack() as es:
        E = es.enter_context
        sx = [E(nc.sbuf_tensor(f"s0x{i}", [128, DM], F32)) for i in range(2)]
        Sx = [Slot(f"s0x{i}") for i in range(2)]
        sln = [E(nc.sbuf_tensor(f"s0l{i}", [128, DM], BF16)) for i in range(2)]
        Sl = [Slot(f"s0l{i}") for i in range(2)]
        stat = E(nc.sbuf_tensor("s0stat", [128, 8], F32))
        Sst = Slot("s0stat")
        ps = [E(nc.psum_tensor(f"s0p{i}", [128, 128], BF16)) for i in range(4)]
        Sp = [Slot(f"s0p{i}") for i in range(4)]
        for i in range(NTile):
            p = i % 2
            k.dma(sx[p][:, :], x_ext[i * 128:(i + 1) * 128, :], writes=(Sx[p],))
            _ln_stats(nc, k, sx[p], Sx[p], stat, Sst, sln[p], Sl[p])
            k.op("scalar",
                 lambda e, p=p: e.activation(sln[p][:, :], sx[p][:, :], AF.Identity,
                                             bias=stat[:, 7:8], scale=stat[:, 6:7]),
                 reads=(Sx[p], Sst), writes=(Sl[p],))
            for kt in range(32):
                b = kt % 4
                k.op("tensor",
                     lambda e, p=p, kt=kt, b=b: e.transpose(
                         ps[b][:, :], sln[p][:, kt * 128:(kt + 1) * 128],
                         c_eyeb[:, :]),
                     reads=(Sl[p], S_eyeb), writes=(Sp[b],))
                k.op("vector",
                     lambda e, i=i, kt=kt, b=b: e.tensor_copy(
                         ln1_fm[:, kt * TEXT + i * 128: kt * TEXT + (i + 1) * 128],
                         ps[b][:, :]),
                     reads=(Sp[b],), writes=(S_ln1,))


def _s1(nc, k, wT, ln1_fm, S_ln1, d_proj, d_dt, DS):
    NTC = [512, 512, 512, 512, 128]
    with ExitStack() as es:
        E = es.enter_context
        swf = E(nc.sbuf_tensor("s1wf", [128, DM], F32))
        Swf = Slot("s1wf")
        swb = [E(nc.sbuf_tensor(f"s1wb{i}", [128, DM], BF16)) for i in range(2)]
        Swb = [Slot(f"s1wb{i}") for i in range(2)]
        spj = [E(nc.sbuf_tensor(f"s1pj{i}", [128, TEXT], BF16)) for i in range(2)]
        Spj = [Slot(f"s1pj{i}") for i in range(2)]
        sdt = E(nc.sbuf_tensor("s1dt", [128, TEXT], F32))
        Sdt = Slot("s1dt")
        pss = [E(nc.psum_tensor(f"s1p{i}", [128, 512], F32)) for i in range(5)]
        Sps = [Slot(f"s1p{i}") for i in range(5)]

        for ot in range(145):
            p = ot % 2
            k.dma(swf[:, :], wT[ot], writes=(Swf,), ch=ot % 4)
            k.op("vector", lambda e, p=p: e.tensor_copy(swb[p][:, :], swf[:, :]),
                 reads=(Swf,), writes=(Swb[p],))

            def mm(e, p=p):
                last = None
                for kt in range(32):
                    for tc, n in enumerate(NTC):
                        o0 = sum(NTC[:tc])
                        last = e.matmul(
                            pss[tc][:, 0:n],
                            swb[p][:, kt * 128:(kt + 1) * 128],
                            ln1_fm[:, kt * TEXT + o0: kt * TEXT + o0 + n],
                            start=(kt == 0), stop=(kt == 31))
                return last
            k.op("tensor", mm, reads=(Swb[p], S_ln1), writes=tuple(Sps))

            if ot < 144:
                def evac(e, p=p):
                    last = None
                    for tc, n in enumerate(NTC):
                        o0 = sum(NTC[:tc])
                        last = e.copy(spj[p][:, o0:o0 + n], pss[tc][:, 0:n])
                    return last
                k.op("scalar", evac, reads=tuple(Sps), writes=(Spj[p],))
                k.dma(d_proj[ot], spj[p][:, :], reads=(Spj[p],),
                      writes=(DS["d_proj"],), eng="scalar", ch=6 + ot % 2)
            else:
                def evacdt(e):
                    last = None
                    for tc, n in enumerate(NTC):
                        o0 = sum(NTC[:tc])
                        last = e.copy(sdt[:, o0:o0 + n], pss[tc][:, 0:n])
                    return last
                k.op("scalar", evacdt, reads=tuple(Sps), writes=(Sdt,))
                k.dma(d_dt, sdt[:, :], reads=(Sdt,), writes=(DS["d_dt"],),
                      eng="scalar", ch=6)


def _s2a(nc, k, DS, d_proj, d_xbca, cb_row, CN):
    c_cw, S_cw = CN["c_cw"]
    c_onesrow, S_onesrow = CN["c_onesrow"]
    with ExitStack() as es:
        E = es.enter_context
        cdg = E(nc.sbuf_tensor("cdg", [128, 40 * 128], BF16))
        Scdg = Slot("cdg")
        sxc = [E(nc.sbuf_tensor(f"s2ax{i}", [128, 10 * TEXT], BF16)) for i in range(2)]
        Sxc = [Slot(f"s2ax{i}") for i in range(2)]
        sxo = [E(nc.sbuf_tensor(f"s2ao{i}", [128, 10 * NT], BF16)) for i in range(2)]
        Sxo = [Slot(f"s2ao{i}") for i in range(2)]
        cbf = E(nc.sbuf_tensor("s2acbf", [1, 1280], F32))
        Scbf = Slot("s2acbf")
        cbb = E(nc.sbuf_tensor("s2acbb", [1, 1280], BF16))
        Scbb = Slot("s2acbb")
        psC = [E(nc.psum_tensor(f"s2ap{i}", [128, 512], F32)) for i in range(8)]
        SpC = [Slot(f"s2ap{i}") for i in range(8)]

        ones4 = bass.AP(c_onesrow[:, :].tensor, 0, [[128, 1], [0, 4], [1, 128]])
        for og in range(8):
            p = og % 2
            k.dma(cbf[:, :], cb_row[0:1, og * 1280:(og + 1) * 1280],
                  writes=(Scbf,))
            k.op("vector", lambda e: e.tensor_copy(cbb[:, :], cbf[:, :]),
                 reads=(Scbf,), writes=(Scbb,))
            for i in range(10):
                ot = og * 10 + i
                k.dma(sxc[p][:, i * TEXT:(i + 1) * TEXT], d_proj[64 + ot],
                      reads=(DS["d_proj"],), writes=(Sxc[p],), ch=i % 4)
            def build_diag(e, og=og):
                last = None
                for i in range(10):
                    for j in range(4):
                        idx = (og * 10 + i) * 4 + j
                        last = e.affine_select(
                            cdg[:, (i * 4 + j) * 128:(i * 4 + j + 1) * 128],
                            c_cw[:, idx:idx + 1].broadcast_to([128, 128]),
                            [[-1, 128]], ALU.is_equal, 0.0,
                            base=0, channel_multiplier=1)
                return last
            k.op("gpsimd", build_diag, reads=(S_cw,), writes=(Scdg,))

            for i in range(10):
                ot = og * 10 + i
                pb = (i % 2) * 4
                def convmm(e, i=i, ot=ot, pb=pb, p=p):
                    last = None
                    for tc in range(4):
                        t0 = PAD + tc * 512
                        for j in range(4):
                            last = e.matmul(
                                psC[pb + tc][:, :],
                                cdg[:, (i * 4 + j) * 128:(i * 4 + j + 1) * 128],
                                sxc[p][:, i * TEXT + t0 + j - 3 + 3:
                                       i * TEXT + t0 + j + 512],
                                start=(j == 0), stop=False)
                        last = e.matmul(
                            psC[pb + tc][:, :],
                            cbb[0:1, i * 128:(i + 1) * 128],
                            ones4,
                            start=False, stop=True)
                    return last
                k.op("tensor", convmm, reads=(Scdg, Sxc[p], S_onesrow, Scbb),
                     writes=tuple(SpC[pb + tc] for tc in range(4)))
                def silu(e, i=i, pb=pb, p=p):
                    last = None
                    for tc in range(4):
                        last = e.activation(
                            sxo[p][:, i * NT + tc * 512:i * NT + (tc + 1) * 512],
                            psC[pb + tc][:, :], AF.Silu)
                    return last
                k.op("scalar", silu, reads=tuple(SpC[pb + tc] for tc in range(4)),
                     writes=(Sxo[p],))
            for i in range(10):
                ot = og * 10 + i
                k.dma(d_xbca[ot], sxo[p][:, i * NT:(i + 1) * NT],
                      reads=(Sxo[p],), writes=(DS["d_xbca"],),
                      eng="scalar", ch=6 + i % 2)


def _s2b(nc, k, DS, d_xbca, d_dt, d_xdt, d_prev, d_row, CN, P):
    c_dtb, S_dtb = CN["c_dtb"]
    c_acol, S_acol = CN["c_acol"]
    c_eyef, S_eyef = CN["c_eyef"]
    c_eyeb, S_eyeb = CN["c_eyeb"]
    cdbc, S_cdbc = CN["cdbc"]
    da_all, S_daall = P["da_all"]
    expcs_all, S_ecs = P["expcs_all"]
    cumdec, S_cumdec = P["cumdec"]
    big, S_big = P["big"]
    csum, S_csum = P["csum"]
    with ExitStack() as es:
        E = es.enter_context
        sxa = [E(nc.sbuf_tensor(f"s2bx{i}", [128, 72 * 128], BF16))
               for i in range(2)]
        Sxa = [Slot(f"s2bx{i}") for i in range(2)]
        sdt = E(nc.sbuf_tensor("s2bdt", [128, 128], F32))
        Sdt = Slot("s2bdt")
        dtw = E(nc.sbuf_tensor("s2bw", [128, 7 * 128], F32))
        Sdtw = Slot("s2bw")
        zz = E(nc.sbuf_tensor("s2bz", [128, 128], F32))
        Szz = Slot("s2bz")
        xdt_sb = E(nc.sbuf_tensor("s2bxd", [128, INTER], BF16))
        Sxdt = Slot("s2bxd")
        xdtw_sb = E(nc.sbuf_tensor("s2bxw", [128, INTER], BF16))
        Sxdtw = Slot("s2bxw")
        btm = E(nc.sbuf_tensor("s2bbt", [128, 8 * 128], BF16))
        Sbtm = Slot("s2bbt")
        st_all = E(nc.sbuf_tensor("s2bst", [128, INTER], BF16))
        S_stall = Slot("s2bst")
        prevb = E(nc.sbuf_tensor("s2bpv", [128, INTER], BF16))
        S_prevb = Slot("s2bpv")
        psT = [E(nc.psum_tensor(f"s2bt{i}", [128, 128], BF16)) for i in range(2)]
        SpT = [Slot(f"s2bt{i}") for i in range(2)]
        psF = E(nc.psum_tensor("s2bf", [128, 128], F32))
        SpF = Slot("s2bf")
        psS = [E(nc.psum_tensor(f"s2bs{i}", [128, 1024], F32)) for i in range(2)]
        SpS = [Slot(f"s2bs{i}") for i in range(2)]

        k.op("vector", lambda e: e.memset(zz[:, :], 0.0), writes=(Szz,))

        DT, DA, CS, DE, DTT, DET, WC = range(7)
        def col(i):
            return dtw[:, i * 128:(i + 1) * 128]

        for c in range(NCH):
            pb4 = c % 2
            for s4 in range(4):
                k.dma(sxa[pb4][:, s4 * 18 * 128:(s4 + 1) * 18 * 128],
                      d_xbca[s4 * 18:(s4 + 1) * 18]
                      .transpose([1, 0, 2])[:, :, c * 128:(c + 1) * 128],
                      reads=(DS["d_xbca"],), writes=(Sxa[pb4],), ch=s4)
            k.dma(sdt[:, :], d_dt[:, 128 + c * 128:128 + (c + 1) * 128],
                  reads=(DS["d_dt"],), writes=(Sdt,), ch=5)

            k.op("scalar", lambda e: e.activation(col(DT), sdt[:, :], AF.Exp,
                                                  bias=c_dtb[:, :]),
                 reads=(Sdt, S_dtb), writes=(Sdtw,))
            k.op("scalar", lambda e: e.activation(col(DT), col(DT), AF.Ln, bias=1.0),
                 reads=(Sdtw,), writes=(Sdtw,))
            k.op("vector", lambda e: e.tensor_scalar(col(DT), col(DT), 1e-3,
                                                     100.0, ALU.max, ALU.min),
                 reads=(Sdtw,), writes=(Sdtw,))
            k.op("vector", lambda e: e.tensor_scalar(col(DA), col(DT),
                                                     c_acol[:, :], -80.0,
                                                     ALU.mult, ALU.max),
                 reads=(Sdtw, S_acol), writes=(Sdtw,))
            k.op("vector", lambda e: e.tensor_tensor_scan(col(CS), col(DA),
                                                          zz[:, :], 0.0,
                                                          ALU.add, ALU.add),
                 reads=(Sdtw, Szz), writes=(Sdtw,))
            k.op("vector",
                 lambda e, c=c: e.tensor_copy(da_all[:, c * 128:(c + 1) * 128],
                                              col(DA)),
                 reads=(Sdtw,), writes=(S_daall,))
            k.op("scalar",
                 lambda e, c=c: e.activation(expcs_all[:, c * 128:(c + 1) * 128],
                                             col(CS), AF.Exp),
                 reads=(Sdtw,), writes=(S_ecs,))
            k.op("scalar", lambda e, c=c: e.activation(cumdec[:, c:c + 1],
                                                       csum[:, 0:1], AF.Exp),
                 reads=(S_csum,), writes=(S_cumdec,))
            k.op("scalar", lambda e: e.activation(col(DE), col(CS), AF.Exp,
                                                  bias=col(CS)[:, 127:128],
                                                  scale=-1.0),
                 reads=(Sdtw,), writes=(Sdtw,))
            k.op("vector", lambda e: e.tensor_tensor(csum[:, 0:1], csum[:, 0:1],
                                                     col(CS)[:, 127:128], ALU.add),
                 reads=(Sdtw,), writes=(S_csum,))
            k.op("tensor", lambda e: e.transpose(psF[:, :], col(DT), c_eyef[:, :]),
                 reads=(Sdtw, S_eyef), writes=(SpF,))
            k.op("vector", lambda e: e.tensor_copy(col(DTT), psF[:, :]),
                 reads=(SpF,), writes=(Sdtw,))
            k.op("tensor", lambda e: e.transpose(psF[:, :], col(DE), c_eyef[:, :]),
                 reads=(Sdtw, S_eyef), writes=(SpF,))
            k.op("vector", lambda e: e.tensor_copy(col(DET), psF[:, :]),
                 reads=(SpF,), writes=(Sdtw,))
            k.op("vector", lambda e: e.tensor_tensor(col(WC), col(DTT),
                                                     col(DET), ALU.mult),
                 reads=(Sdtw,), writes=(Sdtw,))

            for tp in range(64):
                pb = tp % 2
                k.op("tensor",
                     lambda e, tp=tp, pb=pb, pb4=pb4: e.transpose(
                         psT[pb][:, :], sxa[pb4][:, tp * 128:(tp + 1) * 128],
                         c_eyeb[:, :]),
                     reads=(Sxa[pb4], S_eyeb), writes=(SpT[pb],))
                def vxdt(e, tp=tp, pb=pb):
                    h0, h1 = 2 * tp, 2 * tp + 1
                    e.tensor_scalar(xdt_sb[:, tp * 128:tp * 128 + 64],
                                    psT[pb][:, 0:64], col(DTT)[:, h0:h0 + 1],
                                    None, ALU.mult)
                    e.tensor_scalar(xdt_sb[:, tp * 128 + 64:tp * 128 + 128],
                                    psT[pb][:, 64:128], col(DTT)[:, h1:h1 + 1],
                                    None, ALU.mult)
                    e.tensor_scalar(xdtw_sb[:, tp * 128:tp * 128 + 64],
                                    psT[pb][:, 0:64], col(WC)[:, h0:h0 + 1],
                                    None, ALU.mult)
                    return e.tensor_scalar(xdtw_sb[:, tp * 128 + 64:tp * 128 + 128],
                                           psT[pb][:, 64:128], col(WC)[:, h1:h1 + 1],
                                           None, ALU.mult)
                k.op("vector", vxdt, reads=(SpT[pb], Sdtw), writes=(Sxdt, Sxdtw))
            k.dma(d_xdt[c], xdt_sb[:, :], reads=(Sxdt,),
                  writes=(DS["d_xdt"],), eng="gpsimd", ch=6)

            for g in range(8):
                pb = g % 2
                k.op("tensor",
                     lambda e, g=g, pb=pb, pb4=pb4: e.transpose(
                         psT[pb][:, :], sxa[pb4][:, (64 + g) * 128:(65 + g) * 128],
                         c_eyeb[:, :]),
                     reads=(Sxa[pb4], S_eyeb), writes=(SpT[pb],))
                k.op("vector",
                     lambda e, g=g, pb=pb: e.tensor_copy(
                         btm[:, g * 128:(g + 1) * 128], psT[pb][:, :]),
                     reads=(SpT[pb],), writes=(Sbtm,))

            k.op("vector", lambda e: e.tensor_copy(prevb[:, :], big[:, :]),
                 reads=(S_big,), writes=(S_prevb,))
            k.dma(d_prev[c], prevb[:, :], reads=(S_prevb,),
                  writes=(DS["d_prev"],), eng="gpsimd", ch=7)

            cdcol = expcs_all[:, c * 128 + 127:c * 128 + 128]
            k.dma(d_row[0:1, 0:128], cdcol, reads=(S_ecs,),
                  writes=(DS["d_row"],), eng="scalar", ch=8)
            k.dma(cdbc[:, :],
                  bass.AP(d_row.tensor, 0, [[16384, 1], [0, 128], [1, 128]]),
                  reads=(DS["d_row"],), writes=(S_cdbc,), eng="scalar", ch=8)
            rep = bass.AP(cdbc[:, :].tensor, 0, [[128, 128], [1, 128], [0, 64]])
            k.op("vector", lambda e, rep=rep: e.tensor_tensor(
                     big[:, :], big[:, :], rep, ALU.mult),
                 reads=(S_cdbc,), writes=(S_big,))

            for g in range(8):
                pb = g % 2
                def stmm(e, g=g, pb=pb):
                    last = None
                    for hh in range(16):
                        h = g * 16 + hh
                        last = e.matmul(
                            psS[pb][:, hh * 64:(hh + 1) * 64],
                            btm[:, g * 128:(g + 1) * 128],
                            xdtw_sb[:, h * 64:(h + 1) * 64],
                            start=True, stop=True)
                    return last
                k.op("tensor", stmm, reads=(Sbtm, Sxdtw), writes=(SpS[pb],))
                k.op("scalar",
                     lambda e, g=g, pb=pb: e.copy(
                         st_all[:, g * 1024:(g + 1) * 1024], psS[pb][:, :]),
                     reads=(SpS[pb],), writes=(S_stall,))
            k.op("vector", lambda e: e.tensor_tensor(big[:, :], big[:, :],
                                                     st_all[:, :], ALU.add),
                 reads=(S_stall,), writes=(S_big,))


def _s3(nc, k, DS, cc_in_t, cc_out_t, d_row, CN, P):
    c_masks, S_masks = CN["c_masks"]
    c_invm, S_invm = CN["c_invm"]
    cdbc, S_cdbc = CN["cdbc"]
    big, S_big = P["big"]
    csum, S_csum = P["csum"]
    with ExitStack() as es:
        E = es.enter_context
        ccsb = E(nc.sbuf_tensor("s3cc", [128, 8256], BF16))
        Scc = Slot("s3cc")
        shard = E(nc.sbuf_tensor("s3sh", [128, 8256], BF16))
        Ssh = Slot("s3sh")
        dsel = E(nc.sbuf_tensor("s3ds", [128, 4], F32))
        Sds = Slot("s3ds")
        ssel = E(nc.sbuf_tensor("s3ss", [128, INTER], BF16))
        Sss = Slot("s3ss")

        k.op("scalar", lambda e: e.activation(csum[:, 1:2], csum[:, 0:1], AF.Exp),
             reads=(S_csum,), writes=(S_csum,))
        k.op("vector", lambda e: e.tensor_copy(ccsb[:, 0:INTER], big[:, :]),
             reads=(S_big,), writes=(Scc,))
        k.op("vector", lambda e: e.memset(ccsb[:, INTER:8256], 0.0),
             writes=(Scc,))
        k.op("vector", lambda e: e.tensor_copy(ccsb[:, INTER:INTER + 1],
                                               csum[:, 1:2]),
             reads=(S_csum,), writes=(Scc,))
        k.dma(cc_in_t.ap(), ccsb[:, :], reads=(Scc,), writes=(DS["cc"],))
        k.op("gpsimd",
             lambda e: e.collective_compute(
                 "AllGather", ALU.bypass,
                 replica_groups=[[0, 1, 2, 3], [4, 5, 6, 7]],
                 ins=[cc_in_t.ap().opt()], outs=[cc_out_t.ap().opt()]),
             reads=(DS["cc"],), writes=(DS["cc"],))
        k.op("vector", lambda e: e.memset(big[:, :], 0.0),
             reads=(DS["cc"],), writes=(S_big,))
        for j in range(4):
            k.dma(shard[:, :], cc_out_t.ap()[j * 128:(j + 1) * 128, :],
                  reads=(DS["cc"],), writes=(Ssh,))
            k.op("vector",
                 lambda e, j=j: e.scalar_tensor_tensor(
                     dsel[:, j:j + 1], shard[:, INTER:INTER + 1],
                     c_masks[:, j:j + 1], c_invm[:, j:j + 1],
                     ALU.mult, ALU.add),
                 reads=(Ssh, S_masks, S_invm), writes=(Sds,))
            k.dma(d_row[0:1, 0:128], dsel[:, j:j + 1], reads=(Sds,),
                  writes=(DS["d_row"],))
            k.dma(cdbc[:, :],
                  bass.AP(d_row.tensor, 0, [[16384, 1], [0, 128], [1, 128]]),
                  reads=(DS["d_row"],), writes=(S_cdbc,))
            rep = bass.AP(cdbc[:, :].tensor, 0, [[128, 128], [1, 128], [0, 64]])
            k.op("vector", lambda e, rep=rep: e.tensor_tensor(
                     big[:, :], big[:, :], rep, ALU.mult),
                 reads=(S_cdbc,), writes=(S_big,))
            k.op("vector", lambda e, j=j: e.tensor_scalar(
                     ssel[:, :], shard[:, 0:INTER], c_masks[:, j:j + 1],
                     None, ALU.mult),
                 reads=(Ssh, S_masks), writes=(Sss,))
            k.op("vector", lambda e: e.tensor_tensor(big[:, :], big[:, :],
                                                     ssel[:, :], ALU.add),
                 reads=(Sss,), writes=(S_big,))


def _s4(nc, k, DS, d_proj, d_xbca, d_xdt, d_prev, d_ynf, d_row, d_rowb, CN, P):
    c_eyef, S_eyef = CN["c_eyef"]
    c_eyeb, S_eyeb = CN["c_eyeb"]
    c_dpair, S_dpair = CN["c_dpair"]
    c_normw, S_normw = CN["c_normw"]
    c_onescol, S_onescol = CN["c_onescol"]
    cdbc, S_cdbc = CN["cdbc"]
    da_all, S_daall = P["da_all"]
    expcs_all, S_ecs = P["expcs_all"]
    cumdec, S_cumdec = P["cumdec"]
    big, S_big = P["big"]
    with ExitStack() as es:
        E = es.enter_context
        sgate = E(nc.sbuf_tensor("s4g", [128, INTER], BF16))
        Sg = Slot("s4g")
        sxa = [E(nc.sbuf_tensor(f"s4xa{i}", [128, 80 * 128], BF16))
               for i in range(2)]
        Sxa = [Slot(f"s4xa{i}") for i in range(2)]
        sxdt = E(nc.sbuf_tensor("s4xd", [128, INTER], BF16))
        Sxd = Slot("s4xd")
        sprev = E(nc.sbuf_tensor("s4pv", [128, INTER], BF16))
        Spv = Slot("s4pv")
        abc = E(nc.sbuf_tensor("s4abc", [128, 32 * 128], F32))
        Sabc = Slot("s4abc")
        ebc = E(nc.sbuf_tensor("s4ebc", [128, 32 * 128], BF16))
        Sebc = Slot("s4ebc")
        ebf = E(nc.sbuf_tensor("s4ebf", [128, 128], BF16))
        Sebf = Slot("s4ebf")
        scg = E(nc.sbuf_tensor("s4scg", [128, 8 * 128], BF16))
        Sscg = Slot("s4scg")
        ainv = E(nc.sbuf_tensor("s4ai", [128, 128], F32))
        Sai = Slot("s4ai")
        lm = [E(nc.sbuf_tensor(f"s4lm{i}", [128, 128], F32)) for i in range(2)]
        Slm = [Slot(f"s4lm{i}") for i in range(2)]
        scl = [E(nc.sbuf_tensor(f"s4sl{i}", [128, 128], BF16)) for i in range(4)]
        Sscl = [Slot(f"s4sl{i}") for i in range(4)]
        cfe = [E(nc.sbuf_tensor(f"s4cf{i}", [128, 128], BF16)) for i in range(4)]
        Scfe = [Slot(f"s4cf{i}") for i in range(4)]
        ych = E(nc.sbuf_tensor("s4y", [128, INTER], BF16))
        Sy = Slot("s4y")
        gn = E(nc.sbuf_tensor("s4gn", [1, 1024], F32))
        Sgn = Slot("s4gn")
        gnbc = E(nc.sbuf_tensor("s4gnb", [128, 1024], F32))
        Sgnb = Slot("s4gnb")
        if DEBUG:
            d_ych = nc.dram_tensor("dbg_ych", [NCH, 128, INTER], BF16,
                                   kind="ExternalOutput").ap()
            S_dych = Slot("d_ych")
            d_ygt = nc.dram_tensor("dbg_ygt", [NCH, 128, INTER], BF16,
                                   kind="ExternalOutput").ap()
            S_dygt = Slot("d_ygt")
            d_gnr = nc.dram_tensor("dbg_gnr", [NCH, 1, 1024], F32,
                                   kind="ExternalOutput").ap()
            S_dgnr = Slot("d_gnr")
        psSc = [E(nc.psum_tensor(f"s4pc{i}", [128, 128], F32)) for i in range(2)]
        SpSc = [Slot(f"s4pc{i}") for i in range(2)]
        psY = [E(nc.psum_tensor(f"s4py{i}", [128, 128], F32)) for i in range(2)]
        SpY = [Slot(f"s4py{i}") for i in range(2)]
        psF = E(nc.psum_tensor("s4pfp", [128, 128], F32))
        SpF = Slot("s4pfp")
        psGN = E(nc.psum_tensor("s4pgn", [1, 1024], F32))
        SpGN = Slot("s4pgn")

        d_rowA = nc.dram_tensor("d_rowA", [NCH, 1, 16384], F32,
                                kind="Internal").ap()
        S_rA = Slot("d_rowA")
        d_rowB2 = nc.dram_tensor("d_rowB2", [NCH, 1, 16384], BF16,
                                 kind="Internal").ap()
        S_rB2 = Slot("d_rowB2")
        d_rowC = nc.dram_tensor("d_rowC", [NCH, 1, 128], F32,
                                kind="Internal").ap()
        S_rC = Slot("d_rowC")
        for c in range(NCH):
            k.dma(d_rowA[c], da_all[:, c * 128:(c + 1) * 128],
                  reads=(S_daall,), writes=(S_rA,), eng="scalar", ch=9)
            k.op("vector", lambda e, c=c: e.tensor_copy(
                     ebf[:, :], expcs_all[:, c * 128:(c + 1) * 128]),
                 reads=(S_ecs,), writes=(Sebf,))
            k.dma(d_rowB2[c], ebf[:, :], reads=(Sebf,), writes=(S_rB2,),
                  eng="scalar", ch=14)
            k.dma(d_rowC[c], cumdec[:, c:c + 1], reads=(S_cumdec,),
                  writes=(S_rC,), eng="scalar", ch=8)

        for c in range(NCH):
            w0 = PAD + LOOK + c * 128
            pb4 = c % 2
            for s4 in range(4):
                k.dma(sxa[pb4][:, s4 * 20 * 128:(s4 + 1) * 20 * 128],
                      d_xbca[s4 * 20:(s4 + 1) * 20]
                      .transpose([1, 0, 2])[:, :, c * 128:(c + 1) * 128],
                      reads=(DS["d_xbca"],), writes=(Sxa[pb4],), ch=4 + s4)
            for s4 in range(4):
                k.dma(sgate[:, s4 * 16 * 128:(s4 + 1) * 16 * 128],
                      d_proj[s4 * 16:(s4 + 1) * 16]
                      .transpose([1, 0, 2])[:, :, w0:w0 + 128],
                      reads=(DS["d_proj"],), writes=(Sg,), ch=s4)
            k.dma(sxdt[:, :], d_xdt[c], reads=(DS["d_xdt"],), writes=(Sxd,),
                  eng="gpsimd", ch=10)
            k.dma(sprev[:, :], d_prev[c], reads=(DS["d_prev"],), writes=(Spv,),
                  eng="gpsimd", ch=11)

            k.dma(cdbc[:, :],
                  bass.AP(d_rowC.tensor, c * 128,
                          [[16384, 1], [0, 128], [1, 128]]),
                  reads=(S_rC,), writes=(S_cdbc,), eng="scalar", ch=8)
            rep = bass.AP(cdbc[:, :].tensor, 0, [[128, 128], [1, 128], [0, 64]])
            k.op("vector", lambda e, rep=rep: e.tensor_tensor(
                     ych[:, :], big[:, :], rep, ALU.mult),
                 reads=(S_big, S_cdbc), writes=(Sy,))
            k.op("vector", lambda e: e.tensor_tensor(sprev[:, :], ych[:, :],
                                                     sprev[:, :], ALU.add),
                 reads=(Sy, Spv), writes=(Spv,))

            for g in range(8):
                pb = g % 2
                k.op("tensor",
                     lambda e, g=g, pb=pb, pb4=pb4: e.matmul(
                         psSc[pb][:, :],
                         sxa[pb4][:, (64 + g) * 128:(65 + g) * 128],
                         sxa[pb4][:, (72 + g) * 128:(73 + g) * 128],
                         start=True, stop=True),
                     reads=(Sxa[pb4],), writes=(SpSc[pb],))
                k.op("scalar",
                     lambda e, g=g, pb=pb: e.copy(scg[:, g * 128:(g + 1) * 128],
                                                  psSc[pb][:, :]),
                     reads=(SpSc[pb],), writes=(Sscg,))

            k.op("tensor",
                 lambda e, c=c: e.transpose(psF[:, :],
                                            da_all[:, c * 128:(c + 1) * 128],
                                            c_eyef[:, :]),
                 reads=(S_daall, S_eyef), writes=(SpF,))
            k.op("vector", lambda e: e.tensor_copy(ainv[:, :], psF[:, :]),
                 reads=(SpF,), writes=(Sai,))
            k.op("scalar", lambda e: e.activation(ainv[:, :], ainv[:, :],
                                                  AF.Exp, scale=-1.0),
                 reads=(Sai,), writes=(Sai,))

            pend = []
            for q in range(4):
                k.dma(abc[:, :],
                      bass.AP(d_rowA.tensor, c * 16384 + q * 4096,
                              [[16384, 1], [0, 128], [1, 4096]]),
                      reads=(S_rA,), writes=(Sabc,), eng="scalar", ch=9)
                k.op("scalar", lambda e: e.activation(abc[:, :], abc[:, :],
                                                      AF.Exp),
                     reads=(Sabc,), writes=(Sabc,))
                k.dma(ebc[:, :],
                      bass.AP(d_rowB2.tensor, c * 16384 + q * 4096,
                              [[16384, 1], [0, 128], [1, 4096]]),
                      reads=(S_rB2,), writes=(Sebc,), eng="scalar", ch=14)
                for hh in range(32):
                    h = q * 32 + hh
                    g, tp, half = h // 16, h // 2, h % 2
                    sl = h % 4
                    lmp = h % 2
                    while pend and pend[0][0] + 2 <= h:
                        pend.pop(0)[1]()
                    k.op("vector",
                         lambda e, hh=hh, lmp=lmp: e.tensor_tensor_scan(
                             lm[lmp][:, :], c_eyef[:, :],
                             abc[:, hh * 128:(hh + 1) * 128], 0.0,
                             ALU.add, ALU.mult),
                         reads=(S_eyef, Sabc), writes=(Slm[lmp],))
                    k.op("vector",
                         lambda e, h=h, g=g, sl=sl, lmp=lmp:
                         e.scalar_tensor_tensor(
                             scl[sl][:, :], lm[lmp][:, :], ainv[:, h:h + 1],
                             scg[:, g * 128:(g + 1) * 128], ALU.mult, ALU.mult),
                         reads=(Slm[lmp], Sai, Sscg), writes=(Sscl[sl],))
                    k.op("vector",
                         lambda e, hh=hh, g=g, sl=sl, pb4=pb4: e.tensor_tensor(
                             cfe[sl][:, :],
                             sxa[pb4][:, (72 + g) * 128:(73 + g) * 128],
                             ebc[:, hh * 128:(hh + 1) * 128], ALU.mult),
                         reads=(Sxa[pb4], Sebc), writes=(Scfe[sl],))
                    yb = psY[tp % 2]
                    SpYb = SpY[tp % 2]
                    def ymm(e, h=h, half=half, sl=sl, yb=yb):
                        e.matmul(yb[half * 64:(half + 1) * 64, :],
                                 sxdt[:, h * 64:(h + 1) * 64],
                                 scl[sl][:, :],
                                 start=True, stop=False)
                        return e.matmul(yb[half * 64:(half + 1) * 64, :],
                                        sprev[:, h * 64:(h + 1) * 64],
                                        cfe[sl][:, :],
                                        start=False, stop=True)
                    k.op("tensor", ymm, reads=(Sxd, Sscl[sl], Spv, Scfe[sl]),
                         writes=(SpYb,))
                    if half == 1:
                        def emit_yevac(tp=tp, pb4=pb4, yb=yb, SpYb=SpYb):
                            k.op("vector",
                                 lambda e: e.tensor_scalar(
                                     ych[:, tp * 128:(tp + 1) * 128],
                                     sxa[pb4][:, tp * 128:(tp + 1) * 128],
                                     c_dpair[:, tp:tp + 1], None, ALU.mult),
                                 reads=(Sxa[pb4], S_dpair), writes=(Sy,))
                            k.op("vector",
                                 lambda e: e.tensor_tensor(
                                     ych[:, tp * 128:(tp + 1) * 128], yb[:, :],
                                     ych[:, tp * 128:(tp + 1) * 128], ALU.add),
                                 reads=(SpYb, Sy), writes=(Sy,))
                        pend.append((h, emit_yevac))

            while pend:
                pend.pop(0)[1]()
            if DEBUG:
                k.dma(d_ych[c], ych[:, :], reads=(Sy,), writes=(S_dych,))
            k.op("scalar", lambda e: e.activation(sgate[:, :], sgate[:, :],
                                                  AF.Silu),
                 reads=(Sg,), writes=(Sg,))
            k.op("vector", lambda e: e.tensor_tensor(ych[:, :], ych[:, :],
                                                     sgate[:, :], ALU.mult),
                 reads=(Sg, Sy), writes=(Sy,))
            if DEBUG:
                k.dma(d_ygt[c], ych[:, :], reads=(Sy,), writes=(S_dygt,))
            k.op("scalar", lambda e: e.activation(sgate[:, :], ych[:, :],
                                                  AF.Square),
                 reads=(Sy,), writes=(Sg,))
            def gnmm(e):
                last = None
                for g in range(8):
                    for t8 in range(8):
                        last = e.matmul(psGN[0:1, g * 128:(g + 1) * 128],
                                        c_onescol[:, :],
                                        sgate[:, (8 * g + t8) * 128:
                                              (8 * g + t8 + 1) * 128],
                                        start=(t8 == 0),
                                        stop=(t8 == 7))
                return last
            k.op("tensor", gnmm, reads=(Sg, S_onescol), writes=(SpGN,))
            k.op("vector", lambda e: e.tensor_scalar(
                     gn[0:1, 0:1024], psGN[0:1, :], 1.0 / 1024.0, EPS,
                     ALU.mult, ALU.add),
                 reads=(SpGN,), writes=(Sgn,))
            k.op("scalar", lambda e: e.activation(gn[0:1, 0:1024],
                                                  gn[0:1, 0:1024], AF.Ln),
                 reads=(Sgn,), writes=(Sgn,))
            k.op("scalar", lambda e: e.activation(gn[0:1, 0:1024],
                                                  gn[0:1, 0:1024], AF.Exp,
                                                  scale=-0.5),
                 reads=(Sgn,), writes=(Sgn,))
            if DEBUG:
                k.dma(d_gnr[c], gn[0:1, 0:1024], reads=(Sgn,),
                      writes=(S_dgnr,))
            k.dma(d_row[0:1, 0:1024], gn[0:1, 0:1024], reads=(Sgn,),
                  writes=(DS["d_row"],), eng="scalar", ch=8)
            k.dma(gnbc[:, :],
                  bass.AP(d_row.tensor, 0, [[16384, 1], [0, 128], [1, 1024]]),
                  reads=(DS["d_row"],), writes=(Sgnb,), eng="scalar", ch=8)
            def vynf(e):
                last = None
                for tp in range(64):
                    g = tp // 8
                    last = e.scalar_tensor_tensor(
                        ych[:, tp * 128:(tp + 1) * 128],
                        ych[:, tp * 128:(tp + 1) * 128],
                        c_normw[:, tp:tp + 1],
                        gnbc[:, g * 128:(g + 1) * 128],
                        ALU.mult, ALU.mult)
                return last
            k.op("vector", vynf, reads=(Sy, Sgnb, S_normw), writes=(Sy,))
            for s4 in range(4):
                k.dma(d_ynf[s4 * 16:(s4 + 1) * 16]
                      .transpose([1, 0, 2])[:, :, c * 128:(c + 1) * 128],
                      ych[:, s4 * 16 * 128:(s4 + 1) * 16 * 128],
                      reads=(Sy,), writes=(DS["d_ynf"],),
                      eng="gpsimd", ch=12 + s4 % 2)


def _s5(nc, k, DS, woutT, d_ynf, d_attn):
    with ExitStack() as es:
        E = es.enter_context
        synf = E(nc.sbuf_tensor("s5yn", [128, 64 * 1024], BF16))
        Syn = Slot("s5yn")
        swf = E(nc.sbuf_tensor("s5wf", [128, INTER], F32))
        Swf = Slot("s5wf")
        swb = [E(nc.sbuf_tensor(f"s5wb{i}", [128, INTER], BF16)) for i in range(2)]
        Swb = [Slot(f"s5wb{i}") for i in range(2)]
        sat = [E(nc.sbuf_tensor(f"s5at{i}", [128, 1024], BF16)) for i in range(2)]
        Sat = [Slot(f"s5at{i}") for i in range(2)]
        pss = [E(nc.psum_tensor(f"s5p{i}", [128, 512], F32)) for i in range(4)]
        Sps = [Slot(f"s5p{i}") for i in range(4)]

        for hf in range(2):
            for s4 in range(4):
                k.dma(synf[:, s4 * 16 * 1024:(s4 + 1) * 16 * 1024],
                      d_ynf[s4 * 16:(s4 + 1) * 16]
                      .transpose([1, 0, 2])[:, :, hf * 1024:(hf + 1) * 1024],
                      reads=(DS["d_ynf"],), writes=(Syn,), ch=4 + s4 % 2)
            for og in range(32):
                p = og % 2
                k.dma(swf[:, :], woutT[og], writes=(Swf,), ch=og % 4)
                k.op("vector",
                     lambda e, p=p: e.tensor_copy(swb[p][:, :], swf[:, :]),
                     reads=(Swf,), writes=(Swb[p],))
                def mm(e, p=p):
                    last = None
                    for kt in range(64):
                        for tc in range(2):
                            last = e.matmul(
                                pss[2 * p + tc][:, :],
                                swb[p][:, kt * 128:(kt + 1) * 128],
                                synf[:, kt * 1024 + tc * 512:
                                     kt * 1024 + (tc + 1) * 512],
                                start=(kt == 0), stop=(kt == 63))
                    return last
                k.op("tensor", mm, reads=(Swb[p], Syn),
                     writes=(Sps[2 * p], Sps[2 * p + 1]))
                def evac(e, p=p):
                    last = None
                    for tc in range(2):
                        last = e.copy(sat[p][:, tc * 512:(tc + 1) * 512],
                                      pss[2 * p + tc][:, :])
                    return last
                k.op("scalar", evac, reads=(Sps[2 * p], Sps[2 * p + 1]),
                     writes=(Sat[p],))
                k.dma(d_attn[og][:, hf * 1024:(hf + 1) * 1024], sat[p][:, :],
                      reads=(Sat[p],), writes=(DS["d_attn"],),
                      eng="scalar", ch=6 + og % 2)


def _s6(nc, k, DS, x_ext, d_attn, d_x2, ln3_fm, S_ln3, CN):
    c_eyeb, S_eyeb = CN["c_eyeb"]
    with ExitStack() as es:
        E = es.enter_context
        sfm = [E(nc.sbuf_tensor(f"s6f{i}", [128, 32 * 128], BF16)) for i in range(2)]
        Sf = [Slot(f"s6f{i}") for i in range(2)]
        atm = E(nc.sbuf_tensor("s6a", [128, DM], BF16))
        Sa = Slot("s6a")
        nrm = E(nc.sbuf_tensor("s6n", [128, DM], BF16))
        Sn = Slot("s6n")
        xr = E(nc.sbuf_tensor("s6x", [128, DM], F32))
        Sx = Slot("s6x")
        x2t = [E(nc.sbuf_tensor(f"s6x2{i}", [128, DM], BF16)) for i in range(2)]
        Sx2 = [Slot(f"s6x2{i}") for i in range(2)]
        ln3t = E(nc.sbuf_tensor("s6l3", [128, DM], BF16))
        Sl3 = Slot("s6l3")
        ssq, Ssq = None, None
        stat = E(nc.sbuf_tensor("s6st", [128, 8], F32))
        Sst = Slot("s6st")
        ps = [E(nc.psum_tensor(f"s6p{i}", [128, 128], BF16)) for i in range(4)]
        Sp = [Slot(f"s6p{i}") for i in range(4)]

        for tt in range(16):
            p = tt % 2
            for s2_ in range(2):
                k.dma(sfm[p][:, s2_ * 16 * 128:(s2_ + 1) * 16 * 128],
                      d_attn[s2_ * 16:(s2_ + 1) * 16]
                      .transpose([1, 0, 2])[:, :, tt * 128:(tt + 1) * 128],
                      reads=(DS["d_attn"],), writes=(Sf[p],), ch=s2_)
            k.dma(xr[:, :],
                  x_ext[PAD + LOOK + tt * 128:PAD + LOOK + (tt + 1) * 128, :],
                  writes=(Sx,), ch=2)
            for dt_ in range(32):
                b = dt_ % 4
                k.op("tensor",
                     lambda e, dt_=dt_, p=p, b=b: e.transpose(
                         ps[b][:, :], sfm[p][:, dt_ * 128:(dt_ + 1) * 128],
                         c_eyeb[:, :]),
                     reads=(Sf[p], S_eyeb), writes=(Sp[b],))
                k.op("vector",
                     lambda e, dt_=dt_, b=b: e.tensor_copy(
                         atm[:, dt_ * 128:(dt_ + 1) * 128], ps[b][:, :]),
                     reads=(Sp[b],), writes=(Sa,))
            _ln_stats(nc, k, atm, Sa, stat, Sst, nrm, Sn)
            k.op("scalar", lambda e: e.activation(nrm[:, :], atm[:, :], AF.Identity,
                                                  bias=stat[:, 7:8],
                                                  scale=stat[:, 6:7]),
                 reads=(Sa, Sst), writes=(Sn,))
            k.op("vector", lambda e, p=p: e.tensor_tensor(x2t[p][:, :], xr[:, :],
                                                          nrm[:, :], ALU.add),
                 reads=(Sx, Sn), writes=(Sx2[p],))
            k.dma(d_x2[tt], x2t[p][:, :], reads=(Sx2[p],),
                  writes=(DS["d_x2"],), eng="gpsimd", ch=6)
            _ln_stats(nc, k, x2t[p], Sx2[p], stat, Sst, nrm, Sn)
            k.op("scalar", lambda e, p=p: e.activation(ln3t[:, :], x2t[p][:, :],
                                                       AF.Identity, bias=stat[:, 7:8],
                                                       scale=stat[:, 6:7]),
                 reads=(Sx2[p], Sst), writes=(Sl3,))
            for kt in range(32):
                b = kt % 4
                k.op("tensor",
                     lambda e, kt=kt, b=b: e.transpose(
                         ps[b][:, :], ln3t[:, kt * 128:(kt + 1) * 128],
                         c_eyeb[:, :]),
                     reads=(Sl3, S_eyeb), writes=(Sp[b],))
                k.op("vector",
                     lambda e, tt=tt, kt=kt, b=b: e.tensor_copy(
                         ln3_fm[:, kt * NT + tt * 128: kt * NT + (tt + 1) * 128],
                         ps[b][:, :]),
                     reads=(Sp[b],), writes=(S_ln3,))


def _s7(nc, k, DS, gwT, bwT, lAT, lBT, ln3_fm, S_ln3, d_ffn, CN):
    c_eyeb, S_eyeb = CN["c_eyeb"]
    c_bb, S_bb = CN["c_bb"]
    with ExitStack() as es:
        E = es.enter_context
        sgwf = E(nc.sbuf_tensor("s7gwf", [128, 32 * 8], F32))
        sgw = E(nc.sbuf_tensor("s7gw", [128, 32 * 8], BF16))
        Sgw = Slot("s7gw")
        slat = E(nc.sbuf_tensor("s7la", [128, DM], BF16))
        Sla = Slot("s7la")
        slbt = E(nc.sbuf_tensor("s7lb", [128, DM], BF16))
        Slb = Slot("s7lb")
        lgt = E(nc.sbuf_tensor("s7lg", [128, 48], F32))
        Slg = Slot("s7lg")
        wfb = E(nc.sbuf_tensor("s7wfb", [128, 8], BF16))
        Swfb = Slot("s7wfb")
        wfTa = E(nc.sbuf_tensor("s7wt", [8, NT], BF16))
        Swt = Slot("s7wt")
        wrep = E(nc.sbuf_tensor("s7wr", [128, NT], BF16))
        Swr = Slot("s7wr")
        lowb = E(nc.sbuf_tensor("s7lo", [128, NT], BF16))
        Slo = Slot("s7lo")
        loww = E(nc.sbuf_tensor("s7lw", [128, NT], BF16))
        Slw = Slot("s7lw")
        sbwf = E(nc.sbuf_tensor("s7bwf", [128, DM], F32))
        Sbwf = Slot("s7bwf")
        sbwb = [E(nc.sbuf_tensor(f"s7bw{i}", [128, DM], BF16)) for i in range(2)]
        Sbw = [Slot(f"s7bw{i}") for i in range(2)]
        sff = [E(nc.sbuf_tensor(f"s7ff{i}", [128, NT], BF16)) for i in range(2)]
        Sff = [Slot(f"s7ff{i}") for i in range(2)]
        psL = E(nc.psum_tensor("s7pl", [128, 8], F32))
        SpL = Slot("s7pl")
        psW = E(nc.psum_tensor("s7pw", [8, 128], BF16))
        SpW = Slot("s7pw")
        pss = [E(nc.psum_tensor(f"s7p{i}", [128, 512], F32)) for i in range(4)]
        Sps = [Slot(f"s7p{i}") for i in range(4)]

        Sc = Slot("s7consts")
        k.dma(sgwf[:, :], gwT.rearrange("(t p) o -> p t o", p=128), writes=(Sc,))
        k.op("vector", lambda e: e.tensor_copy(sgw[:, :], sgwf[:, :]),
             reads=(Sc,), writes=(Sgw,))
        k.dma(sbwf[:, :], lAT, writes=(Sbwf,))
        k.op("vector", lambda e: e.tensor_copy(slat[:, :], sbwf[:, :]),
             reads=(Sbwf,), writes=(Sla,))
        k.dma(sbwf[:, :], lBT, writes=(Sbwf,))
        k.op("vector", lambda e: e.tensor_copy(slbt[:, :], sbwf[:, :]),
             reads=(Sbwf,), writes=(Slb,))

        for tt in range(16):
            def gmm(e, tt=tt):
                last = None
                for kt in range(32):
                    last = e.matmul(psL[:, :],
                                    ln3_fm[:, kt * NT + tt * 128:
                                           kt * NT + (tt + 1) * 128],
                                    sgw[:, kt * 8:(kt + 1) * 8],
                                    start=(kt == 0), stop=(kt == 31))
                return last
            k.op("tensor", gmm, reads=(S_ln3, Sgw), writes=(SpL,))
            for f in [
                lambda e: e.tensor_copy(lgt[:, 0:8], psL[:, :]),
                lambda e: e.reduce_max(lgt[:, 8:9], lgt[:, 0:8], axis=AX),
                lambda e: e.tensor_tensor(lgt[:, 16:24], lgt[:, 0:8],
                                          lgt[:, 8:9].broadcast_to([128, 8]),
                                          ALU.is_equal),
                lambda e: e.scalar_tensor_tensor(lgt[:, 24:32], lgt[:, 16:24],
                                                 -1e30, lgt[:, 0:8],
                                                 ALU.mult, ALU.add),
                lambda e: e.reduce_max(lgt[:, 9:10], lgt[:, 24:32], axis=AX),
                lambda e: e.tensor_tensor(lgt[:, 12:13], lgt[:, 8:9],
                                          lgt[:, 9:10], ALU.subtract),
            ]:
                k.op("vector", f, reads=(SpL, Slg), writes=(Slg,))
            k.op("scalar", lambda e: e.activation(lgt[:, 10:11], lgt[:, 12:13],
                                                  AF.Sigmoid),
                 reads=(Slg,), writes=(Slg,))
            for f in [
                lambda e: e.tensor_scalar(lgt[:, 11:12], lgt[:, 10:11], -1.0,
                                          1.0, ALU.mult, ALU.add),
                lambda e: e.tensor_tensor(lgt[:, 32:40], lgt[:, 24:32],
                                          lgt[:, 9:10].broadcast_to([128, 8]),
                                          ALU.is_equal),
                lambda e: e.tensor_scalar(lgt[:, 32:40], lgt[:, 32:40],
                                          lgt[:, 11:12], None, ALU.mult),
                lambda e: e.scalar_tensor_tensor(lgt[:, 40:48], lgt[:, 16:24],
                                                 lgt[:, 10:11], lgt[:, 32:40],
                                                 ALU.mult, ALU.add),
            ]:
                k.op("vector", f, reads=(Slg,), writes=(Slg,))
            k.op("vector", lambda e: e.tensor_scalar(wfb[:, :], lgt[:, 40:48],
                                                     2.0, None, ALU.mult),
                 reads=(Slg,), writes=(Swfb,))
            k.op("tensor", lambda e: e.transpose(psW[:, :], wfb[:, :],
                                                 c_eyeb[:, :]),
                 reads=(Swfb, S_eyeb), writes=(SpW,))
            k.op("vector",
                 lambda e, tt=tt: e.tensor_copy(wfTa[0:8, tt * 128:(tt + 1) * 128],
                                                psW[:, :]),
                 reads=(SpW,), writes=(Swt,))
        k.dma(wrep[:, :],
              bass.AP(wfTa[:, :].tensor, 0, [[NT, 8], [0, 16], [1, NT]]),
              reads=(Swt,), writes=(Swr,), eng="gpsimd", ch=5)

        def lowmm(e):
            last = None
            for kt in range(32):
                for tc in range(4):
                    last = e.matmul(pss[tc][:, :],
                                    slat[:, kt * 128:(kt + 1) * 128],
                                    ln3_fm[:, kt * NT + tc * 512:
                                           kt * NT + (tc + 1) * 512],
                                    start=(kt == 0), stop=(kt == 31))
            return last
        k.op("tensor", lowmm, reads=(Sla, S_ln3), writes=tuple(Sps))
        def lowev(e):
            last = None
            for tc in range(4):
                last = e.copy(lowb[:, tc * 512:(tc + 1) * 512], pss[tc][:, :])
            return last
        k.op("scalar", lowev, reads=tuple(Sps), writes=(Slo,))
        k.op("vector", lambda e: e.tensor_tensor(loww[:, :], lowb[:, :],
                                                 wrep[:, :], ALU.mult),
             reads=(Slo, Swr), writes=(Slw,))

        for og in range(32):
            p = og % 2
            k.dma(sbwf[:, :], bwT[og], writes=(Sbwf,), ch=og % 4)
            k.op("vector", lambda e, p=p: e.tensor_copy(sbwb[p][:, :],
                                                        sbwf[:, :]),
                 reads=(Sbwf,), writes=(Sbw[p],))
            def bmm(e, p=p, og=og):
                last = None
                for kt in range(32):
                    for tc in range(4):
                        last = e.matmul(pss[tc][:, :],
                                        sbwb[p][:, kt * 128:(kt + 1) * 128],
                                        ln3_fm[:, kt * NT + tc * 512:
                                               kt * NT + (tc + 1) * 512],
                                        start=(kt == 0), stop=False)
                for tc in range(4):
                    last = e.matmul(pss[tc][:, :],
                                    slbt[:, og * 128:(og + 1) * 128],
                                    loww[:, tc * 512:(tc + 1) * 512],
                                    start=False, stop=True)
                return last
            k.op("tensor", bmm, reads=(Sbw[p], S_ln3, Slb, Slw), writes=tuple(Sps))
            def fevac(e, p=p, og=og):
                last = None
                for tc in range(4):
                    last = e.activation(sff[p][:, tc * 512:(tc + 1) * 512],
                                        pss[tc][:, :], AF.Identity,
                                        bias=c_bb[:, og:og + 1])
                return last
            k.op("scalar", fevac, reads=tuple(Sps) + (S_bb,), writes=(Sff[p],))
            k.dma(d_ffn[og], sff[p][:, :], reads=(Sff[p],),
                  writes=(DS["d_ffn"],), eng="scalar", ch=6 + og % 2)


def _s8(nc, k, DS, d_x2, d_ffn, o_y, CN):
    c_eyeb, S_eyeb = CN["c_eyeb"]
    with ExitStack() as es:
        E = es.enter_context
        sfm = [E(nc.sbuf_tensor(f"s8f{i}", [128, 32 * 128], BF16)) for i in range(2)]
        Sf = [Slot(f"s8f{i}") for i in range(2)]
        ftm = E(nc.sbuf_tensor("s8a", [128, DM], BF16))
        Sa = Slot("s8a")
        nrm = E(nc.sbuf_tensor("s8n", [128, DM], BF16))
        Sn = Slot("s8n")
        x2r = [E(nc.sbuf_tensor(f"s8x{i}", [128, DM], BF16)) for i in range(2)]
        Sx = [Slot(f"s8x{i}") for i in range(2)]
        sout = [E(nc.sbuf_tensor(f"s8o{i}", [128, DM], F32)) for i in range(2)]
        So = [Slot(f"s8o{i}") for i in range(2)]
        stat = E(nc.sbuf_tensor("s8st", [128, 8], F32))
        Sst = Slot("s8st")
        ps = [E(nc.psum_tensor(f"s8p{i}", [128, 128], BF16)) for i in range(4)]
        Sp = [Slot(f"s8p{i}") for i in range(4)]

        for tt in range(16):
            p = tt % 2
            k.dma(sfm[p][:, 0:16 * 128],
                  d_ffn[0:16].transpose([1, 0, 2])[:, :, tt * 128:(tt + 1) * 128],
                  reads=(DS["d_ffn"],), writes=(Sf[p],), ch=0)
            k.dma(sfm[p][:, 16 * 128:32 * 128],
                  d_ffn[16:32].transpose([1, 0, 2])[:, :, tt * 128:(tt + 1) * 128],
                  reads=(DS["d_ffn"],), writes=(Sf[p],), ch=1)
            k.dma(x2r[p][:, :], d_x2[tt], reads=(DS["d_x2"],), writes=(Sx[p],), ch=2)
            for dt_ in range(32):
                b = dt_ % 4
                k.op("tensor",
                     lambda e, dt_=dt_, p=p, b=b: e.transpose(
                         ps[b][:, :], sfm[p][:, dt_ * 128:(dt_ + 1) * 128],
                         c_eyeb[:, :]),
                     reads=(Sf[p], S_eyeb), writes=(Sp[b],))
                k.op("vector",
                     lambda e, dt_=dt_, b=b: e.tensor_copy(
                         ftm[:, dt_ * 128:(dt_ + 1) * 128], ps[b][:, :]),
                     reads=(Sp[b],), writes=(Sa,))
            _ln_stats(nc, k, ftm, Sa, stat, Sst, nrm, Sn)
            k.op("scalar", lambda e: e.activation(nrm[:, :], ftm[:, :], AF.Identity,
                                                  bias=stat[:, 7:8],
                                                  scale=stat[:, 6:7]),
                 reads=(Sa, Sst), writes=(Sn,))
            k.op("vector", lambda e, p=p: e.tensor_tensor(sout[p][:, :],
                                                          x2r[p][:, :],
                                                          nrm[:, :], ALU.add),
                 reads=(Sx[p], Sn), writes=(So[p],))
            k.dma(o_y[tt * 128:(tt + 1) * 128, :], sout[p][:, :],
                  reads=(So[p],), writes=(Slot("oy"),), eng="gpsimd",
                  ch=6 + tt % 2)


def kernel(**inputs):
    x = np.ascontiguousarray(inputs["L0_attn_hidden_states"], dtype=np.float32)
    W = np.ascontiguousarray(inputs["L0_attn_in_proj_w"], dtype=np.float32)
    cw = np.ascontiguousarray(inputs["L0_attn_conv_w"], dtype=np.float32)
    cb = np.ascontiguousarray(inputs["L0_attn_conv_b"], dtype=np.float32)
    dtb = np.ascontiguousarray(inputs["L0_attn_dt_bias"], dtype=np.float32)
    Dv = np.ascontiguousarray(inputs["L0_attn_D"], dtype=np.float32)
    nw = np.ascontiguousarray(inputs["L0_attn_norm_w"], dtype=np.float32)
    Wo = np.ascontiguousarray(inputs["L0_attn_out_proj_w"], dtype=np.float32)
    gw = np.ascontiguousarray(inputs["L0_ffn_gate_w"], dtype=np.float32)
    bw = np.ascontiguousarray(inputs["L0_ffn_base_w"], dtype=np.float32)
    bb = np.ascontiguousarray(inputs["L0_ffn_base_b"], dtype=np.float32)
    lA = np.ascontiguousarray(inputs["L0_ffn_lora_A"], dtype=np.float32)
    lB = np.ascontiguousarray(inputs["L0_ffn_lora_B"], dtype=np.float32)

    wT = np.ascontiguousarray(
        np.pad(W, ((0, 0), (0, 0))).reshape(145, 128, 32, 128)
        .transpose(0, 3, 2, 1).reshape(145, 128, DM))
    cw_cols = np.ascontiguousarray(
        cw[:, 0, :].reshape(80, 128, 4).transpose(1, 0, 2).reshape(128, 320))
    cb_row = np.ascontiguousarray(cb.reshape(1, CONV))
    dtb_col = np.ascontiguousarray(dtb.reshape(128, 1))
    d_pair = np.ascontiguousarray(
        np.repeat(Dv.reshape(64, 2, 1), 64, axis=2).transpose(1, 2, 0)
        .reshape(128, 64))
    normw_cols = np.ascontiguousarray(nw.reshape(64, 128).T)
    woutT = np.ascontiguousarray(
        Wo.reshape(32, 128, 64, 128).transpose(0, 3, 2, 1)
        .reshape(32, 128, INTER))
    gwT = np.ascontiguousarray(gw.T)
    bwT = np.ascontiguousarray(
        bw.reshape(32, 128, 32, 128).transpose(0, 3, 2, 1)
        .reshape(32, 128, DM))
    bb_cols = np.ascontiguousarray(bb.reshape(32, 128).T)
    lAT = np.ascontiguousarray(
        lA.reshape(128, 32, 128).transpose(2, 1, 0).reshape(128, DM))
    lBT = np.ascontiguousarray(lB.transpose(0, 2, 1).reshape(128, DM))

    in_maps = []
    for c in range(8):
        b, q = c // 4, c % 4
        start = q * 2048
        xs = x[b, start:start + 2048]
        look = x[b, start - 3:start] if q > 0 else np.zeros((3, DM), np.float32)
        x_ext = np.concatenate(
            [np.zeros((PAD, DM), np.float32), look, xs], axis=0)
        m = np.zeros((128, 4), np.float32)
        m[:, :q] = 1.0
        im = 1.0 - m
        in_maps.append({
            "x_ext": np.ascontiguousarray(x_ext),
            "wT": wT, "cw_cols": cw_cols, "cb_row": cb_row,
            "dtb_col": dtb_col, "d_pair": d_pair, "normw_cols": normw_cols,
            "woutT": woutT, "gwT": gwT, "bwT": bwT, "bb_cols": bb_cols,
            "lAT": lAT, "lBT": lBT,
            "masks": np.ascontiguousarray(m),
            "invmasks": np.ascontiguousarray(im),
        })

    nc = build_nc(debug=DEBUG)
    res = run_bass_kernel_spmd(nc, in_maps, core_ids=list(range(8)))
    out = np.empty((2, 8192, DM), dtype=np.float32)
    for c in range(8):
        b, q = c // 4, c % 4
        out[b, q * 2048:(q + 1) * 2048] = res.results[c]["o_y"]
    if DEBUG:
        kernel.last_results = res
    return out
